# version 05
# baseline (speedup 1.0000x reference)
import os
import numpy as np
from contextlib import ExitStack

import concourse.bass as bass
import concourse.mybir as mybir
from concourse.bass_utils import run_bass_kernel_spmd

F32 = mybir.dt.float32
BF16 = mybir.dt.bfloat16
AF = mybir.ActivationFunctionType
OP = mybir.AluOpType
AX = mybir.AxisListType

B, T, D = 2, 8192, 1024
NCORES = 8
TC = int(os.environ.get("KTC", (B * T) // NCORES))
TE = TC + 3
DI = 2048
S0 = 16
NH, S1, HD = 32, 64, 64
DPROJ = 2 * DI + NH * (2 * S1 + 1)
DBG = int(os.environ.get("KDBG", "0"))
STAGE = int(os.environ.get("KSTAGE", "99"))

NLEXP = "nlexp"
SILU = "silu"
CONST_VALUES = [1e-5, -0.5, 1.0, -1.0, 0.0]


class Ctx:
    def __init__(self, nc, es):
        self.nc = nc
        self.es = es
        self._sems = {}
        self.cur_table = "default"
        self.bar_n = 0

    def semrec(self, name):
        if name not in self._sems:
            self._semgen = getattr(self, "_semgen", 0) + 1
            h = self.es.enter_context(self.nc.semaphore(f"{name}__{self._semgen}"))
            self._sems[name] = [h, 0]
        return self._sems[name]

    def inc(self, name, instr, k=1):
        rec = self.semrec(name)
        instr.then_inc(rec[0], k)
        rec[1] += k
        return rec[1]

    def val(self, name):
        return self.semrec(name)[1]

    def wait(self, eng, name, v=None):
        rec = self.semrec(name)
        eng.wait_ge(rec[0], rec[1] if v is None else v)

    def seq(self, eng, instr, tag):
        name = f"ord_{tag}"
        n = self.inc(name, instr, 1)
        eng.wait_ge(self.semrec(name)[0], n)

    def sb(self, name, shape, dt=BF16, stack=None):
        return (stack or self.es).enter_context(self.nc.sbuf_tensor(name, shape, dt))

    def ps(self, name, shape, dt=F32, stack=None):
        return (stack or self.es).enter_context(self.nc.psum_tensor(name, shape, dt))

    def make_consts(self, values):
        self.consts = {}
        for v in values:
            t = self.sb(f"cst_{str(v).replace('-','m').replace('.','_')}", [128, 1], F32)
            self.consts[v] = t
            self.nc.const_aps.register(F32, v, t[:, :])

    def constf(self, v):
        return self.consts[v][:, :]

    def table(self, scalar, name):
        if self.cur_table != name:
            from concourse.hw_specs import get_activation_tables
            setname = {NLEXP: "natural_log_exp_and_others",
                       SILU: "silu_and_others"}[name]
            idx = list(get_activation_tables(self.nc.m.arch).keys()).index(setname)
            scalar.drain()
            scalar.add_instruction(mybir.InstLoadActFuncSet(
                name=self.nc.get_next_instruction_name(),
                act_func_set_id=idx, ins=[], outs=[]))
            scalar.drain()
            self.cur_table = name


def barrier(c, block, sems=(), release=()):
    c.bar_n += 1
    n = c.bar_n
    names = ["bT", "bV", "bS", "bG", "bQ"]
    extra = [(nm, c.val(nm)) for nm in sems if c.val(nm) > 0]

    def stage1(eng, myname, incer):
        for nm, v in extra:
            eng.wait_ge(c.semrec(nm)[0], v)
        c.inc(myname, incer(eng), 1)

    block.tensor(lambda e: stage1(e, "bT", lambda e: e.nop()))
    block.vector(lambda e: stage1(e, "bV", lambda e: e.engine_nop()))
    block.scalar(lambda e: stage1(e, "bS", lambda e: e.nop()))
    block.gpsimd(lambda e: stage1(e, "bG", lambda e: e.nop()))
    block.sync(lambda e: stage1(e, "bQ", lambda e: e.nop()))

    @block.gpsimd
    def _(gpsimd):
        for nm in names:
            gpsimd.wait_ge(c.semrec(nm)[0], n)
        todrop = [k for k in list(c._sems)
                  if any(k.startswith(p) for p in release)]
        if todrop:
            c.nc.clear_and_free_semaphores([c._sems[k][0] for k in todrop])
            for k in todrop:
                del c._sems[k]
        c.inc("bC", gpsimd.nop(), 1)

    for mk in [block.tensor, block.vector, block.scalar, block.sync]:
        def body(eng):
            eng.wait_ge(c.semrec("bC")[0], n)
        mk(body)


def wt_kpf(ap, p=128):
    return ap.rearrange("(k p) f -> k p f", p=p).transpose([1, 0, 2])


def ceil_div(a, b):
    return (a + b - 1) // b


def ph_load(c, block, xin, x, ident, consts_gd_target):
    nc = c.nc
    st = ExitStack()
    io_a = c.sb("io_a", [128, D], BF16, st)
    io_b = c.sb("io_b", [128, D], BF16, st)
    ps_tr = c.ps("ps_ld", [128, 8 * 128], BF16, st)

    ntg = ceil_div(TE, 128)
    for gi in range(ntg):
        t0 = gi * 128
        n = min(128, TE - t0)
        buf = io_a if gi % 2 == 0 else io_b

        dsem = f"ld_dma{gi % 2}"

        @block.gpsimd
        def _(gpsimd, t0=t0, n=n, buf=buf, gi=gi, dsem=dsem):
            if gi >= 2:
                c.wait(gpsimd, "ld_tr", gi - 1)
            c.inc(dsem, gpsimd.dma_start(out=buf[0:n, :], in_=xin[t0:t0 + n, :]), 16)

        @block.tensor
        def _(tensor, t0=t0, n=n, buf=buf, gi=gi, dsem=dsem):
            c.wait(tensor, dsem, 16 * (gi // 2 + 1))
            if gi == 0:
                c.wait(tensor, "g_const", consts_gd_target)
            if gi >= 1:
                c.wait(tensor, "ld_ev", gi)
            for dc in range(8):
                h = tensor.transpose(ps_tr[:, dc * 128:dc * 128 + n],
                                     buf[0:n, dc * 128:(dc + 1) * 128], ident[0:n, 0:n])
            c.inc("ld_tr", h, 1)

        @block.vector
        def _(vector, t0=t0, n=n, gi=gi):
            c.wait(vector, "ld_tr", gi + 1)
            c.inc("ld_ev", vector.tensor_copy(
                out=x[:].rearrange("p (d t) -> p d t", d=8)[:, :, t0:t0 + n],
                in_=ps_tr[:].rearrange("p (d t) -> p d t", d=8)[:, :, 0:n]), 1)

    st.close()


def ph_layernorm(c, block, src, dst, tokens, ones128, ones_row, tag):
    nc = c.nc
    st = ExitStack()
    sq = c.sb(f"ln_sq_{tag}", [128, 8 * 512], BF16, st)
    ps_sum = c.ps(f"ln_ps_s_{tag}", [1, 512], F32, st)
    ps_sq = c.ps(f"ln_ps_q_{tag}", [1, 512], F32, st)
    ps_bc = c.ps(f"ln_ps_bc_{tag}", [128, 512], F32, st)
    rs0 = c.sb(f"ln_rs0_{tag}", [1, 512], F32, st)
    rs1 = c.sb(f"ln_rs1_{tag}", [1, 512], F32, st)
    rowm = c.sb(f"ln_rowm_{tag}", [1, 512], BF16, st)
    rowr = c.sb(f"ln_rowr_{tag}", [1, 512], BF16, st)
    rf0 = c.sb(f"ln_rf0_{tag}", [1, 512], F32, st)
    rf1 = c.sb(f"ln_rf1_{tag}", [1, 512], F32, st)
    mbc = c.sb(f"ln_mbc_{tag}", [128, 512], BF16, st)
    rbc = c.sb(f"ln_rbc_{tag}", [128, 512], BF16, st)

    ngr = ceil_div(tokens, 512)
    base_sq = c.val("sS")

    for gi in range(ngr):
        t0 = gi * 512
        n = min(512, tokens - t0)
        sview = src[:].rearrange("p (d t) -> p d t", d=8)

        @block.scalar
        def _(scalar, t0=t0, n=n, gi=gi):
            if gi >= 1:
                c.wait(scalar, "lnT", gi)
            for dc in range(8):
                h = scalar.activation(out=sq[:].rearrange("p (d t) -> p d t", d=8)[:, dc, 0:n],
                                      in_=sview[:, dc, t0:t0 + n], func=AF.Square)
            c.inc("lnS", h, 1)

        @block.tensor
        def _(tensor, t0=t0, n=n, gi=gi):
            c.wait(tensor, "lnS", gi + 1)
            if gi >= 1:
                c.wait(tensor, "lnV", 2 * gi)
            sqv = sq[:].rearrange("p (d t) -> p d t", d=8)
            for dc in range(8):
                tensor.matmul(ps_sum[0:1, 0:n], ones128[:, 0:1],
                              sview[:, dc, t0:t0 + n], start=(dc == 0), stop=(dc == 7))
            for dc in range(8):
                h = tensor.matmul(ps_sq[0:1, 0:n], ones128[:, 0:1],
                                  sqv[:, dc, 0:n], start=(dc == 0), stop=(dc == 7))
            c.inc("lnT", h, 1)

        @block.vector
        def _(vector, t0=t0, n=n, gi=gi):
            c.wait(vector, "lnT", gi + 1)
            vector.tensor_copy(out=rs0[0:1, 0:n], in_=ps_sum[0:1, 0:n])
            c.seq(vector, vector.tensor_copy(out=rs1[0:1, 0:n], in_=ps_sq[0:1, 0:n]), "V")
            vector.tensor_scalar_mul(out=rf0[0:1, 0:n], in0=rs0[0:1, 0:n], scalar1=1.0 / D)
            c.seq(vector, vector.tensor_scalar_mul(out=rf1[0:1, 0:n], in0=rs1[0:1, 0:n],
                                                   scalar1=1.0 / D), "V")
            c.seq(vector, vector.tensor_mul(out=rs0[0:1, 0:n], in0=rf0[0:1, 0:n],
                                            in1=rf0[0:1, 0:n]), "V")
            vector.tensor_sub(out=rs1[0:1, 0:n], in0=rf1[0:1, 0:n], in1=rs0[0:1, 0:n])
            c.inc("lnVm", vector.tensor_copy(out=rowm[:, 0:n], in_=rf0[0:1, 0:n]), 1)

        @block.scalar
        def _(scalar, t0=t0, n=n, gi=gi):
            c.wait(scalar, "lnVm", gi + 1)
            if gi >= 1:
                c.wait(scalar, "lnT2", 2 * gi)
            c.table(scalar, NLEXP)
            if os.environ.get("KNORSTD"):
                c.inc("lnS2", scalar.activation(out=rowr[:, 0:n], in_=rs1[0:1, 0:n],
                                                func=AF.Copy, scale=0.0, bias=1.0), 1)
            else:
                c.seq(scalar, scalar.activation(out=rf1[0:1, 0:n], in_=rs1[0:1, 0:n],
                                                func=AF.Ln, bias=1e-5), "S")
                c.inc("lnS2", scalar.activation(out=rowr[:, 0:n], in_=rf1[0:1, 0:n],
                                                func=AF.Exp, scale=-0.5), 1)

        @block.tensor
        def _(tensor, t0=t0, n=n, gi=gi):
            c.wait(tensor, "lnS2", gi + 1)
            c.inc("lnT2", tensor.matmul(ps_bc[:, 0:n], ones_row[0:1, :],
                                        rowm[0:1, 0:n], start=True, stop=True), 1)

        @block.vector
        def _(vector, gi=gi, n=n):
            c.wait(vector, "lnT2", 2 * gi + 1)
            c.inc("lnV2", vector.tensor_copy(out=mbc[:, 0:n], in_=ps_bc[:, 0:n]), 1)

        @block.tensor
        def _(tensor, t0=t0, n=n, gi=gi):
            c.wait(tensor, "lnV2", 2 * gi + 1)
            c.inc("lnT2", tensor.matmul(ps_bc[:, 0:n], ones_row[0:1, :],
                                        rowr[0:1, 0:n], start=True, stop=True), 1)

        @block.vector
        def _(vector, t0=t0, n=n, gi=gi):
            c.wait(vector, "lnT2", 2 * gi + 2)
            h = vector.tensor_copy(out=rbc[:, 0:n], in_=ps_bc[:, 0:n])
            nn = c.inc("lnV2", h, 1)
            c.wait(vector, "lnV2", nn)
            dview = dst[:].rearrange("p (d t) -> p d t", d=8)
            for dc in range(8):
                h = vector.tensor_sub(out=dview[:, dc, t0:t0 + n],
                                      in0=sview[:, dc, t0:t0 + n], in1=mbc[:, 0:n])
            c.seq(vector, h, "V")
            for dc in range(8):
                h = vector.tensor_mul(out=dview[:, dc, t0:t0 + n],
                                      in0=dview[:, dc, t0:t0 + n], in1=rbc[:, 0:n])
            c.inc("lnV", h, 2)

    barrier(c, block, sems=["lnV", "lnV2", "lnVm", "lnS", "lnS2", "lnT", "lnT2"], release=["ln"])
    st.close()


def ph_store(c, block, x, out, ident, toff=3, tstride=TE):
    nc = c.nc
    st = ExitStack()
    ps_o = c.ps("ps_st", [128, 8 * 128], BF16, st)
    ot_a = c.sb("st_a", [128, D], F32, st)
    ot_b = c.sb("st_b", [128, D], F32, st)

    for gi in range(TC // 128):
        t0 = toff + gi * 128
        ot = ot_a if gi % 2 == 0 else ot_b

        @block.tensor
        def _(tensor, t0=t0, gi=gi):
            if gi >= 1:
                c.wait(tensor, "st_ev", gi)
            for dc in range(8):
                h = tensor.transpose(
                    ps_o[:, dc * 128:(dc + 1) * 128],
                    x[:].rearrange("p (d t) -> p d t", t=tstride)[:, dc, t0:t0 + 128],
                    ident[:])
            c.inc("st_tr", h, 1)

        dsem = f"st_dma{gi % 2}"

        @block.vector
        def _(vector, gi=gi, ot=ot, dsem=dsem):
            c.wait(vector, "st_tr", gi + 1)
            if gi >= 2:
                c.wait(vector, dsem, 16 * (gi // 2))
            c.inc("st_ev", vector.tensor_copy(out=ot[:, :], in_=ps_o[:, :]), 1)

        @block.sync
        def _(sync, gi=gi, ot=ot, dsem=dsem):
            c.wait(sync, "st_ev", gi + 1)
            c.inc(dsem, sync.dma_start(out=out[gi * 128:(gi + 1) * 128, :],
                                       in_=ot[:, :]), 16)

    @block.sync
    def _(sync):
        c.wait(sync, "st_dma0")
        c.wait(sync, "st_dma1")
    st.close()


def ph_debug_dump(c, block, dbg_out, srcs):
    st = ExitStack()
    tmp = c.sb("dbg_tmp", [128, 4096], F32, st)
    col = 0
    for ap in srcs:
        ncol = ap.shape[-1] if len(ap.shape) == 2 else int(np.prod(ap.shape[1:]))

        @block.vector
        def _(vector, ap=ap, col=col, ncol=ncol):
            c.wait(vector, "dbgq", c.val("dbgq"))
            c.inc("dbgv", vector.tensor_copy(out=tmp[0:ap.shape[0], 0:ncol], in_=ap), 1)

        @block.sync
        def _(sync, col=col, ncol=ncol, ap=ap):
            c.wait(sync, "dbgv")
            c.inc("dbgq", sync.dma_start(
                out=dbg_out[0:ap.shape[0], col:col + ncol],
                in_=tmp[0:ap.shape[0], 0:ncol]), 16)
        col += ncol

    @block.sync
    def _(sync):
        c.wait(sync, "dbgq")
    st.close()



def ph_ffn0(c, block, xn, g, wdram, ones128, ones_row, ident, sel8):
    nc = c.nc
    NTG = TC // 128
    NT5 = TC // 512
    xnv = xn[:].rearrange("p (d t) -> p d t", d=8)

    stP = ExitStack()
    combrows = c.sb("ffn_combrows", [8, TC], BF16, stP)
    sgrow = c.sb("ffn_sgrow", [1, TC], BF16, stP)
    sgbc = c.sb("ffn_sgbc", [128, TC], BF16, stP)

    stA = ExitStack()
    rw = c.sb("ffn_rw", [128, 8 * 8], BF16, stA)
    sgwt = c.sb("ffn_sgw", [128, 8], BF16, stA)
    comb = c.sb("ffn_comb", [128, NTG * 8], BF16, stA)
    prob = c.sb("ffn_prob", [128, 8], F32, stA)
    gcmp = c.sb("ffn_gcmp", [128, 64], F32, stA)
    cnt = c.sb("ffn_cnt", [128, 8], F32, stA)
    ncol = c.sb("ffn_ncol", [128, 1], F32, stA)
    scol = c.sb("ffn_scol", [128, 1], F32, stA)
    sgcol = c.sb("ffn_sgcol", [128, 1], F32, stA)
    ps_r = c.ps("ffn_ps_r", [128, 8], F32, stA)
    ps_tt = c.ps("ffn_ps_tt", [8, 128], BF16, stA)
    ps_sg = c.ps("ffn_ps_sg", [128, 1], F32, stA)

    @block.gpsimd
    def _(gpsimd):
        c.inc("ffnAg", gpsimd.dma_start(
            out=rw[:].rearrange("p (k e) -> p k e", k=8),
            in_=wt_kpf(wdram["F0_router_w"])), 16)
        c.inc("ffnAg", gpsimd.dma_start(
            out=sgwt[:], in_=wdram["F0_sgw"].rearrange("(k p) -> k p", p=128).transpose([1, 0]),
            allow_slow_non_contiguous=True), 16)

    for tg in range(NTG):
        t0 = 3 + tg * 128

        @block.tensor
        def _(tensor, tg=tg, t0=t0):
            c.wait(tensor, "ffnAg", 32)
            if tg >= 1:
                c.wait(tensor, "ffnAv", 2 * tg)
            for kc in range(8):
                tensor.matmul(ps_r[:, :], xnv[:, kc, t0:t0 + 128],
                              rw[:, kc * 8:(kc + 1) * 8],
                              start=(kc == 0), stop=(kc == 7))
            for kc in range(8):
                h = tensor.matmul(ps_sg[:, :], xnv[:, kc, t0:t0 + 128],
                                  sgwt[:, kc:kc + 1],
                                  start=(kc == 0), stop=(kc == 7))
            c.inc("ffnAt", h, 1)

        @block.vector
        def _(vector, tg=tg, t0=t0):
            c.wait(vector, "ffnAt", tg + 1)
            c.seq(vector, vector.tensor_copy(out=prob[:], in_=ps_r[:, :]), "V")
            c.seq(vector, vector.tensor_reduce(out=ncol[:], in_=prob[:],
                                               axis=AX.X, op=OP.max), "V")
            c.seq(vector, vector.tensor_scalar_mul(out=ncol[:], in0=ncol[:],
                                                   scalar1=-1.0), "V")
            c.inc("ffnAv1", vector.engine_nop(), 1)

        @block.scalar
        def _(scalar, tg=tg):
            c.wait(scalar, "ffnAv1", tg + 1)
            c.inc("ffnAs", scalar.activation(out=prob[:], in_=prob[:], func=AF.Exp,
                                             bias=ncol[:, :]), 1)

        @block.vector
        def _(vector, tg=tg):
            c.wait(vector, "ffnAs", tg + 1)
            c.seq(vector, vector.tensor_reduce(out=scol[:], in_=prob[:],
                                               axis=AX.X, op=OP.add), "V")
            c.seq(vector, vector.reciprocal(out=scol[:], in_=scol[:]), "V")
            c.seq(vector, vector.tensor_scalar(out=prob[:], in0=prob[:],
                                               scalar1=scol[:, :], scalar2=None,
                                               op0=OP.mult), "V")
            c.seq(vector, vector.tensor_tensor(
                out=gcmp[:],
                in0=prob[:].unsqueeze(1).broadcast_to((128, 8, 8)),
                in1=prob[:].unsqueeze(2).broadcast_to((128, 8, 8)),
                op=OP.is_gt), "V")
            c.seq(vector, vector.tensor_reduce(out=cnt[:], in_=gcmp[:].rearrange(
                "p (e f) -> p e f", e=8), axis=AX.X, op=OP.add), "V")
            c.seq(vector, vector.tensor_single_scalar(out=cnt[:], in_=cnt[:],
                                                      scalar=1.5, op=OP.is_le), "V")
            h = vector.tensor_tensor(out=comb[:, tg * 8:(tg + 1) * 8],
                                     in0=prob[:], in1=cnt[:], op=OP.mult)
            c.inc("ffnAv", h, 2)

    for tg in range(NTG):
        @block.tensor
        def _(tensor, tg=tg):
            c.wait(tensor, "ffnAv", 2 * NTG)
            if tg >= 1:
                c.wait(tensor, "ffnAv2", 2 * tg)
            h = tensor.transpose(ps_tt[:, :], comb[:, tg * 8:(tg + 1) * 8],
                                 ident[:])
            c.inc("ffnAt2", h, 1)

        @block.vector
        def _(vector, tg=tg):
            c.wait(vector, "ffnAt2", tg + 1)
            c.inc("ffnAv2", vector.tensor_copy(
                out=combrows[:, tg * 128:(tg + 1) * 128], in_=ps_tt[:, :]), 1)
            c.inc("ffnAv2", vector.engine_nop(), 1)

    @block.vector
    def _(vector):
        c.wait(vector, "ffnAv2", 2 * NTG)
        c.inc("ffnAv2_done", vector.engine_nop(), 1)

    barrier(c, block, sems=["ffnAv", "ffnAv2", "ffnAv2_done", "ffnAt", "ffnAt2",
                            "ffnAs", "ffnAv1"], release=["ffnAg", "ffnAt", "ffnAv1", "ffnAs"])
    stA.close()

    stB = ExitStack()
    sgl = c.sb("ffn_sgl", [128, 1], BF16, stB)
    ps_sg2 = c.ps("ffn_ps_sg2", [128, 1], F32, stB)
    ps_tt2 = c.ps("ffn_ps_tt2", [1, 128], BF16, stB)
    sgwt2 = c.sb("ffn_sgw2", [128, 8], BF16, stB)
    ps_bc = c.ps("ffn_ps_bc", [128, 512], F32, stB)

    @block.gpsimd
    def _(gpsimd):
        c.inc("ffnBg", gpsimd.dma_start(
            out=sgwt2[:], in_=wdram["F0_sgw"].rearrange("(k p) -> k p", p=128).transpose([1, 0]),
            allow_slow_non_contiguous=True), 16)

    @block.scalar
    def _(scalar):
        c.table(scalar, SILU)
        c.inc("ffnBs0", scalar.nop(), 1)

    for tg in range(NTG):
        t0 = 3 + tg * 128

        @block.tensor
        def _(tensor, tg=tg, t0=t0):
            c.wait(tensor, "ffnBg", 16)
            if tg >= 1:
                c.wait(tensor, "ffnBv", 2 * tg)
            for kc in range(8):
                h = tensor.matmul(ps_sg2[:, :], xnv[:, kc, t0:t0 + 128],
                                  sgwt2[:, kc:kc + 1],
                                  start=(kc == 0), stop=(kc == 7))
            c.inc("ffnBt", h, 1)

        @block.scalar
        def _(scalar, tg=tg):
            c.wait(scalar, "ffnBt", tg + 1)
            c.wait(scalar, "ffnBs0", 1)
            c.inc("ffnBs", scalar.activation(out=sgl[:], in_=ps_sg2[:, :],
                                             func=AF.Tanh, scale=0.5), 1)

        @block.vector
        def _(vector, tg=tg):
            c.wait(vector, "ffnBs", tg + 1)
            c.seq(vector, vector.tensor_scalar(out=sgl[:], in0=sgl[:], scalar1=0.5,
                                               scalar2=0.5, op0=OP.mult, op1=OP.add), "V")
            c.inc("ffnBv1", vector.engine_nop(), 1)

        @block.tensor
        def _(tensor, tg=tg):
            c.wait(tensor, "ffnBv1", tg + 1)
            c.inc("ffnBt2", tensor.transpose(ps_tt2[:, :], sgl[:, :], ident[:]), 1)

        @block.vector
        def _(vector, tg=tg):
            c.wait(vector, "ffnBt2", tg + 1)
            h = vector.tensor_copy(out=sgrow[0:1, tg * 128:(tg + 1) * 128],
                                   in_=ps_tt2[0:1, :])
            c.inc("ffnBv", h, 2)

    for w5 in range(NT5):
        @block.tensor
        def _(tensor, w5=w5):
            c.wait(tensor, "ffnBv", 2 * NTG)
            if w5 >= 1:
                c.wait(tensor, "ffnBv2", w5)
            c.inc("ffnBt3", tensor.matmul(
                ps_bc[:, :], ones_row[0:1, :], sgrow[0:1, w5 * 512:(w5 + 1) * 512],
                start=True, stop=True), 1)

        @block.vector
        def _(vector, w5=w5):
            c.wait(vector, "ffnBt3", w5 + 1)
            c.inc("ffnBv2", vector.tensor_copy(
                out=sgbc[:, w5 * 512:(w5 + 1) * 512], in_=ps_bc[:, :]), 1)

    barrier(c, block, sems=["ffnBv", "ffnBv2", "ffnBs", "ffnBt", "ffnBt2",
                            "ffnBt3", "ffnBv1", "ffnBs0"], release=["ffnB"])
    stB.close()

    stC = ExitStack()
    w1b = [c.sb(f"ffn_w1_{i}", [128, 8 * 1024], BF16, stC) for i in range(2)]
    w2s = c.sb("ffn_w2s", [128, 8 * 1024], BF16, stC)
    he = c.sb("ffn_he", [128, 8 * TC], BF16, stC)
    cbc = c.sb("ffn_cbc", [128, TC], BF16, stC)
    HW = NT5 * 512
    ps_h = c.ps("ffn_ps_h", [128, 2 * HW], F32, stC)
    hev = he[:].rearrange("p (o t) -> p o t", o=8)
    gv = g[:].rearrange("p (d t) -> p d t", d=8)

    for e in range(8):
        wb1, wb2 = w1b[e % 2], w2s

        @block.gpsimd
        def _(gpsimd, e=e, wb1=wb1, wb2=wb2):
            if e >= 2:
                c.wait(gpsimd, "ffnCt_done", e - 1)
            c.inc(f"ffnCg{e % 2}", gpsimd.dma_start(
                out=wb1[:].rearrange("p (k f) -> p k f", k=8),
                in_=wt_kpf(wdram["F0_w1"].rearrange("E d f -> (E d) f")
                           [e * 1024:(e + 1) * 1024, :])), 16)
            if e >= 1:
                c.wait(gpsimd, "ffnCt_w2done", e)
            c.inc("ffnCgw2", gpsimd.dma_start(
                out=wb2[:].rearrange("p (k f) -> p k f", k=8),
                in_=wt_kpf(wdram["F0_w2"].rearrange("E d f -> (E d) f")
                           [e * 1024:(e + 1) * 1024, :])), 16)

        for w5 in range(NT5):
            @block.tensor
            def _(tensor, e=e, w5=w5):
                c.wait(tensor, "ffnCs_h", max(0, 8 * e - 1))
                if w5 == 0:
                    c.wait(tensor, "ffnAv2_done", 1)
                c.inc("ffnCt_cb", tensor.matmul(
                    ps_h[:, w5 * 512:(w5 + 1) * 512], sel8[:, e * 128:(e + 1) * 128],
                    combrows[:, w5 * 512:(w5 + 1) * 512],
                    start=True, stop=True), 1)

        @block.vector
        def _(vector, e=e):
            c.wait(vector, "ffnCt_cb", NT5 * (e + 1))
            c.inc("ffnCv_cb", vector.tensor_copy(out=cbc[:, :], in_=ps_h[:, 0:HW]), 1)

        for oc in range(8):
            slot = (oc % 2) * HW

            @block.tensor
            def _(tensor, e=e, oc=oc, wb1=wb1, slot=slot):
                c.wait(tensor, f"ffnCg{e % 2}", 16 * (e // 2 + 1))
                if oc == 0:
                    c.wait(tensor, "ffnCv_cb", e + 1)
                c.wait(tensor, "ffnCs_h", max(0, 8 * e + oc - 1))
                for kc in range(8):
                    for tg in range(NT5):
                        h = tensor.matmul(
                            ps_h[:, slot + tg * 512:slot + (tg + 1) * 512],
                            wb1[:, (kc * 8 + oc) * 128:(kc * 8 + oc) * 128 + 128],
                            xnv[:, kc, 3 + tg * 512:3 + (tg + 1) * 512],
                            start=(kc == 0), stop=(kc == 7))
                c.inc("ffnCt_h", h, 1)

            @block.scalar
            def _(scalar, e=e, oc=oc, slot=slot):
                c.wait(scalar, "ffnCt_h", 8 * e + oc + 1)
                c.inc("ffnCs_h", scalar.activation(
                    out=hev[:, oc, :], in_=ps_h[:, slot:slot + HW], func=AF.Silu), 1)

        @block.vector
        def _(vector, e=e):
            c.wait(vector, "ffnCs_h", 8 * (e + 1))
            h = vector.tensor_tensor(
                out=hev[:, :, :], in0=hev[:, :, :],
                in1=cbc[:].unsqueeze(1).broadcast_to((128, 8, TC)),
                op=OP.mult)
            c.inc("ffnCv_sc", h, 1)

        for oc in range(8):
            slot = (oc % 2) * HW

            @block.tensor
            def _(tensor, e=e, oc=oc, wb2=wb2, slot=slot):
                c.wait(tensor, "ffnCv_sc", e + 1)
                c.wait(tensor, "ffnCgw2", 16 * (e + 1))
                c.wait(tensor, "ffnCv_a", max(0, 8 * e + oc - 1))
                for kc in range(8):
                    for tg in range(NT5):
                        h = tensor.matmul(
                            ps_h[:, slot + tg * 512:slot + (tg + 1) * 512],
                            wb2[:, (kc * 8 + oc) * 128:(kc * 8 + oc) * 128 + 128],
                            hev[:, kc, tg * 512:(tg + 1) * 512],
                            start=(kc == 0), stop=(kc == 7))
                c.inc("ffnCt_a", h, 1)

            @block.vector
            def _(vector, e=e, oc=oc, slot=slot):
                c.wait(vector, "ffnCt_a", 8 * e + oc + 1)
                if e == 0:
                    h = vector.tensor_copy(out=gv[:, oc, :], in_=ps_h[:, slot:slot + HW])
                else:
                    h = vector.tensor_add(out=gv[:, oc, :], in0=gv[:, oc, :],
                                          in1=ps_h[:, slot:slot + HW])
                c.inc("ffnCv_a", h, 1)

        @block.tensor
        def _(tensor, e=e):
            c.wait(tensor, "ffnCv_a", 8 * (e + 1))
            c.inc("ffnCt_done", tensor.nop(), 1)
            c.inc("ffnCt_w2done", tensor.nop(), 1)

    barrier(c, block, sems=["ffnCv_a", "ffnCs_h", "ffnCt_a", "ffnCt_cb",
                            "ffnCv_cb", "ffnCv_sc", "ffnCt_h"],
            release=["ffnC", "ffnAv"])
    stC.close()

    stD = ExitStack()
    sgu_b = c.sb("ffn_sgu", [128, 16 * 1024], BF16, stD)
    sdn_b = c.sb("ffn_sdn", [128, 8 * 1024], BF16, stD)
    gate = c.sb("ffn_gate", [128, 8 * 512], BF16, stD)
    up = c.sb("ffn_up", [128, 8 * 512], BF16, stD)
    ps_h2 = c.ps("ffn_ps_h2", [128, 2 * 512], F32, stD)
    ps_a2 = c.ps("ffn_ps_a2", [128, 4 * 512], F32, stD)
    gatev = gate[:].rearrange("p (o t) -> p o t", o=8)
    upv = up[:].rearrange("p (o t) -> p o t", o=8)

    @block.gpsimd
    def _(gpsimd):
        c.inc("ffnDg", gpsimd.dma_start(
            out=sgu_b[:].rearrange("p (k f) -> p k f", k=8),
            in_=wt_kpf(wdram["F0_sgu"])), 16)
        c.inc("ffnDg", gpsimd.dma_start(
            out=sdn_b[:].rearrange("p (k f) -> p k f", k=8),
            in_=wt_kpf(wdram["F0_sdown"])), 16)

    for tg in range(NT5):
        t0 = 3 + tg * 512
        for oc in range(16):
            @block.tensor
            def _(tensor, tg=tg, oc=oc, t0=t0):
                c.wait(tensor, "ffnDg", 32)
                k = 16 * tg + oc
                if k >= 2:
                    kp = k - 2
                    ns = 8 * (kp // 16) + min(kp % 16 + 1, 8)
                    nv = 8 * (kp // 16) + max(kp % 16 + 1 - 8, 0)
                    if ns > 0:
                        c.wait(tensor, "ffnDs_g", ns)
                    if nv > 0:
                        c.wait(tensor, "ffnDv_u", nv)
                for kc in range(8):
                    h = tensor.matmul(
                        ps_h2[:, (oc % 2) * 512:(oc % 2) * 512 + 512],
                        sgu_b[:, (kc * 16 + oc) * 128:(kc * 16 + oc) * 128 + 128],
                        xnv[:, kc, t0:t0 + 512],
                        start=(kc == 0), stop=(kc == 7))
                c.inc("ffnDt", h, 1)

            if oc < 8:
                @block.scalar
                def _(scalar, tg=tg, oc=oc):
                    c.wait(scalar, "ffnDt", 16 * tg + oc + 1)
                    c.inc("ffnDs_g", scalar.activation(
                        out=gatev[:, oc, :], in_=ps_h2[:, (oc % 2) * 512:(oc % 2) * 512 + 512],
                        func=AF.Silu), 1)
            else:
                @block.vector
                def _(vector, tg=tg, oc=oc):
                    c.wait(vector, "ffnDt", 16 * tg + oc + 1)
                    c.inc("ffnDv_u", vector.tensor_copy(
                        out=upv[:, oc - 8, :],
                        in_=ps_h2[:, (oc % 2) * 512:(oc % 2) * 512 + 512]), 1)

        @block.vector
        def _(vector, tg=tg):
            c.wait(vector, "ffnDs_g", 8 * (tg + 1))
            c.wait(vector, "ffnDv_u", 8 * (tg + 1))
            c.seq(vector, vector.tensor_tensor(out=gate[:, :], in0=gate[:, :],
                                               in1=up[:, :], op=OP.mult), "V")
            h = vector.tensor_tensor(
                out=gatev[:, :, :], in0=gatev[:, :, :],
                in1=sgbc[:, tg * 512:(tg + 1) * 512].unsqueeze(1)
                    .broadcast_to((128, 8, 512)),
                op=OP.mult)
            c.seq(vector, h, "V")
            c.inc("ffnDv_su", vector.engine_nop(), 1)

        for oc in range(8):
            @block.tensor
            def _(tensor, tg=tg, oc=oc):
                c.wait(tensor, "ffnDv_su", tg + 1)
                k = 8 * tg + oc
                if k >= 4:
                    c.wait(tensor, "ffnDv_a", k - 3)
                for kc in range(8):
                    h = tensor.matmul(
                        ps_a2[:, (oc % 4) * 512:(oc % 4) * 512 + 512],
                        sdn_b[:, (kc * 8 + oc) * 128:(kc * 8 + oc) * 128 + 128],
                        gatev[:, kc, :],
                        start=(kc == 0), stop=(kc == 7))
                c.inc("ffnDt_a", h, 1)

            @block.vector
            def _(vector, tg=tg, oc=oc):
                c.wait(vector, "ffnDt_a", 8 * tg + oc + 1)
                h = vector.tensor_add(
                    out=gv[:, oc, tg * 512:(tg + 1) * 512],
                    in0=gv[:, oc, tg * 512:(tg + 1) * 512],
                    in1=ps_a2[:, (oc % 4) * 512:(oc % 4) * 512 + 512])
                c.inc("ffnDv_a", h, 1)

    barrier(c, block, sems=["ffnDv_a", "ffnDt_a", "ffnDs_g", "ffnDv_u",
                            "ffnDv_su", "ffnDt"], release=["ffnD"])
    stD.close()

    for dc in range(8):
        @block.scalar
        def _(scalar, dc=dc):
            h = scalar.activation(out=gv[:, dc, :], in_=gv[:, dc, :], func=AF.Silu)
            c.inc("ffnEs", h, 1)
    barrier(c, block, sems=["ffnEs"], release=["ffnE"])
    stP.close()



def ph_attn0(c, block, x, xn, g, wdram, ones128, ones_row, ident, sel33, chm, ccs):
    nc = c.nc
    TT5 = 512
    NTT = TC // TT5
    xv = x[:].rearrange("p (d t) -> p d t", d=8)
    xnv = xn[:].rearrange("p (d t) -> p d t", d=8)
    gv = g[:].rearrange("p (d t) -> p d t", d=8)
    cc0_in, cc0_out = ccs[0], ccs[1]

    stP = ExitStack()
    As = c.sb("a0_As", [128, 16 * S0], F32, stP)
    convw = c.sb("a0_convw", [128, 16 * 4], F32, stP)
    convb = c.sb("a0_convb", [128, 16], F32, stP)
    wdtc = c.sb("a0_wdt", [128, 16], F32, stP)
    bdtc = c.sb("a0_bdt", [128, 16], F32, stP)
    dpc = c.sb("a0_dp", [128, 16], F32, stP)
    wxp = c.sb("a0_wxp", [128, 16 * 33], BF16, stP)
    carry = c.sb("a0_carry", [128, 16 * S0], F32, stP)
    cdel = c.sb("a0_cdel", [128, 16], F32, stP)
    dsacc = c.sb("a0_dsacc", [128, 16 * 4], F32, stP)
    xdblk = c.sb("a0_xdbl", [33, TC], BF16, stP)
    Hbuf = c.sb("a0_H", [128, 16 * S0], F32, stP)
    Pacc = c.sb("a0_Pacc", [128, 16 * S0], F32, stP)
    Prun = c.sb("a0_Prun", [128, 16 * S0], F32, stP)
    tcmb = c.sb("a0_tcmb", [128, 16 * S0], F32, stP)
    dsum = c.sb("a0_dsum", [128, 16], F32, stP)

    y2t = c.sb("a0_y2", [128, 16 * TT5], BF16, stP)
    winb = [c.sb(f"a0_win{i}", [128, 8 * 128], BF16, stP) for i in range(2)]
    woutb = [c.sb(f"a0_wout{i}", [128, 16 * 128], BF16, stP) for i in range(2)]
    xd0bc = c.sb("a0_xd0bc", [128, TT5], BF16, stP)
    vpb = c.sb("a0_vpb", [128, TT5], F32, stP)
    schb = [c.sb(f"a0_sch{i}", [128, TT5], F32, stP) for i in range(2)]
    delta = c.sb("a0_delta", [128, TT5], BF16, stP)
    gb = c.sb("a0_gb", [128, 2 * TT5], BF16, stP)
    zs = c.sb("a0_zs", [128, TT5], BF16, stP)
    cum = c.sb("a0_cum", [128, TT5], F32, stP)
    ycor = c.sb("a0_ycor", [128, TT5], BF16, stP)
    zex = c.sb("a0_zex", [128, TT5], F32, stP)
    agj = c.sb("a0_agj", [128, 2 * 16 * S0], F32, stP)
    ones512 = c.sb("a0_ones512", [128, TT5], BF16, stP)

    stPA = ExitStack()
    xcp = c.sb("a0_xcp", [128, 16 * (TT5 + 3)], BF16, stPA)
    ctx3 = c.sb("a0_ctx", [128, 16 * 3], BF16, stPA)
    bbc = c.sb("a0_bbc", [128, 2 * TT5], BF16, stPA)
    cbc2 = c.sb("a0_cbc2", [128, 2 * TT5], BF16, stPA)
    wdc = c.sb("a0_wdc", [128, TT5], BF16, stPA)
    dab = [c.sb(f"a0_dab{i}", [128, 2 * (TT5 + 1)], BF16, stPA) for i in range(2)]
    ub = [c.sb(f"a0_ub{i}", [128, 2 * (TT5 + 1)], BF16, stPA) for i in range(2)]
    hb = [c.sb(f"a0_hb{i}", [128, 2 * (TT5 + 1)], BF16, stPA) for i in range(2)]
    ydc = c.sb("a0_ydc", [128, TT5], BF16, stPA)

    ps_a = c.ps("a0_ps_a", [128, 2 * TT5], F32, stP)
    ps_xd = c.ps("a0_ps_xd", [33, TT5], F32, stP)
    ps_bc = c.ps("a0_ps_bc", [128, 2 * TT5], F32, stP)

    xcpv = xcp[:].rearrange("p (k t) -> p k t", k=16)
    y2v = y2t[:].rearrange("p (k t) -> p k t", k=16)

    @block.gpsimd
    def _(gpsimd):
        c.inc("a0g", gpsimd.dma_start(out=As[:].rearrange("p (k s) -> p k s", k=16),
                                      in_=wt_kpf(wdram["L0_A_log"])), 16)
        c.inc("a0g", gpsimd.dma_start(out=convw[:].rearrange("p (k s) -> p k s", k=16),
                                      in_=wt_kpf(wdram["L0_conv_w"])), 16)
        for nm, dst in [("L0_conv_b", convb), ("L0_W_dt", wdtc),
                        ("L0_b_dt", bdtc), ("L0_Dp", dpc)]:
            c.inc("a0g", gpsimd.dma_start(
                out=dst[:], in_=wdram[nm].rearrange("(k p) -> k p", p=128)
                .transpose([1, 0]), allow_slow_non_contiguous=True), 16)
        c.inc("a0g", gpsimd.dma_start(out=wxp[:].rearrange("p (k s) -> p k s", k=16),
                                      in_=wt_kpf(wdram["L0_W_xproj"])), 16)
        gpsimd.memset(carry[:], 0.0)
        gpsimd.memset(cdel[:], 0.0)
        gpsimd.memset(dab[0][:], 0.0)
        gpsimd.memset(dab[1][:], 0.0)
        c.inc("a0g2", gpsimd.memset(ones512[:], 1.0), 1)

    @block.scalar
    def _(scalar):
        c.wait(scalar, "a0g", c.val("a0g"))
        c.wait(scalar, "a0g2", 1)
        c.table(scalar, NLEXP)
        c.inc("a0s0", scalar.activation(out=As[:], in_=As[:], func=AF.Exp), 1)

    @block.vector
    def _(vector):
        c.wait(vector, "a0s0", 1)
        c.seq(vector, vector.tensor_scalar_mul(out=As[:], in0=As[:], scalar1=-1.0), "V")
        c.inc("a0v0", vector.tensor_scalar_mul(out=convb[:], in0=convb[:],
                                               scalar1=-1.0), 1)

    def tail_z_y2(tt, dc, M):
        tb = 3 + tt * TT5
        k = 16 * tt + dc

        @block.gpsimd
        def _(gpsimd, tt=tt, dc=dc, k=k):
            if M == "A" and dc < 2:
                c.wait(gpsimd, "a0t_xz", 16 * (tt + 1))
            if k >= 2:
                c.wait(gpsimd, f"a0t_{M}z", k - 1)
            c.inc(f"a0g_{M}z{dc % 2}", gpsimd.dma_start(
                out=winb[dc % 2][:].rearrange("p (k f) -> p k f", k=8),
                in_=wt_kpf(wdram["L0_W_in"])[:, :, 2048 + dc * 128:2048 + (dc + 1) * 128]),
                16)

        @block.tensor
        def _(tensor, tt=tt, dc=dc, k=k):
            c.wait(tensor, f"a0g_{M}z{dc % 2}", 16 * (k // 2 + 1))
            if M == "A":
                c.wait(tensor, "a0v_bc", 8 * (k + 1))
            if k >= 2:
                c.wait(tensor, f"a0v_{M}z", k - 1)
            slot = (dc % 2) * TT5
            for kc in range(8):
                h = tensor.matmul(ps_a[:, slot:slot + TT5],
                                  winb[dc % 2][:, kc * 128:(kc + 1) * 128],
                                  xnv[:, kc, tb:tb + TT5],
                                  start=(kc == 0), stop=(kc == 7))
            c.inc(f"a0t_{M}z", h, 1)

        @block.scalar
        def _(scalar, tt=tt, dc=dc, k=k):
            c.wait(scalar, f"a0t_{M}z", k + 1)
            if k >= 1:
                c.wait(scalar, f"a0v_{M}z", k)
            slot = (dc % 2) * TT5
            c.inc(f"a0s_{M}z", scalar.activation(
                out=zex[:, :], in_=ps_a[:, slot:slot + TT5], func=AF.Exp,
                scale=-1.0), 1)

        @block.vector
        def _(vector, tt=tt, dc=dc, k=k):
            c.wait(vector, f"a0s_{M}z", k + 1)
            slot = (dc % 2) * TT5
            c.seq(vector, vector.tensor_copy(out=zs[:, :],
                                             in_=ps_a[:, slot:slot + TT5]), "V")
            c.seq(vector, vector.tensor_scalar(
                out=zex[:, :], in0=zex[:, :], scalar1=1.0, scalar2=None,
                op0=OP.add), "V")
            c.seq(vector, vector.reciprocal(out=zex[:, :], in_=zex[:, :]), "V")
            c.seq(vector, vector.tensor_mul(out=zs[:, :], in0=zs[:, :],
                                            in1=zex[:, :]), "V")
            if M == "A":
                c.seq(vector, vector.scalar_tensor_tensor(
                    out=gb[:, 0:TT5], in0=xcpv[:, dc, 3:3 + TT5],
                    scalar=dpc[:, dc:dc + 1], in1=ydc[:, :],
                    op0=OP.mult, op1=OP.add), "V")
                h = vector.tensor_mul(out=y2v[:, dc, :], in0=gb[:, 0:TT5], in1=zs[:, :])
            else:
                h = vector.tensor_mul(out=y2v[:, dc, :], in0=ycor[:, :], in1=zs[:, :])
            nn = c.inc(f"a0v_{M}z", h, 1)
            c.wait(vector, f"a0v_{M}z", nn)

    def tail_wout_x(tt, M):
        for oc in range(8):
            k = 8 * tt + oc

            @block.gpsimd
            def _(gpsimd, tt=tt, oc=oc, k=k):
                if k >= 2:
                    c.wait(gpsimd, f"a0t_{M}o", k - 1)
                c.inc(f"a0g_{M}o{oc % 2}", gpsimd.dma_start(
                    out=woutb[oc % 2][:].rearrange("p (k f) -> p k f", k=16),
                    in_=wt_kpf(wdram["L0_W_out"])[:, :, oc * 128:(oc + 1) * 128]), 16)

            @block.tensor
            def _(tensor, tt=tt, oc=oc, k=k):
                c.wait(tensor, f"a0g_{M}o{oc % 2}", 16 * (k // 2 + 1))
                c.wait(tensor, f"a0v_{M}z", 16 * (tt + 1))
                if k >= 2:
                    c.wait(tensor, f"a0v_{M}o2", k - 1)
                slot = (oc % 2) * TT5
                for dc in range(16):
                    h = tensor.matmul(ps_a[:, slot:slot + TT5],
                                      woutb[oc % 2][:, dc * 128:(dc + 1) * 128],
                                      y2v[:, dc, :],
                                      start=(dc == 0), stop=(dc == 15))
                c.inc(f"a0t_{M}o", h, 1)

            @block.vector
            def _(vector, tt=tt, oc=oc, k=k):
                c.wait(vector, f"a0t_{M}o", k + 1)
                slot = (oc % 2) * TT5
                tb = 3 + tt * TT5
                c.seq(vector, vector.tensor_mul(
                    out=gb[:, TT5:2 * TT5],
                    in0=gv[:, oc, tt * TT5:(tt + 1) * TT5],
                    in1=ps_a[:, slot:slot + TT5]), "V")
                h = vector.tensor_add(out=xv[:, oc, tb:tb + TT5],
                                      in0=xv[:, oc, tb:tb + TT5],
                                      in1=gb[:, TT5:2 * TT5])
                nn = c.inc(f"a0v_{M}o2", h, 1)
                c.wait(vector, f"a0v_{M}o2", nn)

    for tt in range(NTT):
        tb = 3 + tt * TT5

        for oc in range(16):
            wb = winb[oc % 2]
            k = 16 * tt + oc

            @block.gpsimd
            def _(gpsimd, tt=tt, oc=oc, wb=wb, k=k):
                if tt > 0 and oc < 2:
                    c.wait(gpsimd, "a0t_Az", 16 * tt)
                if k >= 2:
                    c.wait(gpsimd, "a0t_xz", k - 1)
                c.inc(f"a0g_w{oc % 2}", gpsimd.dma_start(
                    out=wb[:].rearrange("p (k f) -> p k f", k=8),
                    in_=wt_kpf(wdram["L0_W_in"])[:, :, oc * 128:(oc + 1) * 128]), 16)

            @block.tensor
            def _(tensor, tt=tt, oc=oc, wb=wb, tb=tb, k=k):
                c.wait(tensor, f"a0g_w{oc % 2}", 16 * (k // 2 + 1))
                if k >= 2:
                    c.wait(tensor, "a0v_xz", k - 1)
                if tt == 0 and oc == 0:
                    c.wait(tensor, "a0v0", 1)
                if tt > 0 and oc == 0:
                    c.wait(tensor, "a0v_ctx", tt)
                if tt > 0 and oc == 0:
                    c.wait(tensor, "a0v_bc", 128 * tt)
                if tt > 0 and oc < 2:
                    c.wait(tensor, "a0v_Ao2", 8 * tt)
                slot = (oc % 2) * TT5
                for kc in range(8):
                    h = tensor.matmul(ps_a[:, slot:slot + TT5],
                                      wb[:, kc * 128:(kc + 1) * 128],
                                      xnv[:, kc, tb:tb + TT5],
                                      start=(kc == 0), stop=(kc == 7))
                if tt == 0:
                    for kc in range(8):
                        h = tensor.matmul(ps_bc[:, slot:slot + 3],
                                          wb[:, kc * 128:(kc + 1) * 128],
                                          xnv[:, kc, 0:3],
                                          start=(kc == 0), stop=(kc == 7))
                c.inc("a0t_xz", h, 1)

            @block.vector
            def _(vector, tt=tt, oc=oc, k=k):
                c.wait(vector, "a0t_xz", k + 1)
                slot = (oc % 2) * TT5
                if tt == 0:
                    c.seq(vector, vector.tensor_copy(out=xcpv[:, oc, 0:3],
                                                     in_=ps_bc[:, slot:slot + 3]), "V")
                else:
                    c.seq(vector, vector.tensor_copy(
                        out=xcpv[:, oc, 0:3],
                        in_=ctx3[:].rearrange("p (k t) -> p k t", k=16)[:, oc, :]), "V")
                h = vector.tensor_copy(out=xcpv[:, oc, 3:3 + TT5],
                                       in_=ps_a[:, slot:slot + TT5])
                c.inc("a0v_xz", h, 1)

        @block.vector
        def _(vector, tt=tt):
            c.wait(vector, "a0v_xz", 16 * (tt + 1))
            h = vector.tensor_copy(out=ctx3[:].rearrange("p (k t) -> p k t", k=16),
                                   in_=xcpv[:, :, TT5:TT5 + 3])
            nn = c.inc("a0v_ctx", h, 1)
            c.wait(vector, "a0v_ctx", nn)

        for dc in range(16):
            k = 16 * tt + dc

            @block.vector
            def _(vector, tt=tt, dc=dc, k=k):
                if k >= 2:
                    c.wait(vector, "a0s_cv", k - 1)
                sch = schb[dc % 2]
                c.seq(vector, vector.tensor_scalar(
                    out=sch[:, :], in0=xcpv[:, dc, 0:TT5],
                    scalar1=convw[:, dc * 4:dc * 4 + 1], scalar2=None,
                    op0=OP.mult), "V")
                for j in range(1, 4):
                    h = vector.scalar_tensor_tensor(
                        out=sch[:, :], in0=xcpv[:, dc, j:j + TT5],
                        scalar=convw[:, dc * 4 + j:dc * 4 + j + 1],
                        in1=sch[:, :], op0=OP.mult, op1=OP.add)
                    if j < 3:
                        c.seq(vector, h, "V")
                c.inc("a0v_cv", h, 1)

            @block.scalar
            def _(scalar, tt=tt, dc=dc, k=k):
                c.wait(scalar, "a0v_cv", k + 1)
                if k >= 1:
                    c.wait(scalar, "a0v_cv2", k)
                h = scalar.activation(out=vpb[:, :], in_=schb[dc % 2][:, :],
                                      func=AF.Exp, scale=-1.0, bias=convb[:, dc:dc + 1])
                c.inc("a0s_cv", h, 1)

            @block.vector
            def _(vector, tt=tt, dc=dc, k=k):
                c.wait(vector, "a0s_cv", k + 1)
                c.seq(vector, vector.tensor_scalar(
                    out=vpb[:, :], in0=vpb[:, :], scalar1=1.0, scalar2=None,
                    op0=OP.add), "V")
                c.seq(vector, vector.reciprocal(out=vpb[:, :], in_=vpb[:, :]), "V")
                c.seq(vector, vector.tensor_scalar(
                    out=gb[:, 0:TT5], in0=schb[dc % 2][:, :],
                    scalar1=convb[:, dc:dc + 1], scalar2=None, op0=OP.subtract), "V")
                h = vector.tensor_mul(out=xcpv[:, dc, 3:3 + TT5], in0=gb[:, 0:TT5],
                                      in1=vpb[:, :])
                c.inc("a0v_cv2", h, 1)

        @block.tensor
        def _(tensor, tt=tt):
            c.wait(tensor, "a0v_cv2", 16 * (tt + 1))
            if tt >= 1:
                c.wait(tensor, "a0v_xd", tt)
            for kc in range(16):
                h = tensor.matmul(ps_xd[:, :], wxp[:, kc * 33:(kc + 1) * 33],
                                  xcpv[:, kc, 3:3 + TT5],
                                  start=(kc == 0), stop=(kc == 15))
            c.inc("a0t_xd", h, 1)

        @block.vector
        def _(vector, tt=tt):
            c.wait(vector, "a0t_xd", tt + 1)
            c.inc("a0v_xd", vector.tensor_copy(
                out=xdblk[:, tt * TT5:(tt + 1) * TT5], in_=ps_xd[:, :]), 1)

        @block.tensor
        def _(tensor, tt=tt):
            c.wait(tensor, "a0v_xd", tt + 1)
            if tt >= 1:
                c.wait(tensor, "a0v_xd0", tt)
                c.wait(tensor, "a0v_bc", 128 * tt)
            c.inc("a0t_xd0", tensor.matmul(
                ps_bc[:, TT5:2 * TT5], sel33[:, 0:128],
                xdblk[:, tt * TT5:(tt + 1) * TT5], start=True, stop=True), 1)

        @block.vector
        def _(vector, tt=tt):
            c.wait(vector, "a0t_xd0", tt + 1)
            c.inc("a0v_xd0", vector.tensor_copy(out=xd0bc[:, :],
                                                in_=ps_bc[:, TT5:2 * TT5]), 1)

        for dc in range(16):
            k = 16 * tt + dc

            @block.vector
            def _(vector, tt=tt, dc=dc, k=k):
                c.wait(vector, "a0v_xd0", tt + 1)
                if k >= 1:
                    c.wait(vector, "a0s_dl", 2 * k)
                h = vector.tensor_scalar(
                    out=vpb[:, :], in0=xd0bc[:, :],
                    scalar1=wdtc[:, dc:dc + 1], scalar2=bdtc[:, dc:dc + 1],
                    op0=OP.mult, op1=OP.add)
                c.inc("a0v_dl1", h, 1)

            @block.scalar
            def _(scalar, tt=tt, dc=dc, k=k):
                c.wait(scalar, "a0v_dl1", k + 1)
                if k >= 1:
                    c.wait(scalar, "a0v_dl3", k)
                c.seq(scalar, scalar.activation(out=schb[0][:, :], in_=vpb[:, :],
                                                func=AF.Abs), "S")
                c.seq(scalar, scalar.activation(out=schb[0][:, :], in_=schb[0][:, :],
                                                func=AF.Exp, scale=-1.0), "S")
                h = scalar.activation(out=schb[1][:, :], in_=schb[0][:, :],
                                      func=AF.Ln, bias=1.0)
                c.inc("a0s_dl", h, 2)

            @block.vector
            def _(vector, tt=tt, dc=dc, k=k):
                c.wait(vector, "a0s_dl", 2 * (k + 1))
                if k >= 1:
                    c.wait(vector, "a0s_da", 8 * k)
                h = vector.scalar_tensor_tensor(
                    out=delta[:, :], in0=vpb[:, :], scalar=0.0, in1=schb[1][:, :],
                    op0=OP.max, op1=OP.add,
                    accum_out=dsacc[:, tt * 16 + dc:tt * 16 + dc + 1])
                nn = c.inc("a0v_dl2", h, 1)
                c.wait(vector, "a0v_dl2", nn)
                h = vector.tensor_mul(out=wdc[:, :], in0=delta[:, :],
                                      in1=xcpv[:, dc, 3:3 + TT5])
                nn = c.inc("a0v_dl3", h, 1)
                c.wait(vector, "a0v_dl3", nn)

            for sp in range(8):
                sidx = (16 * tt + dc) * 8 + sp
                par = sp % 2

                @block.tensor
                def _(tensor, tt=tt, dc=dc, sp=sp, sidx=sidx):
                    c.wait(tensor, "a0v_dl3", 16 * tt + dc + 1)
                    if sidx >= 1:
                        c.wait(tensor, "a0v_bc", sidx)
                    for half in range(2):
                        s = 2 * sp + half
                        tensor.matmul(ps_bc[:, half * TT5:(half + 1) * TT5],
                                      sel33[:, (1 + s) * 128:(2 + s) * 128],
                                      xdblk[:, tt * TT5:(tt + 1) * TT5],
                                      start=True, stop=True)
                        h = tensor.matmul(ps_a[:, half * TT5:(half + 1) * TT5],
                                          sel33[:, (17 + s) * 128:(18 + s) * 128],
                                          xdblk[:, tt * TT5:(tt + 1) * TT5],
                                          start=True, stop=True)
                    c.inc("a0t_bc", h, 1)

                @block.vector
                def _(vector, tt=tt, dc=dc, sp=sp, sidx=sidx, par=par):
                    c.wait(vector, "a0t_bc", sidx + 1)
                    c.seq(vector, vector.tensor_copy(out=bbc[:, :], in_=ps_bc[:, :]), "V")
                    h = vector.tensor_copy(out=cbc2[:, :], in_=ps_a[:, :])
                    nn = c.inc("a0v_bc", h, 1)
                    c.wait(vector, "a0v_bc", nn)
                    ubv = ub[par][:].rearrange("p (a b) -> p a b", a=2)
                    c.seq(vector, vector.tensor_tensor(
                        out=ubv[:, :, 1:1 + TT5],
                        in0=wdc[:].unsqueeze(1).broadcast_to((128, 2, TT5)),
                        in1=bbc[:].rearrange("p (a b) -> p a b", a=2),
                        op=OP.mult), "V")
                    c.seq(vector, vector.tensor_copy(
                        out=ubv[:, :, 0:1],
                        in_=carry[:, dc * S0 + 2 * sp:dc * S0 + 2 * sp + 2]
                        .rearrange("p (a b) -> p a b", a=2)), "V")
                    c.wait(vector, "a0s_da", sidx + 1)
                    h = vector.tensor_tensor_scan(
                        out=hb[par][:, :], data0=dab[par][:, :], data1=ub[par][:, :],
                        initial=0.0, op0=OP.mult, op1=OP.add)
                    c.seq(vector, h, "V")
                    hbv = hb[par][:].rearrange("p (a b) -> p a b", a=2)
                    c.seq(vector, vector.tensor_copy(
                        out=carry[:, dc * S0 + 2 * sp:dc * S0 + 2 * sp + 2]
                        .rearrange("p (a b) -> p a b", a=2),
                        in_=hbv[:, :, TT5:TT5 + 1]), "V")
                    c.seq(vector, vector.tensor_tensor(
                        out=gb[:].rearrange("p (a b) -> p a b", a=2),
                        in0=hbv[:, :, 1:1 + TT5],
                        in1=cbc2[:].rearrange("p (a b) -> p a b", a=2),
                        op=OP.mult), "V")
                    if sp == 0:
                        h = vector.tensor_add(out=ydc[:, :], in0=gb[:, 0:TT5],
                                              in1=gb[:, TT5:2 * TT5])
                    else:
                        c.seq(vector, vector.tensor_add(
                            out=ydc[:, :], in0=ydc[:, :], in1=gb[:, 0:TT5]), "V")
                        h = vector.tensor_add(out=ydc[:, :], in0=ydc[:, :],
                                              in1=gb[:, TT5:2 * TT5])
                    nn = c.inc("a0v_scan", h, 1)
                    c.wait(vector, "a0v_scan", nn)

                @block.scalar
                def _(scalar, tt=tt, dc=dc, sp=sp, sidx=sidx, par=par):
                    c.wait(scalar, "a0v_dl2", 16 * tt + dc + 1)
                    if sidx >= 2:
                        c.wait(scalar, "a0v_scan", sidx - 1)
                    dav = dab[par][:].rearrange("p (a b) -> p a b", a=2)
                    c.seq(scalar, scalar.activation(
                        out=dav[:, 0, 1:1 + TT5], in_=delta[:, :], func=AF.Exp,
                        scale=As[:, dc * S0 + 2 * sp:dc * S0 + 2 * sp + 1]), "S")
                    h = scalar.activation(
                        out=dav[:, 1, 1:1 + TT5], in_=delta[:, :], func=AF.Exp,
                        scale=As[:, dc * S0 + 2 * sp + 1:dc * S0 + 2 * sp + 2])
                    c.inc("a0s_da", h, 1)

            tail_z_y2(tt, dc, "A")

        tail_wout_x(tt, "A")

    barrier(c, block, sems=["a0v_Az", "a0v_Ao2", "a0t_Ao", "a0s_Az", "a0v_scan",
                            "a0s_da", "a0v_bc", "a0t_bc", "a0s_cv", "a0v_cv",
                            "a0v_xz", "a0t_xz", "a0v_dl2", "a0v_dl3", "a0s_dl", "a0v_cv2"],
            release=["a0g_w", "a0t_xz", "a0v_xz", "a0v_ctx", "a0v_cv", "a0s_cv", "a0v_cv2",
                     "a0t_xd", "a0v_xd", "a0v_dl", "a0s_dl", "a0t_bc", "a0v_bc",
                     "a0s_da", "a0v_scan", "a0g_Az", "a0t_Az", "a0s_Az", "a0v_Az",
                     "a0g_Ao", "a0t_Ao", "a0v_Ao2"])
    stPA.close()
    cbcall = c.sb("a0_cbcall", [128, 16 * TT5], BF16, stP)
    cbav = cbcall[:].rearrange("p (s t) -> p s t", s=16)

    @block.vector
    def _(vector):
        c.seq(vector, vector.tensor_copy(out=dsum[:, :], in_=dsacc[:, 0:16]), "V")
        for _tt in range(1, NTT):
            c.seq(vector, vector.tensor_add(
                out=dsum[:, :], in0=dsum[:, :],
                in1=dsacc[:, _tt * 16:(_tt + 1) * 16]), "V")
        for dc in range(16):
            h = vector.tensor_scalar(
                out=Pacc[:, dc * S0:(dc + 1) * S0],
                in0=As[:, dc * S0:(dc + 1) * S0],
                scalar1=dsum[:, dc:dc + 1], scalar2=None, op0=OP.mult)
        c.inc("a0v_dec1", h, 1)

    @block.scalar
    def _(scalar):
        c.wait(scalar, "a0v_dec1", 1)
        c.inc("a0s_dec", scalar.activation(out=Pacc[:], in_=Pacc[:], func=AF.Exp), 1)

    @block.sync
    def _(sync):
        c.wait(sync, "a0s_dec", 1)
        c.inc("a0q_cc", sync.dma_start(out=cc0_in[:, 0:256], in_=Pacc[:]), 16)
        c.inc("a0q_cc", sync.dma_start(out=cc0_in[:, 256:512], in_=carry[:]), 16)

    @block.gpsimd
    def _(gpsimd):
        c.wait(gpsimd, "a0q_cc", 32)
        c.inc("a0g_cc", gpsimd.collective_compute(
            "AllGather", OP.bypass, ins=[cc0_in], outs=[cc0_out],
            replica_groups=[[0, 1, 2, 3], [4, 5, 6, 7]]), 1)

    @block.gpsimd
    def _(gpsimd):
        c.wait(gpsimd, "a0g_cc", 1)
        gpsimd.memset(Hbuf[:], 0.0)
        c.inc("a0g_hm", gpsimd.memset(Prun[:], 1.0), 1)

    for j in [2, 1, 0]:
        @block.sync
        def _(sync, j=j):
            c.wait(sync, "a0g_cc", 1)
            if j < 2:
                c.wait(sync, "a0v_cmb", 2 - j)
            c.inc("a0q_ag", sync.dma_start(
                out=agj[:], in_=cc0_out[j * 128:(j + 1) * 128, :]), 16)

        @block.vector
        def _(vector, j=j):
            c.wait(vector, "a0q_ag", 16 * (3 - j))
            c.wait(vector, "a0g_hm", 1)
            m = chm[:, j:j + 1]
            c.seq(vector, vector.tensor_mul(out=tcmb[:, :], in0=agj[:, 256:512],
                                            in1=Prun[:, :]), "V")
            c.seq(vector, vector.tensor_scalar(out=tcmb[:, :], in0=tcmb[:, :],
                                               scalar1=m, scalar2=None,
                                               op0=OP.mult), "V")
            c.seq(vector, vector.tensor_add(out=Hbuf[:, :], in0=Hbuf[:, :],
                                            in1=tcmb[:, :]), "V")
            c.seq(vector, vector.tensor_scalar(out=tcmb[:, :], in0=agj[:, 0:256],
                                               scalar1=1.0, scalar2=m,
                                               op0=OP.subtract, op1=OP.mult), "V")
            c.seq(vector, vector.tensor_scalar(out=tcmb[:, :], in0=tcmb[:, :],
                                               scalar1=1.0, scalar2=None,
                                               op0=OP.add), "V")
            h = vector.tensor_mul(out=Prun[:, :], in0=Prun[:, :], in1=tcmb[:, :])
            nn = c.inc("a0v_cmb", h, 1)
            c.wait(vector, "a0v_cmb", nn)

    barrier(c, block, sems=["a0v_cmb"])

    for tt in range(NTT):
        tb = 3 + tt * TT5

        @block.tensor
        def _(tensor, tt=tt):
            if tt >= 1:
                c.wait(tensor, "a0v_xd0B", tt)
                c.wait(tensor, "a0v_cbl", 16 * tt)
            c.inc("a0t_xd0B", tensor.matmul(
                ps_bc[:, 0:TT5], sel33[:, 0:128],
                xdblk[:, tt * TT5:(tt + 1) * TT5], start=True, stop=True), 1)

        @block.vector
        def _(vector, tt=tt):
            c.wait(vector, "a0t_xd0B", tt + 1)
            c.inc("a0v_xd0B", vector.tensor_copy(out=xd0bc[:, :],
                                                 in_=ps_bc[:, 0:TT5]), 1)

        for s in range(16):
            @block.tensor
            def _(tensor, tt=tt, s=s):
                c.wait(tensor, "a0v_xd0B", tt + 1)
                if tt * 16 + s >= 2:
                    c.wait(tensor, "a0v_cbl", 16 * tt + s - 1)
                c.inc("a0t_cbl", tensor.matmul(
                    ps_bc[:, (s % 2) * TT5:(s % 2 + 1) * TT5],
                    sel33[:, (17 + s) * 128:(18 + s) * 128],
                    xdblk[:, tt * TT5:(tt + 1) * TT5], start=True, stop=True), 1)

            @block.vector
            def _(vector, tt=tt, s=s):
                c.wait(vector, "a0t_cbl", 16 * tt + s + 1)
                c.inc("a0v_cbl", vector.tensor_copy(
                    out=cbav[:, s, :], in_=ps_bc[:, (s % 2) * TT5:(s % 2 + 1) * TT5]), 1)

        for dc in range(16):
            k = 16 * tt + dc

            @block.vector
            def _(vector, tt=tt, dc=dc, k=k):
                c.wait(vector, "a0v_cbl", 16 * (tt + 1))
                if k >= 1:
                    c.wait(vector, "a0s_dlB", 2 * k)
                h = vector.tensor_scalar(
                    out=vpb[:, :], in0=xd0bc[:, :],
                    scalar1=wdtc[:, dc:dc + 1], scalar2=bdtc[:, dc:dc + 1],
                    op0=OP.mult, op1=OP.add)
                c.inc("a0v_dl1B", h, 1)

            @block.scalar
            def _(scalar, tt=tt, dc=dc, k=k):
                c.wait(scalar, "a0v_dl1B", k + 1)
                if k >= 1:
                    c.wait(scalar, "a0v_dl2B", k)
                    c.wait(scalar, "a0v_pb", 16 * k)
                c.seq(scalar, scalar.activation(out=schb[0][:, :], in_=vpb[:, :],
                                                func=AF.Abs), "S")
                c.seq(scalar, scalar.activation(out=schb[0][:, :], in_=schb[0][:, :],
                                                func=AF.Exp, scale=-1.0), "S")
                h = scalar.activation(out=schb[1][:, :], in_=schb[0][:, :],
                                      func=AF.Ln, bias=1.0)
                c.inc("a0s_dlB", h, 2)

            @block.vector
            def _(vector, tt=tt, dc=dc, k=k):
                c.wait(vector, "a0s_dlB", 2 * (k + 1))
                if k >= 1:
                    c.wait(vector, "a0s_pb", 16 * k)
                c.seq(vector, vector.scalar_tensor_tensor(
                    out=delta[:, :], in0=vpb[:, :], scalar=0.0, in1=schb[1][:, :],
                    op0=OP.max, op1=OP.add), "V")
                c.seq(vector, vector.tensor_tensor_scan(
                    out=cum[:, :], data0=ones512[:, :], data1=delta[:, :],
                    initial=cdel[:, dc:dc + 1], op0=OP.mult, op1=OP.add), "V")
                h = vector.tensor_copy(out=cdel[:, dc:dc + 1], in_=cum[:, TT5 - 1:TT5])
                nn = c.inc("a0v_dl2B", h, 1)
                c.wait(vector, "a0v_dl2B", nn)

            for s in range(16):
                pidx = (16 * tt + dc) * 16 + s

                @block.scalar
                def _(scalar, tt=tt, dc=dc, s=s, pidx=pidx):
                    c.wait(scalar, "a0v_dl2B", 16 * tt + dc + 1)
                    if pidx >= 2:
                        c.wait(scalar, "a0v_pb", pidx - 1)
                    h = scalar.activation(
                        out=schb[s % 2][:, :], in_=cum[:, :], func=AF.Exp,
                        scale=As[:, dc * S0 + s:dc * S0 + s + 1])
                    c.inc("a0s_pb", h, 1)

                @block.vector
                def _(vector, tt=tt, dc=dc, s=s, pidx=pidx):
                    c.wait(vector, "a0s_pb", pidx + 1)
                    c.seq(vector, vector.tensor_mul(
                        out=gb[:, 0:TT5], in0=schb[s % 2][:, :], in1=cbav[:, s, :]), "V")
                    if s == 0:
                        h = vector.tensor_scalar(
                            out=ycor[:, :], in0=gb[:, 0:TT5],
                            scalar1=Hbuf[:, dc * S0 + s:dc * S0 + s + 1],
                            scalar2=None, op0=OP.mult)
                    else:
                        h = vector.scalar_tensor_tensor(
                            out=ycor[:, :], in0=gb[:, 0:TT5],
                            scalar=Hbuf[:, dc * S0 + s:dc * S0 + s + 1],
                            in1=ycor[:, :], op0=OP.mult, op1=OP.add)
                    nn = c.inc("a0v_pb", h, 1)
                    c.wait(vector, "a0v_pb", nn)

            tail_z_y2(tt, dc, "B")

        tail_wout_x(tt, "B")

    barrier(c, block, sems=["a0v_Bz", "a0v_Bo2", "a0t_Bo", "a0s_Bz", "a0v_pb",
                            "a0s_pb", "a0v_cbl", "a0t_cbl", "a0v_dl2B", "a0s_dlB"],
            release=["a0"])
    stP.close()


def ph_edge_exchange(c, block, x, chm, prevsel, ccs):
    nc = c.nc
    ccx_in, ccx_out = ccs
    st = ExitStack()
    ed = c.sb("ed_buf", [128, 8 * 3], F32, st)
    eda = c.sb("ed_all", [128, 4 * 24], F32, st)
    xv = x[:].rearrange("p (d t) -> p d t", d=8)

    @block.vector
    def _(vector):
        c.inc("edv", vector.tensor_copy(
            out=ed[:].rearrange("p (d t) -> p d t", d=8),
            in_=xv[:, :, TE - 3:TE]), 1)

    @block.sync
    def _(sync):
        c.wait(sync, "edv", 1)
        c.inc("edq", sync.dma_start(out=ccx_in[:, :], in_=ed[:]), 16)

    @block.gpsimd
    def _(gpsimd):
        c.wait(gpsimd, "edq", 16)
        c.inc("edg", gpsimd.collective_compute(
            "AllGather", OP.bypass, ins=[ccx_in], outs=[ccx_out],
            replica_groups=[[0, 1, 2, 3], [4, 5, 6, 7]]), 1)

    @block.sync
    def _(sync):
        c.wait(sync, "edg", 1)
        c.inc("edq2", sync.dma_start(
            out=eda[:].rearrange("p (j t) -> p j t", j=4),
            in_=ccx_out.rearrange("(j p) t -> j p t", p=128).transpose([1, 0, 2])), 16)

    @block.vector
    def _(vector):
        c.wait(vector, "edq2", 16)
        c.seq(vector, vector.tensor_scalar(
            out=ed[:, :], in0=eda[:, 0:24], scalar1=prevsel[:, 0:1],
            scalar2=None, op0=OP.mult), "V")
        for j in range(1, 4):
            c.seq(vector, vector.scalar_tensor_tensor(
                out=ed[:, :], in0=eda[:, j * 24:(j + 1) * 24],
                scalar=prevsel[:, j:j + 1], in1=ed[:, :],
                op0=OP.mult, op1=OP.add), "V")
        h = vector.tensor_copy(out=xv[:, :, 0:3],
                               in_=ed[:].rearrange("p (d t) -> p d t", d=8))
        nn = c.inc("edv2", h, 1)
        c.wait(vector, "edv2", nn)

    barrier(c, block, sems=["edv2"], release=["ed"])
    st.close()


def ph_ffn1(c, block, xn, g, wdram, ones128, ones_row, ones_rowf, ident, identf):
    nc = c.nc
    NT5 = TC // 512
    xnv = xn[:].rearrange("p (d t) -> p d t", d=8)
    gv = g[:].rearrange("p (d t) -> p d t", d=8)
    st = ExitStack()
    qk = c.sb("f1_qk", [128, 8 * 1024], BF16, st)
    qf = c.sb("f1_qf", [128, 4 * 1024], F32, st)
    qh = c.sb("f1_qh", [128, 4 * 1024], F32, st)
    at = c.sb("f1_a", [128, 8 * 16], BF16, st)
    bt = c.sb("f1_b", [16, 1024], BF16, st)
    bias = c.sb("f1_bias", [128, 8], F32, st)
    mx = c.sb("f1_mx", [128, 2], F32, st)
    mxr = c.sb("f1_mxr", [1, 128], F32, st)
    mxc = c.sb("f1_mxc", [128, 1], F32, st)
    u16 = c.sb("f1_u", [16, TC], BF16, st)
    ps_m = c.ps("f1_ps", [128, 2 * 512], F32, st)
    ps_u = c.ps("f1_psu", [16, 512], F32, st)
    ps_t = c.ps("f1_pst", [1, 128], F32, st)
    mxf = c.sb("f1_mxf", [128, 1], F32, st)

    @block.gpsimd
    def _(gpsimd):
        c.inc("f1g", gpsimd.dma_start(
            out=at[:].rearrange("p (k r) -> p k r", k=8), in_=wt_kpf(wdram["Q_a"])), 16)
        c.inc("f1g", gpsimd.dma_start(out=bt[:], in_=wdram["Q_b"]), 16)
        c.inc("f1g", gpsimd.dma_start(
            out=bias[:], in_=wdram["Q_bias"].rearrange("(k p) -> k p", p=128)
            .transpose([1, 0]), allow_slow_non_contiguous=True), 16)

    for hh in range(2):
        @block.gpsimd
        def _(gpsimd, hh=hh):
            if hh == 1:
                c.wait(gpsimd, "f1v_mx", 1)
            c.inc("f1g_q", gpsimd.dma_start(
                out=qf[:].rearrange("p (k f) -> p k f", k=4),
                in_=wt_kpf(wdram["Q_kernel"])[:, hh * 4:(hh + 1) * 4, :]), 16)

        @block.vector
        def _(vector, hh=hh):
            c.wait(vector, "f1g_q", 16 * (hh + 1))
            h = vector.tensor_reduce(out=mx[:, hh:hh + 1], in_=qf[:, :],
                                     axis=AX.X, op=OP.max, apply_absolute_value=True)
            c.inc("f1v_mx", h, 1)

    @block.vector
    def _(vector):
        c.wait(vector, "f1v_mx", 2)
        c.seq(vector, vector.tensor_reduce(out=mx[:, 0:1], in_=mx[:, 0:2],
                                           axis=AX.X, op=OP.max), "V")
        c.inc("f1v_mx2", vector.tensor_copy(out=mxf[:, 0:1], in_=mx[:, 0:1]), 1)

    @block.tensor
    def _(tensor):
        c.wait(tensor, "f1v_mx2", 1)
        c.inc("f1t_mx", tensor.transpose(ps_t[0:1, 0:128], mxf[:, 0:1], identf[:]), 1)

    @block.vector
    def _(vector):
        c.wait(vector, "f1t_mx", 1)
        c.seq(vector, vector.tensor_copy(out=mxr[0:1, :], in_=ps_t[0:1, 0:128]), "V")
        c.seq(vector, vector.tensor_reduce(out=mx[0:1, 0:1], in_=mxr[0:1, :],
                                           axis=AX.X, op=OP.max), "V")
        c.inc("f1v_mx3", vector.engine_nop(), 1)

    @block.tensor
    def _(tensor):
        c.wait(tensor, "f1v_mx3", 1)
        c.inc("f1t_bc", tensor.matmul(ps_m[:, 0:1], ones_rowf[0:1, :],
                                      mx[0:1, 0:1], start=True, stop=True), 1)

    @block.vector
    def _(vector):
        c.wait(vector, "f1t_bc", 1)
        c.seq(vector, vector.tensor_scalar_mul(out=mxc[:, 0:1], in0=ps_m[:, 0:1],
                                               scalar1=1.0 / 7.0), "V")
        c.seq(vector, vector.reciprocal(out=mx[:, 0:1], in_=mxc[:, 0:1]), "V")
        c.inc("f1v_sc", vector.engine_nop(), 1)

    for hh in range(2):
        @block.gpsimd
        def _(gpsimd, hh=hh):
            c.wait(gpsimd, "f1v_mx", 2)
            if hh == 1:
                c.wait(gpsimd, "f1v_qz", 1)
            c.inc("f1g_q2", gpsimd.dma_start(
                out=qf[:].rearrange("p (k f) -> p k f", k=4),
                in_=wt_kpf(wdram["Q_kernel"])[:, hh * 4:(hh + 1) * 4, :]), 16)

        @block.vector
        def _(vector, hh=hh):
            c.wait(vector, "f1g_q2", 16 * (hh + 1))
            c.wait(vector, "f1v_sc", 1)
            c.seq(vector, vector.tensor_scalar(
                out=qf[:, :], in0=qf[:, :], scalar1=mx[:, 0:1], scalar2=None,
                op0=OP.mult), "V")
            c.seq(vector, vector.tensor_single_scalar(
                out=qh[:, :], in_=qf[:, :], scalar=-7.5, op=OP.is_ge), "V")
            for kk in range(-6, 8):
                c.seq(vector, vector.scalar_tensor_tensor(
                    out=qh[:, :], in0=qf[:, :], scalar=float(kk) - 0.5,
                    in1=qh[:, :], op0=OP.is_ge, op1=OP.add), "V")
            c.seq(vector, vector.tensor_scalar(
                out=qf[:, :], in0=qh[:, :], scalar1=-8.0, scalar2=None,
                op0=OP.add), "V")
            h = vector.tensor_scalar(
                out=qk[:, hh * 4096:(hh + 1) * 4096], in0=qf[:, :],
                scalar1=mxc[:, 0:1], scalar2=None, op0=OP.mult)
            nn = c.inc("f1v_qz", h, 1)
            c.wait(vector, "f1v_qz", nn)

    for tg in range(NT5):
        @block.tensor
        def _(tensor, tg=tg):
            c.wait(tensor, "f1g", 48)
            if tg >= 1:
                c.wait(tensor, "f1v_u", tg)
            for kc in range(8):
                h = tensor.matmul(ps_u[:, :], at[:, kc * 16:(kc + 1) * 16],
                                  xnv[:, kc, 3 + tg * 512:3 + (tg + 1) * 512],
                                  start=(kc == 0), stop=(kc == 7))
            c.inc("f1t_u", h, 1)

        @block.vector
        def _(vector, tg=tg):
            c.wait(vector, "f1t_u", tg + 1)
            c.inc("f1v_u", vector.tensor_copy(
                out=u16[:, tg * 512:(tg + 1) * 512], in_=ps_u[:, :]), 1)

    @block.scalar
    def _(scalar):
        c.table(scalar, SILU)
        c.inc("f1s0", scalar.nop(), 1)

    for oc in range(8):
        for tg in range(NT5):
            k = NT5 * oc + tg
            slot = (k % 2) * 512

            @block.tensor
            def _(tensor, oc=oc, tg=tg, k=k, slot=slot):
                c.wait(tensor, "f1v_qz", 2)
                c.wait(tensor, "f1v_u", NT5)
                if k >= 2:
                    c.wait(tensor, "f1s_g", k - 1)
                for kc in range(8):
                    tensor.matmul(ps_m[:, slot:slot + 512],
                                  qk[:, (kc * 8 + oc) * 128:(kc * 8 + oc) * 128 + 128],
                                  xnv[:, kc, 3 + tg * 512:3 + (tg + 1) * 512],
                                  start=(kc == 0), stop=False)
                h = tensor.matmul(ps_m[:, slot:slot + 512],
                                  bt[:, oc * 128:(oc + 1) * 128],
                                  u16[:, tg * 512:(tg + 1) * 512],
                                  start=False, stop=True)
                c.inc("f1t_g", h, 1)

            @block.scalar
            def _(scalar, oc=oc, tg=tg, k=k, slot=slot):
                c.wait(scalar, "f1t_g", k + 1)
                c.wait(scalar, "f1s0", 1)
                h = scalar.activation(out=gv[:, oc, tg * 512:(tg + 1) * 512],
                                      in_=ps_m[:, slot:slot + 512], func=AF.Silu,
                                      bias=bias[:, oc:oc + 1])
                c.inc("f1s_g", h, 1)

    barrier(c, block, sems=["f1s_g", "f1t_g", "f1v_qz", "f1v_u", "f1v_sc",
                            "f1t_u", "f1g", "f1s0"], release=["f1"])
    st.close()


def ph_attn1(c, block, x, xn, g, wdram, ones128, ones_row, ident, sel32_in,
             sel128_in, chm, ccs):
    nc = c.nc
    TT5 = 512
    NTT = TC // TT5
    S1L = 64
    xv = x[:].rearrange("p (d t) -> p d t", d=8)
    xnv = xn[:].rearrange("p (d t) -> p d t", d=8)
    gv = g[:].rearrange("p (d t) -> p d t", d=8)
    cc1_in, cc1_out = ccs

    stP = ExitStack()
    sel32 = c.sb("a1_sel32", [32, 16 * 128], F32, stP)[:]
    sel128 = c.sb("a1_sel128", [128, 64 * 128], BF16, stP)[:]
    Arep = c.sb("a1_Arep", [128, 16 * S1L], F32, stP)
    Atmp = c.sb("a1_Atmp", [32, S1L], F32, stP)
    Dsrc = c.sb("a1_Dsrc", [32, 1], F32, stP)
    Drep = c.sb("a1_Drep", [128, 16], F32, stP)
    dtb = c.sb("a1_dtb", [32, 1], F32, stP)
    dtw = c.sb("a1_dtw", [128, 8 * 32], BF16, stP)
    convw = c.sb("a1_convw", [128, 16 * 4], F32, stP)
    convb = c.sb("a1_convb", [128, 16], F32, stP)
    carry = c.sb("a1_carry", [128, 16 * S1L], F32, stP)
    dsacc = c.sb("a1_dsacc", [32, 4], F32, stP)
    dsum32 = c.sb("a1_dsum32", [32, 1], F32, stP)
    cdel32 = c.sb("a1_cdel32", [32, 1], F32, stP)
    dtt = c.sb("a1_dtt", [32, TT5], F32, stP)
    dts1 = c.sb("a1_dts1", [32, TT5], F32, stP)
    dts2 = c.sb("a1_dts2", [32, TT5], F32, stP)
    ones512 = c.sb("a1_ones512", [32, TT5], BF16, stP)
    y2t = c.sb("a1_y2", [128, 16 * TT5], BF16, stP)
    winb = [c.sb(f"a1_win{i}", [128, 8 * 128], BF16, stP) for i in range(2)]
    woutb = [c.sb("a1_wout", [128, 16 * 128], BF16, stP)] * 2
    y2v = y2t[:].rearrange("p (k t) -> p k t", k=16)

    stPA = ExitStack()
    bcw = [c.sb("a1_bcw", [128, 8 * 256], BF16, stPA)] * 2
    xh = c.sb("a1_xh", [128, TT5 + 3], BF16, stPA)
    ctx3 = c.sb("a1_ctx", [128, 16 * 3], BF16, stPA)
    Bt = c.sb("a1_Bt", [128, TT5], BF16, stPA)
    Ct = c.sb("a1_Ct", [128, TT5], BF16, stPA)
    drep = c.sb("a1_drep", [128, TT5], F32, stPA)
    wdc = c.sb("a1_wdc", [128, TT5], BF16, stPA)
    sch1 = c.sb("a1_sch1", [128, TT5], F32, stPA)
    vpb = c.sb("a1_vpb", [128, TT5], F32, stPA)
    dab = [c.sb(f"a1_dab{i}", [128, 2 * (TT5 + 1)], F32, stPA) for i in range(2)]
    ub = [c.sb("a1_ub", [128, 2 * (TT5 + 1)], BF16, stPA)] * 2
    hb = [c.sb("a1_hb", [128, 2 * (TT5 + 1)], BF16, stPA)] * 2
    ydc = c.sb("a1_ydc", [128, TT5], F32, stPA)
    gb = c.sb("a1_gb", [128, 2 * TT5], BF16, stPA)
    zs = c.sb("a1_zs", [128, TT5], BF16, stPA)
    zex = c.sb("a1_zex", [128, TT5], F32, stPA)

    ps_a = c.ps("a1_ps_a", [128, 2 * TT5], F32, stP)
    ps_bc = c.ps("a1_ps_bc", [128, 2 * TT5], F32, stP)
    ps_c = c.ps("a1_ps_c", [128, 2 * TT5], F32, stP)
    ps_dr = c.ps("a1_ps_dr", [128, TT5], F32, stP)
    ps_K = c.ps("a1_ps_K", [32, TT5], F32, stP)

    WIN = wdram["L1_W_inT"]
    WOUT = wdram["L1_W_outT"]

    @block.gpsimd
    def _(gpsimd):
        c.inc("a1g", gpsimd.dma_start(out=sel32, in_=sel32_in), 16)
        c.inc("a1g", gpsimd.dma_start(out=sel128, in_=sel128_in), 16)
        c.inc("a1g", gpsimd.dma_start(out=Atmp[:], in_=wdram["L1_A_log"]), 16)
        c.inc("a1g", gpsimd.dma_start(
            out=Dsrc[:], in_=wdram["L1_D"].rearrange("(n o) -> n o", o=1)), 16)
        c.inc("a1g", gpsimd.dma_start(
            out=dtb[:], in_=wdram["L1_dt_bias"].rearrange("(n o) -> n o", o=1)), 16)
        c.inc("a1g", gpsimd.dma_start(out=convw[:].rearrange("p (k s) -> p k s", k=16),
                                      in_=wt_kpf(wdram["L1_conv_w"])), 16)
        c.inc("a1g", gpsimd.dma_start(
            out=convb[:], in_=wdram["L1_conv_b"].rearrange("(k p) -> k p", p=128)
            .transpose([1, 0]), allow_slow_non_contiguous=True), 16)
        for kc in range(8):
            c.inc("a1g", gpsimd.dma_start(
                out=dtw[:, kc * 32:(kc + 1) * 32],
                in_=wt_kpf(WIN)[:, kc, 4096:8224].rearrange(
                    "p (n r) -> p n r", r=129)[:, :, 128],
                allow_slow_non_contiguous=True), 16)
        gpsimd.memset(carry[:], 0.0)
        gpsimd.memset(cdel32[:], 0.0)
        gpsimd.memset(dab[0][:], 0.0)
        gpsimd.memset(dab[1][:], 0.0)
        c.inc("a1g2", gpsimd.memset(ones512[:], 1.0), 1)

    @block.scalar
    def _(scalar):
        c.wait(scalar, "a1g", c.val("a1g"))
        c.wait(scalar, "a1g2", 1)
        c.table(scalar, NLEXP)
        c.inc("a1s0", scalar.activation(out=Atmp[:], in_=Atmp[:], func=AF.Exp), 1)

    @block.vector
    def _(vector):
        c.wait(vector, "a1s0", 1)
        c.seq(vector, vector.tensor_scalar_mul(out=Atmp[:], in0=Atmp[:],
                                               scalar1=-1.0), "V")
        c.inc("a1v0", vector.tensor_scalar_mul(out=convb[:], in0=convb[:],
                                               scalar1=-1.0), 1)

    for dc in range(16):
        @block.tensor
        def _(tensor, dc=dc):
            c.wait(tensor, "a1v0", 1)
            if dc >= 1:
                c.wait(tensor, "a1v_ar", dc)
            tensor.matmul(ps_dr[:, 0:S1L], sel32[:, dc * 128:(dc + 1) * 128],
                          Atmp[:], start=True, stop=True)
            c.inc("a1t_ar", tensor.matmul(
                ps_dr[:, S1L:S1L + 1], sel32[:, dc * 128:(dc + 1) * 128],
                Dsrc[:], start=True, stop=True), 1)

        @block.vector
        def _(vector, dc=dc):
            c.wait(vector, "a1t_ar", dc + 1)
            c.seq(vector, vector.tensor_copy(
                out=Arep[:, dc * S1L:(dc + 1) * S1L], in_=ps_dr[:, 0:S1L]), "V")
            h = vector.tensor_copy(out=Drep[:, dc:dc + 1],
                                   in_=ps_dr[:, S1L:S1L + 1])
            nn = c.inc("a1v_ar", h, 1)
            c.wait(vector, "a1v_ar", nn)

    def tail_z_y2(tt, dc, M):
        tb = 3 + tt * TT5
        k = 16 * tt + dc

        @block.gpsimd
        def _(gpsimd, tt=tt, dc=dc, k=k):
            if M == "A":
                c.wait(gpsimd, "a1t_pr", k + 1)
            if k >= 2:
                c.wait(gpsimd, f"a1t_{M}z", k - 1)
            c.inc(f"a1g_{M}z{dc % 2}", gpsimd.dma_start(
                out=winb[dc % 2][:].rearrange("p (k f) -> p k f", k=8),
                in_=wt_kpf(WIN)[:, :, 2048 + dc * 128:2048 + (dc + 1) * 128]), 16)

        @block.tensor
        def _(tensor, tt=tt, dc=dc, k=k):
            c.wait(tensor, f"a1g_{M}z{dc % 2}", 16 * (k // 2 + 1))
            if M == "A":
                c.wait(tensor, "a1v_yD", k + 1)
            else:
                c.wait(tensor, "a1v_pb", S1L * (k + 1))
            if k >= 2:
                c.wait(tensor, f"a1v_{M}z", k - 1)
            for kc in range(8):
                h = tensor.matmul(ps_a[:, TT5:2 * TT5],
                                  winb[dc % 2][:, kc * 128:(kc + 1) * 128],
                                  xnv[:, kc, tb:tb + TT5],
                                  start=(kc == 0), stop=(kc == 7))
            c.inc(f"a1t_{M}z", h, 1)

        @block.scalar
        def _(scalar, tt=tt, dc=dc, k=k):
            c.wait(scalar, f"a1t_{M}z", k + 1)
            if k >= 1:
                c.wait(scalar, f"a1v_{M}z", k)
            c.inc(f"a1s_{M}z", scalar.activation(
                out=zex[:, :], in_=ps_a[:, TT5:2 * TT5], func=AF.Exp,
                scale=-1.0), 1)

        @block.vector
        def _(vector, tt=tt, dc=dc, k=k):
            c.wait(vector, f"a1s_{M}z", k + 1)
            c.seq(vector, vector.tensor_copy(out=zs[:, :],
                                             in_=ps_a[:, TT5:2 * TT5]), "V")
            c.seq(vector, vector.tensor_scalar(
                out=zex[:, :], in0=zex[:, :], scalar1=1.0, scalar2=None,
                op0=OP.add), "V")
            c.seq(vector, vector.reciprocal(out=zex[:, :], in_=zex[:, :]), "V")
            c.seq(vector, vector.tensor_mul(out=zs[:, :], in0=zs[:, :],
                                            in1=zex[:, :]), "V")
            if M == "A":
                h = vector.tensor_mul(out=y2v[:, dc, :], in0=ydc[:, :], in1=zs[:, :])
            else:
                h = vector.tensor_mul(out=y2v[:, dc, :], in0=gb[:, 0:TT5],
                                      in1=zs[:, :])
            nn = c.inc(f"a1v_{M}z", h, 1)
            c.wait(vector, f"a1v_{M}z", nn)

    def tail_wout_x(tt, M):
        for oc in range(8):
            k = 8 * tt + oc

            @block.gpsimd
            def _(gpsimd, tt=tt, oc=oc, k=k):
                if k >= 1:
                    c.wait(gpsimd, f"a1t_{M}o", k)
                c.inc(f"a1g_{M}o{oc % 2}", gpsimd.dma_start(
                    out=woutb[oc % 2][:].rearrange("p (k f) -> p k f", k=16),
                    in_=wt_kpf(WOUT)[:, :, oc * 128:(oc + 1) * 128]), 16)

            @block.tensor
            def _(tensor, tt=tt, oc=oc, k=k):
                c.wait(tensor, f"a1g_{M}o{oc % 2}", 16 * (k // 2 + 1))
                c.wait(tensor, f"a1v_{M}z", 16 * (tt + 1))
                if k >= 2:
                    c.wait(tensor, f"a1v_{M}o2", k - 1)
                slot = (oc % 2) * TT5
                for dc in range(16):
                    h = tensor.matmul(ps_c[:, slot:slot + TT5],
                                      woutb[oc % 2][:, dc * 128:(dc + 1) * 128],
                                      y2v[:, dc, :],
                                      start=(dc == 0), stop=(dc == 15))
                c.inc(f"a1t_{M}o", h, 1)

            @block.vector
            def _(vector, tt=tt, oc=oc, k=k):
                c.wait(vector, f"a1t_{M}o", k + 1)
                slot = (oc % 2) * TT5
                tb = 3 + tt * TT5
                c.seq(vector, vector.tensor_mul(
                    out=gb[:, TT5:2 * TT5],
                    in0=gv[:, oc, tt * TT5:(tt + 1) * TT5],
                    in1=ps_c[:, slot:slot + TT5]), "V")
                h = vector.tensor_add(out=xv[:, oc, tb:tb + TT5],
                                      in0=xv[:, oc, tb:tb + TT5],
                                      in1=gb[:, TT5:2 * TT5])
                nn = c.inc(f"a1v_{M}o2", h, 1)
                c.wait(vector, f"a1v_{M}o2", nn)

    def dt_block(tt, M):
        tb = 3 + tt * TT5
        ti = (0 if M == "A" else NTT) + tt

        @block.tensor
        def _(tensor, tt=tt, ti=ti):
            if ti == 0:
                c.wait(tensor, "a1v_ar", 16)
            else:
                c.wait(tensor, f"a1v_dt", ti)
            for kc in range(8):
                h = tensor.matmul(ps_K[:, :], dtw[:, kc * 32:(kc + 1) * 32],
                                  xnv[:, kc, tb:tb + TT5],
                                  start=(kc == 0), stop=(kc == 7))
            c.inc("a1t_dt", h, 1)

        @block.scalar
        def _(scalar, tt=tt, ti=ti):
            c.wait(scalar, "a1t_dt", ti + 1)
            if ti >= 1:
                c.wait(scalar, "a1v_dt", ti)
            c.seq(scalar, scalar.activation(out=dts1[:, :], in_=ps_K[:, :],
                                            func=AF.Abs, bias=dtb[:, 0:1]), "S")
            c.seq(scalar, scalar.activation(out=dts1[:, :], in_=dts1[:, :],
                                            func=AF.Exp, scale=-1.0), "S")
            h = scalar.activation(out=dts1[:, :], in_=dts1[:, :],
                                  func=AF.Ln, bias=1.0)
            c.inc("a1s_dt", h, 1)

        @block.vector
        def _(vector, tt=tt, ti=ti, M=M):
            c.wait(vector, "a1s_dt", ti + 1)
            if M == "A" and tt >= 1:
                c.wait(vector, "a1t_pr", 16 * tt)
            if M == "B" and tt >= 1:
                c.wait(vector, "a1t_prB", 16 * tt)
            c.seq(vector, vector.tensor_scalar(
                out=dts2[:, :], in0=ps_K[:, :], scalar1=dtb[:, 0:1],
                scalar2=None, op0=OP.add), "V")
            if M == "A":
                h = vector.scalar_tensor_tensor(
                    out=dtt[:, :], in0=dts2[:, :], scalar=0.0, in1=dts1[:, :],
                    op0=OP.max, op1=OP.add,
                    accum_out=dsacc[:, tt:tt + 1])
            else:
                c.seq(vector, vector.scalar_tensor_tensor(
                    out=dtt[:, :], in0=dts2[:, :], scalar=0.0, in1=dts1[:, :],
                    op0=OP.max, op1=OP.add), "V")
                c.seq(vector, vector.tensor_tensor_scan(
                    out=cum32[:, :], data0=ones512[0:32, :], data1=dtt[:, :],
                    initial=cdel32[:, 0:1], op0=OP.mult, op1=OP.add), "V")
                h = vector.tensor_copy(out=cdel32[:, 0:1],
                                       in_=cum32[:, TT5 - 1:TT5])
            nn = c.inc("a1v_dt", h, 1)
            c.wait(vector, "a1v_dt", nn)

    for tt in range(NTT):
        tb = 3 + tt * TT5
        dt_block(tt, "A")

        for dc in range(16):
            k = 16 * tt + dc

            @block.gpsimd
            def _(gpsimd, tt=tt, dc=dc, k=k):
                if k >= 1:
                    c.wait(gpsimd, "a1t_pr", k)
                if k >= 2:
                    c.wait(gpsimd, "a1t_Az", k - 1)
                c.inc(f"a1g_x{dc % 2}", gpsimd.dma_start(
                    out=winb[dc % 2][:].rearrange("p (k f) -> p k f", k=8),
                    in_=wt_kpf(WIN)[:, :, dc * 128:(dc + 1) * 128]), 16)
                for q in range(2):
                    n = 2 * dc + q
                    c.inc(f"a1g_b{dc % 2}", gpsimd.dma_start(
                        out=bcw[dc % 2][:].rearrange("p (k f) -> p k f", k=8)
                        [:, :, q * 128:(q + 1) * 128],
                        in_=wt_kpf(WIN)[:, :, 4096 + n * 129:4096 + n * 129 + 128]), 16)

            @block.tensor
            def _(tensor, tt=tt, dc=dc, k=k):
                c.wait(tensor, f"a1g_x{dc % 2}", 16 * (k // 2 + 1))
                c.wait(tensor, f"a1g_b{dc % 2}", 32 * (k // 2 + 1))
                c.wait(tensor, "a1v_dt", tt + 1)
                if k >= 1:
                    c.wait(tensor, "a1v_pr", k)
                if k >= 1:
                    c.wait(tensor, "a1v_scan", 32 * k)
                if tt >= 1:
                    c.wait(tensor, "a1v_Ao2", 8 * tt)
                for kc in range(8):
                    tensor.matmul(ps_a[:, 0:TT5],
                                  winb[dc % 2][:, kc * 128:(kc + 1) * 128],
                                  xnv[:, kc, tb:tb + TT5],
                                  start=(kc == 0), stop=(kc == 7))
                if tt == 0:
                    for kc in range(8):
                        tensor.matmul(ps_dr[:, 0:3],
                                      winb[dc % 2][:, kc * 128:(kc + 1) * 128],
                                      xnv[:, kc, 0:3],
                                      start=(kc == 0), stop=(kc == 7))
                for q in range(2):
                    for kc in range(8):
                        tensor.matmul(ps_bc[:, q * TT5:(q + 1) * TT5],
                                      bcw[dc % 2][:, kc * 256 + q * 128:
                                                  kc * 256 + (q + 1) * 128],
                                      xnv[:, kc, tb:tb + TT5],
                                      start=(kc == 0), stop=(kc == 7))
                h = tensor.matmul(ps_c[:, 0:TT5],
                                  sel32[:, dc * 128:(dc + 1) * 128],
                                  dtt[:, :], start=True, stop=True)
                c.inc("a1t_pr", h, 1)

            @block.vector
            def _(vector, tt=tt, dc=dc, k=k):
                c.wait(vector, "a1t_pr", k + 1)
                if k >= 1:
                    c.wait(vector, "a1t_bc", 32 * k)
                    c.wait(vector, "a1s_da", 32 * k)
                if tt == 0:
                    c.seq(vector, vector.tensor_copy(out=xh[:, 0:3],
                                                     in_=ps_dr[:, 0:3]), "V")
                else:
                    c.seq(vector, vector.tensor_copy(
                        out=xh[:, 0:3],
                        in_=ctx3[:].rearrange("p (k t) -> p k t", k=16)[:, dc, :]), "V")
                c.seq(vector, vector.tensor_copy(out=xh[:, 3:3 + TT5],
                                                 in_=ps_a[:, 0:TT5]), "V")
                c.seq(vector, vector.tensor_copy(
                    out=ctx3[:].rearrange("p (k t) -> p k t", k=16)[:, dc, :],
                    in_=xh[:, TT5:TT5 + 3]), "V")
                for q in range(2):
                    c.seq(vector, vector.tensor_copy(
                        out=Bt[q * 64:(q + 1) * 64, :],
                        in_=ps_bc[0:64, q * TT5:(q + 1) * TT5]), "V")
                    c.seq(vector, vector.tensor_copy(
                        out=Ct[q * 64:(q + 1) * 64, :],
                        in_=ps_bc[64:128, q * TT5:(q + 1) * TT5]), "V")
                h = vector.tensor_copy(out=drep[:, :], in_=ps_c[:, 0:TT5])
                nn = c.inc("a1v_pr", h, 1)
                c.wait(vector, "a1v_pr", nn)

            @block.vector
            def _(vector, tt=tt, dc=dc, k=k):
                if k >= 1:
                    c.wait(vector, "a1s_cv", k)
                c.seq(vector, vector.tensor_scalar(
                    out=sch1[:, :], in0=xh[:, 0:TT5],
                    scalar1=convw[:, dc * 4:dc * 4 + 1], scalar2=None,
                    op0=OP.mult), "V")
                for j in range(1, 4):
                    h = vector.scalar_tensor_tensor(
                        out=sch1[:, :], in0=xh[:, j:j + TT5],
                        scalar=convw[:, dc * 4 + j:dc * 4 + j + 1],
                        in1=sch1[:, :], op0=OP.mult, op1=OP.add)
                    if j < 3:
                        c.seq(vector, h, "V")
                c.inc("a1v_cv", h, 1)

            @block.scalar
            def _(scalar, tt=tt, dc=dc, k=k):
                c.wait(scalar, "a1v_cv", k + 1)
                if k >= 1:
                    c.wait(scalar, "a1v_cv2", k)
                h = scalar.activation(out=vpb[:, :], in_=sch1[:, :],
                                      func=AF.Exp, scale=-1.0,
                                      bias=convb[:, dc:dc + 1])
                c.inc("a1s_cv", h, 1)

            @block.vector
            def _(vector, tt=tt, dc=dc, k=k):
                c.wait(vector, "a1s_cv", k + 1)
                c.seq(vector, vector.tensor_scalar(
                    out=vpb[:, :], in0=vpb[:, :], scalar1=1.0, scalar2=None,
                    op0=OP.add), "V")
                c.seq(vector, vector.reciprocal(out=vpb[:, :], in_=vpb[:, :]), "V")
                c.seq(vector, vector.tensor_scalar(
                    out=gb[:, 0:TT5], in0=sch1[:, :],
                    scalar1=convb[:, dc:dc + 1], scalar2=None, op0=OP.subtract), "V")
                c.seq(vector, vector.tensor_mul(out=xh[:, 3:3 + TT5],
                                                in0=gb[:, 0:TT5], in1=vpb[:, :]), "V")
                h = vector.tensor_mul(out=wdc[:, :], in0=drep[:, :],
                                      in1=xh[:, 3:3 + TT5])
                nn = c.inc("a1v_cv2", h, 1)
                c.wait(vector, "a1v_cv2", nn)

            for sp in range(32):
                sidx = (16 * tt + dc) * 32 + sp
                par = sp % 2

                @block.tensor
                def _(tensor, tt=tt, dc=dc, sp=sp, sidx=sidx):
                    c.wait(tensor, "a1v_cv2", 16 * tt + dc + 1)
                    if sidx >= 1:
                        c.wait(tensor, "a1v_scan", sidx)
                    if tt >= 1:
                        c.wait(tensor, "a1v_Ao2", 8 * tt)
                    for half in range(2):
                        s = 2 * sp + half
                        tensor.matmul(ps_bc[:, half * TT5:(half + 1) * TT5],
                                      sel128[:, s * 128:(s + 1) * 128],
                                      Bt[:, :], start=True, stop=True)
                        h = tensor.matmul(ps_c[:, half * TT5:(half + 1) * TT5],
                                          sel128[:, s * 128:(s + 1) * 128],
                                          Ct[:, :], start=True, stop=True)
                    c.inc("a1t_bc", h, 1)

                @block.vector
                def _(vector, tt=tt, dc=dc, sp=sp, sidx=sidx, par=par):
                    c.wait(vector, "a1t_bc", sidx + 1)
                    ubv = ub[par][:].rearrange("p (a b) -> p a b", a=2)
                    c.seq(vector, vector.tensor_tensor(
                        out=ubv[:, :, 1:1 + TT5],
                        in0=wdc[:].unsqueeze(1).broadcast_to((128, 2, TT5)),
                        in1=ps_bc[:].rearrange("p (a b) -> p a b", a=2),
                        op=OP.mult), "V")
                    c.seq(vector, vector.tensor_copy(
                        out=ubv[:, :, 0:1],
                        in_=carry[:, dc * S1L + 2 * sp:dc * S1L + 2 * sp + 2]
                        .rearrange("p (a b) -> p a b", a=2)), "V")
                    c.wait(vector, "a1s_da", sidx + 1)
                    h = vector.tensor_tensor_scan(
                        out=hb[par][:, :], data0=dab[par][:, :], data1=ub[par][:, :],
                        initial=0.0, op0=OP.mult, op1=OP.add)
                    c.seq(vector, h, "V")
                    hbv = hb[par][:].rearrange("p (a b) -> p a b", a=2)
                    c.seq(vector, vector.tensor_copy(
                        out=carry[:, dc * S1L + 2 * sp:dc * S1L + 2 * sp + 2]
                        .rearrange("p (a b) -> p a b", a=2),
                        in_=hbv[:, :, TT5:TT5 + 1]), "V")
                    c.seq(vector, vector.tensor_tensor(
                        out=gb[:].rearrange("p (a b) -> p a b", a=2),
                        in0=hbv[:, :, 1:1 + TT5],
                        in1=ps_c[:].rearrange("p (a b) -> p a b", a=2),
                        op=OP.mult), "V")
                    if sp == 0:
                        h = vector.tensor_add(out=ydc[:, :], in0=gb[:, 0:TT5],
                                              in1=gb[:, TT5:2 * TT5])
                    else:
                        c.seq(vector, vector.tensor_add(
                            out=ydc[:, :], in0=ydc[:, :], in1=gb[:, 0:TT5]), "V")
                        h = vector.tensor_add(out=ydc[:, :], in0=ydc[:, :],
                                              in1=gb[:, TT5:2 * TT5])
                    nn = c.inc("a1v_scan", h, 1)
                    c.wait(vector, "a1v_scan", nn)

                @block.scalar
                def _(scalar, tt=tt, dc=dc, sp=sp, sidx=sidx, par=par):
                    c.wait(scalar, "a1v_cv2", 16 * tt + dc + 1)
                    if sidx >= 2:
                        c.wait(scalar, "a1v_scan", sidx - 1)
                    dav = dab[par][:].rearrange("p (a b) -> p a b", a=2)
                    c.seq(scalar, scalar.activation(
                        out=dav[:, 0, 1:1 + TT5], in_=drep[:, :], func=AF.Exp,
                        scale=Arep[:, dc * S1L + 2 * sp:dc * S1L + 2 * sp + 1]), "S")
                    h = scalar.activation(
                        out=dav[:, 1, 1:1 + TT5], in_=drep[:, :], func=AF.Exp,
                        scale=Arep[:, dc * S1L + 2 * sp + 1:dc * S1L + 2 * sp + 2])
                    c.inc("a1s_da", h, 1)

            @block.vector
            def _(vector, tt=tt, dc=dc, k=k):
                c.wait(vector, "a1v_scan", 32 * (k + 1))
                h = vector.scalar_tensor_tensor(
                    out=ydc[:, :], in0=xh[:, 3:3 + TT5],
                    scalar=Drep[:, dc:dc + 1], in1=ydc[:, :],
                    op0=OP.mult, op1=OP.add)
                nn = c.inc("a1v_yD", h, 1)
                c.wait(vector, "a1v_yD", nn)

            tail_z_y2(tt, dc, "A")

        tail_wout_x(tt, "A")

    barrier(c, block, sems=["a1v_Az", "a1v_Ao2", "a1t_Ao", "a1s_Az", "a1v_scan",
                            "a1s_da", "a1t_bc", "a1s_cv", "a1v_cv",
                            "a1v_cv2", "a1t_pr", "a1v_pr", "a1v_yD", "a1v_dt",
                            "a1s_dt", "a1t_dt"])
    stPA.close()
    Hbuf = c.sb("a1_H", [128, 16 * S1L], F32, stP)
    cum32 = c.sb("a1_cum32", [32, TT5], F32, stP)

    stE = ExitStack()
    Pacc = c.sb("a1_Pacc", [128, 16 * S1L], F32, stE)
    Prun = c.sb("a1_Prun", [128, 16 * S1L], F32, stE)
    tcmb = c.sb("a1_tcmb", [128, 16 * S1L], F32, stE)
    agj = c.sb("a1_agj", [128, 2 * 16 * S1L], F32, stE)
    dsr = c.sb("a1_dsr", [128, 16], F32, stE)

    @block.vector
    def _(vector):
        h = vector.tensor_copy(out=dsum32[:, :], in_=dsacc[:, 0:1])
        for _tt in range(1, NTT):
            c.seq(vector, h, "V")
            h = vector.tensor_add(out=dsum32[:, :], in0=dsum32[:, :],
                                  in1=dsacc[:, _tt:_tt + 1])
        nn = c.inc("a1v_ds", h, 1)
        c.wait(vector, "a1v_ds", nn)

    for dc in range(16):
        @block.tensor
        def _(tensor, dc=dc):
            c.wait(tensor, "a1v_ds", 1)
            if dc >= 1:
                c.wait(tensor, "a1v_dsr", dc)
            c.inc("a1t_dsr", tensor.matmul(
                ps_dr[:, 0:1], sel32[:, dc * 128:(dc + 1) * 128],
                dsum32[:, :], start=True, stop=True), 1)

        @block.vector
        def _(vector, dc=dc):
            c.wait(vector, "a1t_dsr", dc + 1)
            h = vector.tensor_copy(out=dsr[:, dc:dc + 1], in_=ps_dr[:, 0:1])
            nn = c.inc("a1v_dsr", h, 1)
            c.wait(vector, "a1v_dsr", nn)

    @block.vector
    def _(vector):
        c.wait(vector, "a1v_dsr", 16)
        for dc in range(16):
            h = vector.tensor_scalar(
                out=Pacc[:, dc * S1L:(dc + 1) * S1L],
                in0=Arep[:, dc * S1L:(dc + 1) * S1L],
                scalar1=dsr[:, dc:dc + 1], scalar2=None, op0=OP.mult)
        c.inc("a1v_dec1", h, 1)

    @block.scalar
    def _(scalar):
        c.wait(scalar, "a1v_dec1", 1)
        c.inc("a1s_dec", scalar.activation(out=Pacc[:], in_=Pacc[:], func=AF.Exp), 1)

    @block.sync
    def _(sync):
        c.wait(sync, "a1s_dec", 1)
        c.inc("a1q_cc", sync.dma_start(out=cc1_in[:, 0:1024], in_=Pacc[:]), 16)
        c.inc("a1q_cc", sync.dma_start(out=cc1_in[:, 1024:2048], in_=carry[:]), 16)

    @block.gpsimd
    def _(gpsimd):
        c.wait(gpsimd, "a1q_cc", 32)
        c.inc("a1g_cc", gpsimd.collective_compute(
            "AllGather", OP.bypass, ins=[cc1_in], outs=[cc1_out],
            replica_groups=[[0, 1, 2, 3], [4, 5, 6, 7]]), 1)

    @block.gpsimd
    def _(gpsimd):
        c.wait(gpsimd, "a1g_cc", 1)
        gpsimd.memset(Hbuf[:], 0.0)
        c.inc("a1g_hm", gpsimd.memset(Prun[:], 1.0), 1)

    for j in [2, 1, 0]:
        @block.sync
        def _(sync, j=j):
            c.wait(sync, "a1g_cc", 1)
            if j < 2:
                c.wait(sync, "a1v_cmb", 2 - j)
            c.inc("a1q_ag", sync.dma_start(
                out=agj[:], in_=cc1_out[j * 128:(j + 1) * 128, :]), 16)

        @block.vector
        def _(vector, j=j):
            c.wait(vector, "a1q_ag", 16 * (3 - j))
            c.wait(vector, "a1g_hm", 1)
            m = chm[:, j:j + 1]
            c.seq(vector, vector.tensor_mul(out=tcmb[:, :], in0=agj[:, 1024:2048],
                                            in1=Prun[:, :]), "V")
            c.seq(vector, vector.tensor_scalar(out=tcmb[:, :], in0=tcmb[:, :],
                                               scalar1=m, scalar2=None,
                                               op0=OP.mult), "V")
            c.seq(vector, vector.tensor_add(out=Hbuf[:, :], in0=Hbuf[:, :],
                                            in1=tcmb[:, :]), "V")
            c.seq(vector, vector.tensor_scalar(out=tcmb[:, :], in0=agj[:, 0:1024],
                                               scalar1=1.0, scalar2=m,
                                               op0=OP.subtract, op1=OP.mult), "V")
            c.seq(vector, vector.tensor_scalar(out=tcmb[:, :], in0=tcmb[:, :],
                                               scalar1=1.0, scalar2=None,
                                               op0=OP.add), "V")
            h = vector.tensor_mul(out=Prun[:, :], in0=Prun[:, :], in1=tcmb[:, :])
            nn = c.inc("a1v_cmb", h, 1)
            c.wait(vector, "a1v_cmb", nn)

    barrier(c, block, sems=["a1v_cmb", "a1v_ds", "a1t_dsr", "a1v_dsr",
                            "a1v_dec1", "a1s_dec", "a1q_cc", "a1g_cc", "a1g_hm",
                            "a1q_ag"])
    stE.close()

    stPB = ExitStack()
    bcwB = [c.sb("a1_bcwB", [128, 8 * 256], BF16, stPB)] * 2
    CtB = c.sb("a1_CtB", [128, TT5], BF16, stPB)
    cumrep = c.sb("a1_cumrep", [128, TT5], F32, stPB)
    schB = [c.sb(f"a1_schB{i}", [128, TT5], F32, stPB) for i in range(2)]
    ycor = c.sb("a1_ycor", [128, TT5], F32, stPB)
    gbB = c.sb("a1_gbB", [128, 2 * TT5], BF16, stPB)
    zsB = c.sb("a1_zsB", [128, TT5], BF16, stPB)
    zexB = c.sb("a1_zexB", [128, TT5], F32, stPB)

    gb = gbB
    zs = zsB
    zex = zexB

    for tt in range(NTT):
        tb = 3 + tt * TT5
        dt_block(tt, "B")

        for dc in range(16):
            k = 16 * tt + dc

            @block.gpsimd
            def _(gpsimd, tt=tt, dc=dc, k=k):
                if k >= 1:
                    c.wait(gpsimd, "a1t_prB", k)
                for q in range(2):
                    n = 2 * dc + q
                    c.inc(f"a1g_bB{dc % 2}", gpsimd.dma_start(
                        out=bcwB[dc % 2][:].rearrange("p (k f) -> p k f", k=8)
                        [:, :, q * 128:(q + 1) * 128],
                        in_=wt_kpf(WIN)[:, :, 4096 + n * 129:4096 + n * 129 + 128]), 16)

            @block.tensor
            def _(tensor, tt=tt, dc=dc, k=k):
                c.wait(tensor, f"a1g_bB{dc % 2}", 32 * (k // 2 + 1))
                c.wait(tensor, "a1v_dt", NTT + tt + 1)
                if k >= 1:
                    c.wait(tensor, "a1v_prB", k)
                for q in range(2):
                    for kc in range(8):
                        tensor.matmul(ps_bc[:, q * TT5:(q + 1) * TT5],
                                      bcwB[dc % 2][:, kc * 256 + q * 128:
                                                   kc * 256 + (q + 1) * 128],
                                      xnv[:, kc, tb:tb + TT5],
                                      start=(kc == 0), stop=(kc == 7))
                h = tensor.matmul(ps_dr[:, 0:TT5],
                                  sel32[:, dc * 128:(dc + 1) * 128],
                                  cum32[:, :], start=True, stop=True)
                c.inc("a1t_prB", h, 1)

            @block.vector
            def _(vector, tt=tt, dc=dc, k=k):
                c.wait(vector, "a1t_prB", k + 1)
                if k >= 1:
                    c.wait(vector, "a1t_cb", S1L * k)
                    c.wait(vector, "a1s_pb", S1L * k)
                for q in range(2):
                    c.seq(vector, vector.tensor_copy(
                        out=CtB[q * 64:(q + 1) * 64, :],
                        in_=ps_bc[64:128, q * TT5:(q + 1) * TT5]), "V")
                h = vector.tensor_copy(out=cumrep[:, :], in_=ps_dr[:, 0:TT5])
                nn = c.inc("a1v_prB", h, 1)
                c.wait(vector, "a1v_prB", nn)

            for s in range(S1L):
                pidx = (16 * tt + dc) * S1L + s
                par = s % 2

                @block.tensor
                def _(tensor, tt=tt, dc=dc, s=s, pidx=pidx, par=par):
                    c.wait(tensor, "a1v_prB", k + 1)
                    if pidx >= 2:
                        c.wait(tensor, "a1v_pb", pidx - 1)
                    if tt >= 1:
                        c.wait(tensor, "a1v_Bo2", 8 * tt)
                    h = tensor.matmul(ps_c[:, par * TT5:(par + 1) * TT5],
                                      sel128[:, s * 128:(s + 1) * 128],
                                      CtB[:, :], start=True, stop=True)
                    c.inc("a1t_cb", h, 1)

                @block.scalar
                def _(scalar, tt=tt, dc=dc, s=s, pidx=pidx, par=par):
                    c.wait(scalar, "a1v_prB", k + 1)
                    if pidx >= 2:
                        c.wait(scalar, "a1v_pb", pidx - 1)
                    h = scalar.activation(
                        out=schB[par][:, :], in_=cumrep[:, :], func=AF.Exp,
                        scale=Arep[:, dc * S1L + s:dc * S1L + s + 1])
                    c.inc("a1s_pb", h, 1)

                @block.vector
                def _(vector, tt=tt, dc=dc, s=s, pidx=pidx, par=par):
                    c.wait(vector, "a1s_pb", pidx + 1)
                    c.wait(vector, "a1t_cb", pidx + 1)
                    c.seq(vector, vector.tensor_tensor(
                        out=schB[par][:, :], in0=schB[par][:, :],
                        in1=ps_c[:, par * TT5:(par + 1) * TT5], op=OP.mult), "V")
                    if s == 0:
                        h = vector.tensor_scalar(
                            out=ycor[:, :], in0=schB[par][:, :],
                            scalar1=Hbuf[:, dc * S1L + s:dc * S1L + s + 1],
                            scalar2=None, op0=OP.mult)
                    else:
                        h = vector.scalar_tensor_tensor(
                            out=ycor[:, :], in0=schB[par][:, :],
                            scalar=Hbuf[:, dc * S1L + s:dc * S1L + s + 1],
                            in1=ycor[:, :], op0=OP.mult, op1=OP.add)
                    nn = c.inc("a1v_pb", h, 1)
                    c.wait(vector, "a1v_pb", nn)

            @block.vector
            def _(vector, tt=tt, dc=dc, k=k):
                c.wait(vector, "a1v_pb", S1L * (k + 1))
                h = vector.tensor_copy(out=gb[:, 0:TT5], in_=ycor[:, :])
                nn = c.inc("a1v_yc", h, 1)
                c.wait(vector, "a1v_yc", nn)

            tail_z_y2(tt, dc, "B")

        tail_wout_x(tt, "B")

    barrier(c, block, sems=["a1v_Bz", "a1v_Bo2", "a1t_Bo", "a1s_Bz", "a1v_pb",
                            "a1s_pb", "a1t_cb", "a1v_prB", "a1t_prB", "a1v_yc",
                            "a1v_dt", "a1s_dt", "a1t_dt"],
            release=["a1"])
    stPB.close()
    stP.close()


def build():
    nc = bass.Bass("TRN2", target_bir_lowering=False)

    def din(name, shape, dt=F32):
        return nc.dram_tensor(name, list(shape), dt, kind="ExternalInput").ap()

    xin = din("xin", (TE, D))
    ident_in = din("ident", (128, 128))
    sel8_in = din("sel8", (8, 8 * 128))
    sel33_in = din("sel33", (33, 33 * 128))
    sel32_in = din("sel32", (32, 16 * 128))
    sel128_in = din("sel128", (128, 64 * 128))
    prevsel_in = din("prevsel", (128, 4))
    w = {k: din(k, shp) for k, shp in WEIGHT_SHAPES.items()}
    dram_chm = din("chmask", (128, 4))
    cc0_in = nc.dram_tensor("cc0_in", [128, 512], F32, kind="Internal").ap()
    cc0_out = nc.dram_tensor("cc0_out", [512, 512], F32, kind="Internal").ap()
    ccx_in = nc.dram_tensor("ccx_in", [128, 24], F32, kind="Internal").ap()
    ccx_out = nc.dram_tensor("ccx_out", [512, 24], F32, kind="Internal").ap()
    cc1_in = nc.dram_tensor("cc1_in", [128, 2048], F32, kind="Internal").ap()
    cc1_out = nc.dram_tensor("cc1_out", [512, 2048], F32, kind="Internal").ap()
    out = nc.dram_tensor("out", [TC, D], F32, kind="ExternalOutput").ap()
    dbg_out = nc.dram_tensor("dbg", [128, 1 << 15], F32, kind="ExternalOutput").ap() if DBG else None

    es = ExitStack()
    with es:
        c = Ctx(nc, es)
        for _nm in ["bT", "bV", "bS", "bG", "bQ", "bC", "ord_V", "ord_S"]:
            c.semrec(_nm)
        x = c.sb("x", [128, 8 * TE], BF16)
        xn = c.sb("xn", [128, 8 * TE], BF16)
        ident = c.sb("identt", [128, 128], BF16)
        sel8 = c.sb("sel8t", [8, 8 * 128], BF16)
        sel33 = c.sb("sel33t", [33, 33 * 128], BF16)
        chmt = c.sb("chmt", [128, 4], F32)
        prevselt = c.sb("prevselt", [128, 4], F32)
        identf = c.sb("identtf", [128, 128], F32)
        ones_rowf = c.sb("ones_rowf", [1, 128], F32)
        ones128 = c.sb("ones128", [128, 1], BF16)
        ones_row = c.sb("ones_row", [1, 128], BF16)

        block = es.enter_context(nc.Block())

        c.make_consts(CONST_VALUES)

        @block.gpsimd
        def _(gpsimd):
            gpsimd.memset(ones128[:], 1.0)
            gpsimd.memset(ones_row[:], 1.0)
            for v, t in c.consts.items():
                gpsimd.memset(t[:], float(v))
            c.inc("g_const", gpsimd.dma_start(out=sel8[:], in_=sel8_in), 16)
            c.inc("g_const", gpsimd.dma_start(out=sel33[:], in_=sel33_in), 16)
            c.inc("g_const", gpsimd.dma_start(out=chmt[:], in_=dram_chm), 16)
            c.inc("g_const", gpsimd.dma_start(out=prevselt[:], in_=prevsel_in), 16)
            c.inc("g_const", gpsimd.dma_start(out=identf[:], in_=ident_in), 16)
            gpsimd.memset(ones_rowf[:], 1.0)
            c.inc("g_const", gpsimd.dma_start(out=ident[:], in_=ident_in), 16)
        const_target = c.val("g_const")

        ph_load(c, block, xin, x, ident, const_target)
        barrier(c, block, sems=["ld_ev", "ld_tr"], release=["ld_"])
        if STAGE == 0:
            ph_store(c, block, x, out, ident)
            return nc
        ph_layernorm(c, block, x, xn, TE, ones128, ones_row, "ln1")

        if STAGE == 1:
            ph_store(c, block, xn, out, ident)
            return nc

        g = c.sb("gbuf", [128, 8 * TC], BF16)
        ph_ffn0(c, block, xn, g, w, ones128, ones_row, ident, sel8)
        if STAGE == 2:
            ph_store(c, block, g, out, ident, toff=0, tstride=TC)
            return nc

        ph_attn0(c, block, x, xn, g, w, ones128, ones_row, ident, sel33, chmt[:],
                 (cc0_in, cc0_out))
        if STAGE == 3:
            ph_store(c, block, x, out, ident)
            return nc

        ph_edge_exchange(c, block, x, chmt[:], prevselt[:], (ccx_in, ccx_out))
        ph_layernorm(c, block, x, xn, TE, ones128, ones_row, "ln2")
        ph_ffn1(c, block, xn, g, w, ones128, ones_row, ones_rowf, ident, identf)
        if STAGE == 4:
            ph_store(c, block, g, out, ident, toff=0, tstride=TC)
            return nc

        ph_attn1(c, block, x, xn, g, w, ones128, ones_row, ident, sel32_in,
                 sel128_in, chmt[:], (cc1_in, cc1_out))
        ph_store(c, block, x, out, ident)
    return nc


WEIGHT_SHAPES = {
    "L0_W_in": (D, 2 * DI),
    "L0_conv_w": (DI, 4),
    "L0_conv_b": (DI,),
    "L0_W_xproj": (DI, 33),
    "L0_W_dt": (DI,),
    "L0_b_dt": (DI,),
    "L0_A_log": (DI, S0),
    "L0_Dp": (DI,),
    "L0_W_out": (DI, D),
    "F0_router_w": (D, 8),
    "F0_w1": (8, D, D),
    "F0_w2": (8, D, D),
    "F0_sgu": (D, 2 * D),
    "F0_sdown": (D, D),
    "F0_sgw": (D,),
    "L1_W_inT": (D, DPROJ),
    "L1_conv_w": (DI, 4),
    "L1_conv_b": (DI,),
    "L1_A_log": (NH, S1),
    "L1_D": (NH,),
    "L1_dt_bias": (NH,),
    "L1_W_outT": (DI, D),
    "Q_kernel": (D, D),
    "Q_bias": (D,),
    "Q_a": (D, 16),
    "Q_b": (16, D),
}


_BUILT = None


def make_in_maps(inputs):
    xs = np.ascontiguousarray(inputs["L0_attn_x"], dtype=np.float32)
    flat = xs.reshape(B * T, D)
    wmap = {
        "L0_W_in": inputs["L0_attn_W_in"],
        "L0_conv_w": inputs["L0_attn_conv_w"][:, 0, :],
        "L0_conv_b": inputs["L0_attn_conv_b"],
        "L0_W_xproj": inputs["L0_attn_W_xproj"],
        "L0_W_dt": inputs["L0_attn_W_dt"][0],
        "L0_b_dt": inputs["L0_attn_b_dt"],
        "L0_A_log": inputs["L0_attn_A_log"],
        "L0_Dp": inputs["L0_attn_Dp"],
        "L0_W_out": inputs["L0_attn_W_out"],
        "F0_router_w": inputs["L0_ffn_router_w"],
        "F0_w1": inputs["L0_ffn_w1"],
        "F0_w2": inputs["L0_ffn_w2"],
        "F0_sgu": inputs["L0_ffn_shared_gate_up"],
        "F0_sdown": inputs["L0_ffn_shared_down"],
        "F0_sgw": inputs["L0_ffn_shared_gate_w"][:, 0],
        "L1_W_inT": inputs["L1_attn_W_in"].T,
        "L1_conv_w": inputs["L1_attn_conv_w"][:, 0, :],
        "L1_conv_b": inputs["L1_attn_conv_b"],
        "L1_A_log": inputs["L1_attn_A_log"],
        "L1_D": inputs["L1_attn_D"],
        "L1_dt_bias": inputs["L1_attn_dt_bias"],
        "L1_W_outT": inputs["L1_attn_W_out"].T,
        "Q_kernel": inputs["L1_ffn_kernel"],
        "Q_bias": inputs["L1_ffn_bias"],
        "Q_a": inputs["L1_ffn_a"],
        "Q_b": inputs["L1_ffn_b"],
    }
    wmap = {k: np.ascontiguousarray(v, dtype=np.float32) for k, v in wmap.items()}
    ident = np.eye(128, dtype=np.float32)
    sel8 = np.zeros((8, 8 * 128), np.float32)
    for e in range(8):
        sel8[e, e * 128:(e + 1) * 128] = 1.0
    sel33 = np.zeros((33, 33 * 128), np.float32)
    for r in range(33):
        sel33[r, r * 128:(r + 1) * 128] = 1.0
    sel32 = np.zeros((32, 16 * 128), np.float32)
    for dcq in range(16):
        for p in range(128):
            sel32[2 * dcq + p // 64, dcq * 128 + p] = 1.0
    sel128 = np.zeros((128, 64 * 128), np.float32)
    for s in range(64):
        for p in range(128):
            sel128[(p // 64) * 64 + s, s * 128 + p] = 1.0

    in_maps = []
    for core in range(NCORES):
        t0 = core * TC
        pre = np.zeros((3, D), np.float32) if core % 4 == 0 else flat[t0 - 3:t0]
        xin_c = np.ascontiguousarray(np.concatenate([pre, flat[t0:t0 + TC]], axis=0))
        m = np.zeros((128, 4), np.float32)
        m[:, :core % 4] = 1.0
        pv = np.zeros((128, 4), np.float32)
        if core % 4 > 0:
            pv[:, core % 4 - 1] = 1.0
        im = dict(wmap)
        im["xin"] = xin_c
        im["ident"] = ident
        im["sel8"] = sel8
        im["sel33"] = sel33
        im["sel32"] = sel32
        im["sel128"] = sel128
        im["chmask"] = m
        im["prevsel"] = pv
        in_maps.append(im)
    return in_maps


def kernel(**inputs):
    global _BUILT
    in_maps = make_in_maps(inputs)
    if _BUILT is None:
        _BUILT = build()
    res = run_bass_kernel_spmd(_BUILT, in_maps, core_ids=list(range(NCORES)))
    parts = [res.results[i]["out"] for i in range(NCORES)]
    return np.concatenate(parts, axis=0).reshape(B, T, D).astype(np.float32)


# version 09
# speedup vs baseline: 1.2078x; 1.2078x over previous
import os
import numpy as np
from contextlib import ExitStack

import concourse.bass as bass
import concourse.mybir as mybir
from concourse.bass_utils import run_bass_kernel_spmd

F32 = mybir.dt.float32
BF16 = mybir.dt.bfloat16
AF = mybir.ActivationFunctionType
OP = mybir.AluOpType
AX = mybir.AxisListType

B, T, D = 2, 8192, 1024
NCORES = 8
TC = int(os.environ.get("KTC", (B * T) // NCORES))
TE = TC + 3
DI = 2048
S0 = 16
NH, S1, HD = 32, 64, 64
DPROJ = 2 * DI + NH * (2 * S1 + 1)
DBG = int(os.environ.get("KDBG", "0"))
STAGE = int(os.environ.get("KSTAGE", "99"))

NLEXP = "nlexp"
SILU = "silu"
CONST_VALUES = [1e-5, -0.5, 1.0, -1.0, 0.0]


class Ctx:
    def __init__(self, nc, es):
        self.nc = nc
        self.es = es
        self._sems = {}
        self.cur_table = "default"
        self.bar_n = 0

    def semrec(self, name):
        if name not in self._sems:
            self._semgen = getattr(self, "_semgen", 0) + 1
            h = self.es.enter_context(self.nc.semaphore(f"{name}__{self._semgen}"))
            self._sems[name] = [h, 0]
        return self._sems[name]

    def inc(self, name, instr, k=1):
        rec = self.semrec(name)
        instr.then_inc(rec[0], k)
        rec[1] += k
        return rec[1]

    def val(self, name):
        return self.semrec(name)[1]

    def wait(self, eng, name, v=None):
        rec = self.semrec(name)
        eng.wait_ge(rec[0], rec[1] if v is None else v)

    def seq(self, eng, instr, tag):
        name = f"ord_{tag}"
        n = self.inc(name, instr, 1)
        eng.wait_ge(self.semrec(name)[0], n)

    def sb(self, name, shape, dt=BF16, stack=None):
        return (stack or self.es).enter_context(self.nc.sbuf_tensor(name, shape, dt))

    def ps(self, name, shape, dt=F32, stack=None):
        return (stack or self.es).enter_context(self.nc.psum_tensor(name, shape, dt))

    def make_consts(self, values):
        self.consts = {}
        for v in values:
            t = self.sb(f"cst_{str(v).replace('-','m').replace('.','_')}", [128, 1], F32)
            self.consts[v] = t
            self.nc.const_aps.register(F32, v, t[:, :])

    def constf(self, v):
        return self.consts[v][:, :]

    def table(self, scalar, name):
        if self.cur_table != name:
            from concourse.hw_specs import get_activation_tables
            setname = {NLEXP: "natural_log_exp_and_others",
                       SILU: "silu_and_others"}[name]
            idx = list(get_activation_tables(self.nc.m.arch).keys()).index(setname)
            scalar.drain()
            scalar.add_instruction(mybir.InstLoadActFuncSet(
                name=self.nc.get_next_instruction_name(),
                act_func_set_id=idx, ins=[], outs=[]))
            scalar.drain()
            self.cur_table = name


def barrier(c, block, sems=(), release=()):
    c.bar_n += 1
    n = c.bar_n
    names = ["bT", "bV", "bS", "bG", "bQ"]
    extra = [(nm, c.val(nm)) for nm in sems if c.val(nm) > 0]

    def stage1(eng, myname, incer):
        for nm, v in extra:
            eng.wait_ge(c.semrec(nm)[0], v)
        c.inc(myname, incer(eng), 1)

    block.tensor(lambda e: stage1(e, "bT", lambda e: e.nop()))
    block.vector(lambda e: stage1(e, "bV", lambda e: e.engine_nop()))
    block.scalar(lambda e: stage1(e, "bS", lambda e: e.nop()))
    block.gpsimd(lambda e: stage1(e, "bG", lambda e: e.nop()))
    block.sync(lambda e: stage1(e, "bQ", lambda e: e.nop()))

    @block.gpsimd
    def _(gpsimd):
        for nm in names:
            gpsimd.wait_ge(c.semrec(nm)[0], n)
        todrop = [k for k in list(c._sems)
                  if any(k.startswith(p) for p in release)]
        if todrop:
            c.nc.clear_and_free_semaphores([c._sems[k][0] for k in todrop])
            for k in todrop:
                del c._sems[k]
        c.inc("bC", gpsimd.nop(), 1)

    for mk in [block.tensor, block.vector, block.scalar, block.sync]:
        def body(eng):
            eng.wait_ge(c.semrec("bC")[0], n)
        mk(body)


def wt_kpf(ap, p=128):
    return ap.rearrange("(k p) f -> k p f", p=p).transpose([1, 0, 2])


def ceil_div(a, b):
    return (a + b - 1) // b


def ph_load(c, block, xin, x, ident, consts_gd_target):
    nc = c.nc
    st = ExitStack()
    io_a = c.sb("io_a", [128, D], BF16, st)
    io_b = c.sb("io_b", [128, D], BF16, st)
    ps_tr = c.ps("ps_ld", [128, 8 * 128], BF16, st)

    ntg = ceil_div(TE, 128)
    for gi in range(ntg):
        t0 = gi * 128
        n = min(128, TE - t0)
        buf = io_a if gi % 2 == 0 else io_b

        dsem = f"ld_dma{gi % 2}"

        @block.gpsimd
        def _(gpsimd, t0=t0, n=n, buf=buf, gi=gi, dsem=dsem):
            if gi >= 2:
                c.wait(gpsimd, "ld_tr", gi - 1)
            c.inc(dsem, gpsimd.dma_start(out=buf[0:n, :], in_=xin[t0:t0 + n, :]), 16)

        @block.tensor
        def _(tensor, t0=t0, n=n, buf=buf, gi=gi, dsem=dsem):
            c.wait(tensor, dsem, 16 * (gi // 2 + 1))
            if gi == 0:
                c.wait(tensor, "g_const", consts_gd_target)
            if gi >= 1:
                c.wait(tensor, "ld_ev", gi)
            for dc in range(8):
                h = tensor.transpose(ps_tr[:, dc * 128:dc * 128 + n],
                                     buf[0:n, dc * 128:(dc + 1) * 128], ident[0:n, 0:n])
            c.inc("ld_tr", h, 1)

        @block.vector
        def _(vector, t0=t0, n=n, gi=gi):
            c.wait(vector, "ld_tr", gi + 1)
            c.inc("ld_ev", vector.tensor_copy(
                out=x[:].rearrange("p (d t) -> p d t", d=8)[:, :, t0:t0 + n],
                in_=ps_tr[:].rearrange("p (d t) -> p d t", d=8)[:, :, 0:n]), 1)

    st.close()


def ph_layernorm(c, block, src, dst, tokens, ones128, ones_row, tag):
    nc = c.nc
    st = ExitStack()
    sq = c.sb(f"ln_sq_{tag}", [128, 8 * 512], BF16, st)
    ps_sum = c.ps(f"ln_ps_s_{tag}", [1, 512], F32, st)
    ps_sq = c.ps(f"ln_ps_q_{tag}", [1, 512], F32, st)
    ps_bc = c.ps(f"ln_ps_bc_{tag}", [128, 512], F32, st)
    rs0 = c.sb(f"ln_rs0_{tag}", [1, 512], F32, st)
    rs1 = c.sb(f"ln_rs1_{tag}", [1, 512], F32, st)
    rowm = c.sb(f"ln_rowm_{tag}", [1, 512], BF16, st)
    rowr = c.sb(f"ln_rowr_{tag}", [1, 512], BF16, st)
    rf0 = c.sb(f"ln_rf0_{tag}", [1, 512], F32, st)
    rf1 = c.sb(f"ln_rf1_{tag}", [1, 512], F32, st)
    mbc = c.sb(f"ln_mbc_{tag}", [128, 512], BF16, st)
    rbc = c.sb(f"ln_rbc_{tag}", [128, 512], BF16, st)

    ngr = ceil_div(tokens, 512)
    base_sq = c.val("sS")

    for gi in range(ngr):
        t0 = gi * 512
        n = min(512, tokens - t0)
        sview = src[:].rearrange("p (d t) -> p d t", d=8)

        @block.scalar
        def _(scalar, t0=t0, n=n, gi=gi):
            if gi >= 1:
                c.wait(scalar, "lnT", gi)
            for dc in range(8):
                h = scalar.activation(out=sq[:].rearrange("p (d t) -> p d t", d=8)[:, dc, 0:n],
                                      in_=sview[:, dc, t0:t0 + n], func=AF.Square)
            c.inc("lnS", h, 1)

        @block.tensor
        def _(tensor, t0=t0, n=n, gi=gi):
            c.wait(tensor, "lnS", gi + 1)
            if gi >= 1:
                c.wait(tensor, "lnV", 2 * gi)
            sqv = sq[:].rearrange("p (d t) -> p d t", d=8)
            for dc in range(8):
                tensor.matmul(ps_sum[0:1, 0:n], ones128[:, 0:1],
                              sview[:, dc, t0:t0 + n], start=(dc == 0), stop=(dc == 7))
            for dc in range(8):
                h = tensor.matmul(ps_sq[0:1, 0:n], ones128[:, 0:1],
                                  sqv[:, dc, 0:n], start=(dc == 0), stop=(dc == 7))
            c.inc("lnT", h, 1)

        @block.vector
        def _(vector, t0=t0, n=n, gi=gi):
            c.wait(vector, "lnT", gi + 1)
            vector.tensor_copy(out=rs0[0:1, 0:n], in_=ps_sum[0:1, 0:n])
            c.seq(vector, vector.tensor_copy(out=rs1[0:1, 0:n], in_=ps_sq[0:1, 0:n]), "V")
            vector.tensor_scalar_mul(out=rf0[0:1, 0:n], in0=rs0[0:1, 0:n], scalar1=1.0 / D)
            c.seq(vector, vector.tensor_scalar_mul(out=rf1[0:1, 0:n], in0=rs1[0:1, 0:n],
                                                   scalar1=1.0 / D), "V")
            c.seq(vector, vector.tensor_mul(out=rs0[0:1, 0:n], in0=rf0[0:1, 0:n],
                                            in1=rf0[0:1, 0:n]), "V")
            vector.tensor_sub(out=rs1[0:1, 0:n], in0=rf1[0:1, 0:n], in1=rs0[0:1, 0:n])
            c.inc("lnVm", vector.tensor_copy(out=rowm[:, 0:n], in_=rf0[0:1, 0:n]), 1)

        @block.scalar
        def _(scalar, t0=t0, n=n, gi=gi):
            c.wait(scalar, "lnVm", gi + 1)
            if gi >= 1:
                c.wait(scalar, "lnT2", 2 * gi)
            c.table(scalar, NLEXP)
            if os.environ.get("KNORSTD"):
                c.inc("lnS2", scalar.activation(out=rowr[:, 0:n], in_=rs1[0:1, 0:n],
                                                func=AF.Copy, scale=0.0, bias=1.0), 1)
            else:
                c.seq(scalar, scalar.activation(out=rf1[0:1, 0:n], in_=rs1[0:1, 0:n],
                                                func=AF.Ln, bias=1e-5), "S")
                c.inc("lnS2", scalar.activation(out=rowr[:, 0:n], in_=rf1[0:1, 0:n],
                                                func=AF.Exp, scale=-0.5), 1)

        @block.tensor
        def _(tensor, t0=t0, n=n, gi=gi):
            c.wait(tensor, "lnS2", gi + 1)
            c.inc("lnT2", tensor.matmul(ps_bc[:, 0:n], ones_row[0:1, :],
                                        rowm[0:1, 0:n], start=True, stop=True), 1)

        @block.vector
        def _(vector, gi=gi, n=n):
            c.wait(vector, "lnT2", 2 * gi + 1)
            c.inc("lnV2", vector.tensor_copy(out=mbc[:, 0:n], in_=ps_bc[:, 0:n]), 1)

        @block.tensor
        def _(tensor, t0=t0, n=n, gi=gi):
            c.wait(tensor, "lnV2", 2 * gi + 1)
            c.inc("lnT2", tensor.matmul(ps_bc[:, 0:n], ones_row[0:1, :],
                                        rowr[0:1, 0:n], start=True, stop=True), 1)

        @block.vector
        def _(vector, t0=t0, n=n, gi=gi):
            c.wait(vector, "lnT2", 2 * gi + 2)
            h = vector.tensor_copy(out=rbc[:, 0:n], in_=ps_bc[:, 0:n])
            nn = c.inc("lnV2", h, 1)
            c.wait(vector, "lnV2", nn)
            dview = dst[:].rearrange("p (d t) -> p d t", d=8)
            for dc in range(8):
                h = vector.tensor_sub(out=dview[:, dc, t0:t0 + n],
                                      in0=sview[:, dc, t0:t0 + n], in1=mbc[:, 0:n])
            c.seq(vector, h, "V")
            for dc in range(8):
                h = vector.tensor_mul(out=dview[:, dc, t0:t0 + n],
                                      in0=dview[:, dc, t0:t0 + n], in1=rbc[:, 0:n])
            c.inc("lnV", h, 2)

    barrier(c, block, sems=["lnV", "lnV2", "lnVm", "lnS", "lnS2", "lnT", "lnT2"], release=["ln"])
    st.close()


def ph_store(c, block, x, out, ident, toff=3, tstride=TE):
    nc = c.nc
    st = ExitStack()
    ps_o = c.ps("ps_st", [128, 8 * 128], BF16, st)
    ot_a = c.sb("st_a", [128, D], F32, st)
    ot_b = c.sb("st_b", [128, D], F32, st)

    for gi in range(TC // 128):
        t0 = toff + gi * 128
        ot = ot_a if gi % 2 == 0 else ot_b

        @block.tensor
        def _(tensor, t0=t0, gi=gi):
            if gi >= 1:
                c.wait(tensor, "st_ev", gi)
            for dc in range(8):
                h = tensor.transpose(
                    ps_o[:, dc * 128:(dc + 1) * 128],
                    x[:].rearrange("p (d t) -> p d t", t=tstride)[:, dc, t0:t0 + 128],
                    ident[:])
            c.inc("st_tr", h, 1)

        dsem = f"st_dma{gi % 2}"

        @block.vector
        def _(vector, gi=gi, ot=ot, dsem=dsem):
            c.wait(vector, "st_tr", gi + 1)
            if gi >= 2:
                c.wait(vector, dsem, 16 * (gi // 2))
            c.inc("st_ev", vector.tensor_copy(out=ot[:, :], in_=ps_o[:, :]), 1)

        @block.sync
        def _(sync, gi=gi, ot=ot, dsem=dsem):
            c.wait(sync, "st_ev", gi + 1)
            c.inc(dsem, sync.dma_start(out=out[gi * 128:(gi + 1) * 128, :],
                                       in_=ot[:, :]), 16)

    @block.sync
    def _(sync):
        c.wait(sync, "st_dma0")
        c.wait(sync, "st_dma1")
    st.close()


def ph_debug_dump(c, block, dbg_out, srcs):
    st = ExitStack()
    tmp = c.sb("dbg_tmp", [128, 4096], F32, st)
    col = 0
    for ap in srcs:
        ncol = ap.shape[-1] if len(ap.shape) == 2 else int(np.prod(ap.shape[1:]))

        @block.vector
        def _(vector, ap=ap, col=col, ncol=ncol):
            c.wait(vector, "dbgq", c.val("dbgq"))
            c.inc("dbgv", vector.tensor_copy(out=tmp[0:ap.shape[0], 0:ncol], in_=ap), 1)

        @block.sync
        def _(sync, col=col, ncol=ncol, ap=ap):
            c.wait(sync, "dbgv")
            c.inc("dbgq", sync.dma_start(
                out=dbg_out[0:ap.shape[0], col:col + ncol],
                in_=tmp[0:ap.shape[0], 0:ncol]), 16)
        col += ncol

    @block.sync
    def _(sync):
        c.wait(sync, "dbgq")
    st.close()



def ph_ffn0(c, block, xn, g, wdram, ones128, ones_row, ident, sel8):
    nc = c.nc
    NTG = TC // 128
    NT5 = TC // 512
    xnv = xn[:].rearrange("p (d t) -> p d t", d=8)

    stP = ExitStack()
    combrows = c.sb("ffn_combrows", [8, TC], BF16, stP)
    sgrow = c.sb("ffn_sgrow", [1, TC], BF16, stP)
    sgbc = c.sb("ffn_sgbc", [128, TC], BF16, stP)

    stA = ExitStack()
    rw = c.sb("ffn_rw", [128, 8 * 8], BF16, stA)
    sgwt = c.sb("ffn_sgw", [128, 8], BF16, stA)
    comb = c.sb("ffn_comb", [128, NTG * 8], BF16, stA)
    prob = c.sb("ffn_prob", [128, 8], F32, stA)
    gcmp = c.sb("ffn_gcmp", [128, 64], F32, stA)
    cnt = c.sb("ffn_cnt", [128, 8], F32, stA)
    ncol = c.sb("ffn_ncol", [128, 1], F32, stA)
    scol = c.sb("ffn_scol", [128, 1], F32, stA)
    sgcol = c.sb("ffn_sgcol", [128, 1], F32, stA)
    ps_r = c.ps("ffn_ps_r", [128, 8], F32, stA)
    ps_tt = c.ps("ffn_ps_tt", [8, 128], BF16, stA)
    ps_sg = c.ps("ffn_ps_sg", [128, 1], F32, stA)

    @block.gpsimd
    def _(gpsimd):
        c.inc("ffnAg", gpsimd.dma_start(
            out=rw[:].rearrange("p (k e) -> p k e", k=8),
            in_=wt_kpf(wdram["F0_router_w"])), 16)
        c.inc("ffnAg", gpsimd.dma_start(
            out=sgwt[:], in_=wdram["F0_sgw"].rearrange("(k p) -> k p", p=128).transpose([1, 0]),
            allow_slow_non_contiguous=True), 16)

    for tg in range(NTG):
        t0 = 3 + tg * 128

        @block.tensor
        def _(tensor, tg=tg, t0=t0):
            c.wait(tensor, "ffnAg", 32)
            if tg >= 1:
                c.wait(tensor, "ffnAv", 2 * tg)
            for kc in range(8):
                tensor.matmul(ps_r[:, :], xnv[:, kc, t0:t0 + 128],
                              rw[:, kc * 8:(kc + 1) * 8],
                              start=(kc == 0), stop=(kc == 7))
            for kc in range(8):
                h = tensor.matmul(ps_sg[:, :], xnv[:, kc, t0:t0 + 128],
                                  sgwt[:, kc:kc + 1],
                                  start=(kc == 0), stop=(kc == 7))
            c.inc("ffnAt", h, 1)

        @block.vector
        def _(vector, tg=tg, t0=t0):
            c.wait(vector, "ffnAt", tg + 1)
            c.seq(vector, vector.tensor_copy(out=prob[:], in_=ps_r[:, :]), "V")
            c.seq(vector, vector.tensor_reduce(out=ncol[:], in_=prob[:],
                                               axis=AX.X, op=OP.max), "V")
            c.seq(vector, vector.tensor_scalar_mul(out=ncol[:], in0=ncol[:],
                                                   scalar1=-1.0), "V")
            c.inc("ffnAv1", vector.engine_nop(), 1)

        @block.scalar
        def _(scalar, tg=tg):
            c.wait(scalar, "ffnAv1", tg + 1)
            c.inc("ffnAs", scalar.activation(out=prob[:], in_=prob[:], func=AF.Exp,
                                             bias=ncol[:, :]), 1)

        @block.vector
        def _(vector, tg=tg):
            c.wait(vector, "ffnAs", tg + 1)
            c.seq(vector, vector.tensor_reduce(out=scol[:], in_=prob[:],
                                               axis=AX.X, op=OP.add), "V")
            c.seq(vector, vector.reciprocal(out=scol[:], in_=scol[:]), "V")
            c.seq(vector, vector.tensor_scalar(out=prob[:], in0=prob[:],
                                               scalar1=scol[:, :], scalar2=None,
                                               op0=OP.mult), "V")
            c.seq(vector, vector.tensor_tensor(
                out=gcmp[:],
                in0=prob[:].unsqueeze(1).broadcast_to((128, 8, 8)),
                in1=prob[:].unsqueeze(2).broadcast_to((128, 8, 8)),
                op=OP.is_gt), "V")
            c.seq(vector, vector.tensor_reduce(out=cnt[:], in_=gcmp[:].rearrange(
                "p (e f) -> p e f", e=8), axis=AX.X, op=OP.add), "V")
            c.seq(vector, vector.tensor_single_scalar(out=cnt[:], in_=cnt[:],
                                                      scalar=1.5, op=OP.is_le), "V")
            h = vector.tensor_tensor(out=comb[:, tg * 8:(tg + 1) * 8],
                                     in0=prob[:], in1=cnt[:], op=OP.mult)
            c.inc("ffnAv", h, 2)

    for tg in range(NTG):
        @block.tensor
        def _(tensor, tg=tg):
            c.wait(tensor, "ffnAv", 2 * NTG)
            if tg >= 1:
                c.wait(tensor, "ffnAv2", 2 * tg)
            h = tensor.transpose(ps_tt[:, :], comb[:, tg * 8:(tg + 1) * 8],
                                 ident[:])
            c.inc("ffnAt2", h, 1)

        @block.vector
        def _(vector, tg=tg):
            c.wait(vector, "ffnAt2", tg + 1)
            c.inc("ffnAv2", vector.tensor_copy(
                out=combrows[:, tg * 128:(tg + 1) * 128], in_=ps_tt[:, :]), 1)
            c.inc("ffnAv2", vector.engine_nop(), 1)

    @block.vector
    def _(vector):
        c.wait(vector, "ffnAv2", 2 * NTG)
        c.inc("ffnAv2_done", vector.engine_nop(), 1)

    barrier(c, block, sems=["ffnAv", "ffnAv2", "ffnAv2_done", "ffnAt", "ffnAt2",
                            "ffnAs", "ffnAv1"], release=["ffnAg", "ffnAt", "ffnAv1", "ffnAs"])
    stA.close()

    stB = ExitStack()
    sgl = c.sb("ffn_sgl", [128, 1], BF16, stB)
    ps_sg2 = c.ps("ffn_ps_sg2", [128, 1], F32, stB)
    ps_tt2 = c.ps("ffn_ps_tt2", [1, 128], BF16, stB)
    sgwt2 = c.sb("ffn_sgw2", [128, 8], BF16, stB)
    ps_bc = c.ps("ffn_ps_bc", [128, 512], F32, stB)

    @block.gpsimd
    def _(gpsimd):
        c.inc("ffnBg", gpsimd.dma_start(
            out=sgwt2[:], in_=wdram["F0_sgw"].rearrange("(k p) -> k p", p=128).transpose([1, 0]),
            allow_slow_non_contiguous=True), 16)

    @block.scalar
    def _(scalar):
        c.table(scalar, SILU)
        c.inc("ffnBs0", scalar.nop(), 1)

    for tg in range(NTG):
        t0 = 3 + tg * 128

        @block.tensor
        def _(tensor, tg=tg, t0=t0):
            c.wait(tensor, "ffnBg", 16)
            if tg >= 1:
                c.wait(tensor, "ffnBv", 2 * tg)
            for kc in range(8):
                h = tensor.matmul(ps_sg2[:, :], xnv[:, kc, t0:t0 + 128],
                                  sgwt2[:, kc:kc + 1],
                                  start=(kc == 0), stop=(kc == 7))
            c.inc("ffnBt", h, 1)

        @block.scalar
        def _(scalar, tg=tg):
            c.wait(scalar, "ffnBt", tg + 1)
            c.wait(scalar, "ffnBs0", 1)
            c.inc("ffnBs", scalar.activation(out=sgl[:], in_=ps_sg2[:, :],
                                             func=AF.Tanh, scale=0.5), 1)

        @block.vector
        def _(vector, tg=tg):
            c.wait(vector, "ffnBs", tg + 1)
            c.seq(vector, vector.tensor_scalar(out=sgl[:], in0=sgl[:], scalar1=0.5,
                                               scalar2=0.5, op0=OP.mult, op1=OP.add), "V")
            c.inc("ffnBv1", vector.engine_nop(), 1)

        @block.tensor
        def _(tensor, tg=tg):
            c.wait(tensor, "ffnBv1", tg + 1)
            c.inc("ffnBt2", tensor.transpose(ps_tt2[:, :], sgl[:, :], ident[:]), 1)

        @block.vector
        def _(vector, tg=tg):
            c.wait(vector, "ffnBt2", tg + 1)
            h = vector.tensor_copy(out=sgrow[0:1, tg * 128:(tg + 1) * 128],
                                   in_=ps_tt2[0:1, :])
            c.inc("ffnBv", h, 2)

    for w5 in range(NT5):
        @block.tensor
        def _(tensor, w5=w5):
            c.wait(tensor, "ffnBv", 2 * NTG)
            if w5 >= 1:
                c.wait(tensor, "ffnBv2", w5)
            c.inc("ffnBt3", tensor.matmul(
                ps_bc[:, :], ones_row[0:1, :], sgrow[0:1, w5 * 512:(w5 + 1) * 512],
                start=True, stop=True), 1)

        @block.vector
        def _(vector, w5=w5):
            c.wait(vector, "ffnBt3", w5 + 1)
            c.inc("ffnBv2", vector.tensor_copy(
                out=sgbc[:, w5 * 512:(w5 + 1) * 512], in_=ps_bc[:, :]), 1)

    barrier(c, block, sems=["ffnBv", "ffnBv2", "ffnBs", "ffnBt", "ffnBt2",
                            "ffnBt3", "ffnBv1", "ffnBs0"], release=["ffnB"])
    stB.close()

    stC = ExitStack()
    w1b = [c.sb(f"ffn_w1_{i}", [128, 8 * 1024], BF16, stC) for i in range(2)]
    w2s = c.sb("ffn_w2s", [128, 8 * 1024], BF16, stC)
    he = c.sb("ffn_he", [128, 8 * TC], BF16, stC)
    cbc = c.sb("ffn_cbc", [128, TC], BF16, stC)
    HW = NT5 * 512
    ps_h = c.ps("ffn_ps_h", [128, 2 * HW], F32, stC)
    hev = he[:].rearrange("p (o t) -> p o t", o=8)
    gv = g[:].rearrange("p (d t) -> p d t", d=8)

    for e in range(8):
        wb1, wb2 = w1b[e % 2], w2s

        @block.gpsimd
        def _(gpsimd, e=e, wb1=wb1, wb2=wb2):
            if e >= 2:
                c.wait(gpsimd, "ffnCt_done", e - 1)
            c.inc(f"ffnCg{e % 2}", gpsimd.dma_start(
                out=wb1[:].rearrange("p (k f) -> p k f", k=8),
                in_=wt_kpf(wdram["F0_w1"].rearrange("E d f -> (E d) f")
                           [e * 1024:(e + 1) * 1024, :])), 16)
            if e >= 1:
                c.wait(gpsimd, "ffnCt_w2done", e)
            c.inc("ffnCgw2", gpsimd.dma_start(
                out=wb2[:].rearrange("p (k f) -> p k f", k=8),
                in_=wt_kpf(wdram["F0_w2"].rearrange("E d f -> (E d) f")
                           [e * 1024:(e + 1) * 1024, :])), 16)

        for w5 in range(NT5):
            @block.tensor
            def _(tensor, e=e, w5=w5):
                c.wait(tensor, "ffnCs_h", max(0, 8 * e - 1))
                if w5 == 0:
                    c.wait(tensor, "ffnAv2_done", 1)
                c.inc("ffnCt_cb", tensor.matmul(
                    ps_h[:, w5 * 512:(w5 + 1) * 512], sel8[:, e * 128:(e + 1) * 128],
                    combrows[:, w5 * 512:(w5 + 1) * 512],
                    start=True, stop=True), 1)

        @block.vector
        def _(vector, e=e):
            c.wait(vector, "ffnCt_cb", NT5 * (e + 1))
            c.inc("ffnCv_cb", vector.tensor_copy(out=cbc[:, :], in_=ps_h[:, 0:HW]), 1)

        for oc in range(8):
            slot = (oc % 2) * HW

            @block.tensor
            def _(tensor, e=e, oc=oc, wb1=wb1, slot=slot):
                c.wait(tensor, f"ffnCg{e % 2}", 16 * (e // 2 + 1))
                if oc == 0:
                    c.wait(tensor, "ffnCv_cb", e + 1)
                c.wait(tensor, "ffnCs_h", max(0, 8 * e + oc - 1))
                for kc in range(8):
                    for tg in range(NT5):
                        h = tensor.matmul(
                            ps_h[:, slot + tg * 512:slot + (tg + 1) * 512],
                            wb1[:, (kc * 8 + oc) * 128:(kc * 8 + oc) * 128 + 128],
                            xnv[:, kc, 3 + tg * 512:3 + (tg + 1) * 512],
                            start=(kc == 0), stop=(kc == 7))
                c.inc("ffnCt_h", h, 1)

            @block.scalar
            def _(scalar, e=e, oc=oc, slot=slot):
                c.wait(scalar, "ffnCt_h", 8 * e + oc + 1)
                c.inc("ffnCs_h", scalar.activation(
                    out=hev[:, oc, :], in_=ps_h[:, slot:slot + HW], func=AF.Silu), 1)

        @block.vector
        def _(vector, e=e):
            c.wait(vector, "ffnCs_h", 8 * (e + 1))
            h = vector.tensor_tensor(
                out=hev[:, :, :], in0=hev[:, :, :],
                in1=cbc[:].unsqueeze(1).broadcast_to((128, 8, TC)),
                op=OP.mult)
            c.inc("ffnCv_sc", h, 1)

        for oc in range(8):
            slot = (oc % 2) * HW

            @block.tensor
            def _(tensor, e=e, oc=oc, wb2=wb2, slot=slot):
                c.wait(tensor, "ffnCv_sc", e + 1)
                c.wait(tensor, "ffnCgw2", 16 * (e + 1))
                c.wait(tensor, "ffnCv_a", max(0, 8 * e + oc - 1))
                for kc in range(8):
                    for tg in range(NT5):
                        h = tensor.matmul(
                            ps_h[:, slot + tg * 512:slot + (tg + 1) * 512],
                            wb2[:, (kc * 8 + oc) * 128:(kc * 8 + oc) * 128 + 128],
                            hev[:, kc, tg * 512:(tg + 1) * 512],
                            start=(kc == 0), stop=(kc == 7))
                c.inc("ffnCt_a", h, 1)

            @block.vector
            def _(vector, e=e, oc=oc, slot=slot):
                c.wait(vector, "ffnCt_a", 8 * e + oc + 1)
                if e == 0:
                    h = vector.tensor_copy(out=gv[:, oc, :], in_=ps_h[:, slot:slot + HW])
                else:
                    h = vector.tensor_add(out=gv[:, oc, :], in0=gv[:, oc, :],
                                          in1=ps_h[:, slot:slot + HW])
                c.inc("ffnCv_a", h, 1)

        @block.tensor
        def _(tensor, e=e):
            c.wait(tensor, "ffnCv_a", 8 * (e + 1))
            c.inc("ffnCt_done", tensor.nop(), 1)
            c.inc("ffnCt_w2done", tensor.nop(), 1)

    barrier(c, block, sems=["ffnCv_a", "ffnCs_h", "ffnCt_a", "ffnCt_cb",
                            "ffnCv_cb", "ffnCv_sc", "ffnCt_h"],
            release=["ffnC", "ffnAv"])
    stC.close()

    stD = ExitStack()
    sgu_b = c.sb("ffn_sgu", [128, 16 * 1024], BF16, stD)
    sdn_b = c.sb("ffn_sdn", [128, 8 * 1024], BF16, stD)
    gate = c.sb("ffn_gate", [128, 8 * 512], BF16, stD)
    up = c.sb("ffn_up", [128, 8 * 512], BF16, stD)
    ps_h2 = c.ps("ffn_ps_h2", [128, 2 * 512], F32, stD)
    ps_a2 = c.ps("ffn_ps_a2", [128, 4 * 512], F32, stD)
    gatev = gate[:].rearrange("p (o t) -> p o t", o=8)
    upv = up[:].rearrange("p (o t) -> p o t", o=8)

    @block.gpsimd
    def _(gpsimd):
        c.inc("ffnDg", gpsimd.dma_start(
            out=sgu_b[:].rearrange("p (k f) -> p k f", k=8),
            in_=wt_kpf(wdram["F0_sgu"])), 16)
        c.inc("ffnDg", gpsimd.dma_start(
            out=sdn_b[:].rearrange("p (k f) -> p k f", k=8),
            in_=wt_kpf(wdram["F0_sdown"])), 16)

    for tg in range(NT5):
        t0 = 3 + tg * 512
        for oc in range(16):
            @block.tensor
            def _(tensor, tg=tg, oc=oc, t0=t0):
                c.wait(tensor, "ffnDg", 32)
                k = 16 * tg + oc
                if k >= 2:
                    kp = k - 2
                    ns = 8 * (kp // 16) + min(kp % 16 + 1, 8)
                    nv = 8 * (kp // 16) + max(kp % 16 + 1 - 8, 0)
                    if ns > 0:
                        c.wait(tensor, "ffnDs_g", ns)
                    if nv > 0:
                        c.wait(tensor, "ffnDv_u", nv)
                for kc in range(8):
                    h = tensor.matmul(
                        ps_h2[:, (oc % 2) * 512:(oc % 2) * 512 + 512],
                        sgu_b[:, (kc * 16 + oc) * 128:(kc * 16 + oc) * 128 + 128],
                        xnv[:, kc, t0:t0 + 512],
                        start=(kc == 0), stop=(kc == 7))
                c.inc("ffnDt", h, 1)

            if oc < 8:
                @block.scalar
                def _(scalar, tg=tg, oc=oc):
                    c.wait(scalar, "ffnDt", 16 * tg + oc + 1)
                    c.inc("ffnDs_g", scalar.activation(
                        out=gatev[:, oc, :], in_=ps_h2[:, (oc % 2) * 512:(oc % 2) * 512 + 512],
                        func=AF.Silu), 1)
            else:
                @block.vector
                def _(vector, tg=tg, oc=oc):
                    c.wait(vector, "ffnDt", 16 * tg + oc + 1)
                    c.inc("ffnDv_u", vector.tensor_copy(
                        out=upv[:, oc - 8, :],
                        in_=ps_h2[:, (oc % 2) * 512:(oc % 2) * 512 + 512]), 1)

        @block.vector
        def _(vector, tg=tg):
            c.wait(vector, "ffnDs_g", 8 * (tg + 1))
            c.wait(vector, "ffnDv_u", 8 * (tg + 1))
            c.seq(vector, vector.tensor_tensor(out=gate[:, :], in0=gate[:, :],
                                               in1=up[:, :], op=OP.mult), "V")
            h = vector.tensor_tensor(
                out=gatev[:, :, :], in0=gatev[:, :, :],
                in1=sgbc[:, tg * 512:(tg + 1) * 512].unsqueeze(1)
                    .broadcast_to((128, 8, 512)),
                op=OP.mult)
            c.seq(vector, h, "V")
            c.inc("ffnDv_su", vector.engine_nop(), 1)

        for oc in range(8):
            @block.tensor
            def _(tensor, tg=tg, oc=oc):
                c.wait(tensor, "ffnDv_su", tg + 1)
                k = 8 * tg + oc
                if k >= 4:
                    c.wait(tensor, "ffnDv_a", k - 3)
                for kc in range(8):
                    h = tensor.matmul(
                        ps_a2[:, (oc % 4) * 512:(oc % 4) * 512 + 512],
                        sdn_b[:, (kc * 8 + oc) * 128:(kc * 8 + oc) * 128 + 128],
                        gatev[:, kc, :],
                        start=(kc == 0), stop=(kc == 7))
                c.inc("ffnDt_a", h, 1)

            @block.vector
            def _(vector, tg=tg, oc=oc):
                c.wait(vector, "ffnDt_a", 8 * tg + oc + 1)
                h = vector.tensor_add(
                    out=gv[:, oc, tg * 512:(tg + 1) * 512],
                    in0=gv[:, oc, tg * 512:(tg + 1) * 512],
                    in1=ps_a2[:, (oc % 4) * 512:(oc % 4) * 512 + 512])
                c.inc("ffnDv_a", h, 1)

    barrier(c, block, sems=["ffnDv_a", "ffnDt_a", "ffnDs_g", "ffnDv_u",
                            "ffnDv_su", "ffnDt"], release=["ffnD"])
    stD.close()

    for dc in range(8):
        @block.scalar
        def _(scalar, dc=dc):
            h = scalar.activation(out=gv[:, dc, :], in_=gv[:, dc, :], func=AF.Silu)
            c.inc("ffnEs", h, 1)
    barrier(c, block, sems=["ffnEs"], release=["ffnE"])
    stP.close()



def ph_attn0(c, block, x, xn, g, wdram, ones128, ones_row, ident, sel33, chm, ccs):
    nc = c.nc
    TT5 = 512
    NTT = TC // TT5
    xv = x[:].rearrange("p (d t) -> p d t", d=8)
    xnv = xn[:].rearrange("p (d t) -> p d t", d=8)
    gv = g[:].rearrange("p (d t) -> p d t", d=8)
    cc0_in, cc0_out = ccs[0], ccs[1]

    stP = ExitStack()
    As = c.sb("a0_As", [128, 16 * S0], F32, stP)
    convw = c.sb("a0_convw", [128, 16 * 4], F32, stP)
    convb = c.sb("a0_convb", [128, 16], F32, stP)
    wdtc = c.sb("a0_wdt", [128, 16], F32, stP)
    bdtc = c.sb("a0_bdt", [128, 16], F32, stP)
    dpc = c.sb("a0_dp", [128, 16], F32, stP)
    wxp = c.sb("a0_wxp", [128, 16 * 33], BF16, stP)
    carry = c.sb("a0_carry", [128, 16 * S0], F32, stP)
    cdel = c.sb("a0_cdel", [128, 16], F32, stP)
    dsacc = c.sb("a0_dsacc", [128, 16 * 4], F32, stP)
    xdblk = c.sb("a0_xdbl", [33, TC], BF16, stP)
    Hbuf = c.sb("a0_H", [128, 16 * S0], F32, stP)
    Pacc = c.sb("a0_Pacc", [128, 16 * S0], F32, stP)
    Prun = c.sb("a0_Prun", [128, 16 * S0], F32, stP)
    tcmb = c.sb("a0_tcmb", [128, 16 * S0], F32, stP)
    dsum = c.sb("a0_dsum", [128, 16], F32, stP)

    y2t = c.sb("a0_y2", [128, 16 * TT5], BF16, stP)
    winb = [c.sb(f"a0_win{i}", [128, 8 * 128], BF16, stP) for i in range(2)]
    woutb = [c.sb(f"a0_wout{i}", [128, 16 * 128], BF16, stP) for i in range(2)]
    xd0bc = c.sb("a0_xd0bc", [128, TT5], BF16, stP)
    vpb = c.sb("a0_vpb", [128, TT5], F32, stP)
    schb = [c.sb(f"a0_sch{i}", [128, TT5], F32, stP) for i in range(2)]
    delta = c.sb("a0_delta", [128, TT5], BF16, stP)
    gb = c.sb("a0_gb", [128, 2 * TT5], BF16, stP)
    zs = c.sb("a0_zs", [128, TT5], BF16, stP)
    cum = c.sb("a0_cum", [128, TT5], F32, stP)
    ycor = c.sb("a0_ycor", [128, TT5], BF16, stP)
    zex = c.sb("a0_zex", [128, TT5], F32, stP)
    agj = c.sb("a0_agj", [128, 2 * 16 * S0], F32, stP)
    ones512 = c.sb("a0_ones512", [128, TT5], BF16, stP)

    stPA = ExitStack()
    xcp = c.sb("a0_xcp", [128, 16 * (TT5 + 3)], BF16, stPA)
    ctx3 = c.sb("a0_ctx", [128, 16 * 3], BF16, stPA)
    bbc = c.sb("a0_bbc", [128, 2 * TT5], BF16, stPA)
    cbc2 = c.sb("a0_cbc2", [128, 2 * TT5], BF16, stPA)
    wdc = c.sb("a0_wdc", [128, TT5], BF16, stPA)
    dab = [c.sb(f"a0_dab{i}", [128, 2 * (TT5 + 1)], BF16, stPA) for i in range(2)]
    ub = [c.sb(f"a0_ub{i}", [128, 2 * (TT5 + 1)], BF16, stPA) for i in range(2)]
    hb = [c.sb(f"a0_hb{i}", [128, 2 * (TT5 + 1)], BF16, stPA) for i in range(2)]
    ydc = c.sb("a0_ydc", [128, TT5], BF16, stPA)

    ps_a = c.ps("a0_ps_a", [128, 2 * TT5], F32, stP)
    ps_xd = c.ps("a0_ps_xd", [33, TT5], F32, stP)
    ps_bc = c.ps("a0_ps_bc", [128, 2 * TT5], F32, stP)

    xcpv = xcp[:].rearrange("p (k t) -> p k t", k=16)
    y2v = y2t[:].rearrange("p (k t) -> p k t", k=16)

    @block.gpsimd
    def _(gpsimd):
        c.inc("a0g", gpsimd.dma_start(out=As[:].rearrange("p (k s) -> p k s", k=16),
                                      in_=wt_kpf(wdram["L0_A_log"])), 16)
        c.inc("a0g", gpsimd.dma_start(out=convw[:].rearrange("p (k s) -> p k s", k=16),
                                      in_=wt_kpf(wdram["L0_conv_w"])), 16)
        for nm, dst in [("L0_conv_b", convb), ("L0_W_dt", wdtc),
                        ("L0_b_dt", bdtc), ("L0_Dp", dpc)]:
            c.inc("a0g", gpsimd.dma_start(
                out=dst[:], in_=wdram[nm].rearrange("(k p) -> k p", p=128)
                .transpose([1, 0]), allow_slow_non_contiguous=True), 16)
        c.inc("a0g", gpsimd.dma_start(out=wxp[:].rearrange("p (k s) -> p k s", k=16),
                                      in_=wt_kpf(wdram["L0_W_xproj"])), 16)
        gpsimd.memset(carry[:], 0.0)
        gpsimd.memset(cdel[:], 0.0)
        gpsimd.memset(dab[0][:], 0.0)
        gpsimd.memset(dab[1][:], 0.0)
        c.inc("a0g2", gpsimd.memset(ones512[:], 1.0), 1)

    @block.scalar
    def _(scalar):
        c.wait(scalar, "a0g", c.val("a0g"))
        c.wait(scalar, "a0g2", 1)
        c.table(scalar, NLEXP)
        c.inc("a0s0", scalar.activation(out=As[:], in_=As[:], func=AF.Exp), 1)

    @block.vector
    def _(vector):
        c.wait(vector, "a0s0", 1)
        c.seq(vector, vector.tensor_scalar_mul(out=As[:], in0=As[:], scalar1=-1.0), "V")
        c.inc("a0v0", vector.tensor_scalar_mul(out=convb[:], in0=convb[:],
                                               scalar1=-1.0), 1)

    def tail_z_y2(tt, dc, M):
        tb = 3 + tt * TT5
        k = 16 * tt + dc

        @block.gpsimd
        def _(gpsimd, tt=tt, dc=dc, k=k):
            if M == "A" and dc < 2:
                c.wait(gpsimd, "a0t_xz", 16 * (tt + 1))
            if k >= 2:
                c.wait(gpsimd, f"a0t_{M}z", k - 1)
            c.inc(f"a0g_{M}z{dc % 2}", gpsimd.dma_start(
                out=winb[dc % 2][:].rearrange("p (k f) -> p k f", k=8),
                in_=wt_kpf(wdram["L0_W_in"])[:, :, 2048 + dc * 128:2048 + (dc + 1) * 128]),
                16)

        @block.tensor
        def _(tensor, tt=tt, dc=dc, k=k):
            c.wait(tensor, f"a0g_{M}z{dc % 2}", 16 * (k // 2 + 1))
            if M == "A":
                c.wait(tensor, "a0v_bc", 8 * (k + 1))
            if k >= 2:
                c.wait(tensor, f"a0v_{M}z", k - 1)
            slot = (dc % 2) * TT5
            for kc in range(8):
                h = tensor.matmul(ps_a[:, slot:slot + TT5],
                                  winb[dc % 2][:, kc * 128:(kc + 1) * 128],
                                  xnv[:, kc, tb:tb + TT5],
                                  start=(kc == 0), stop=(kc == 7))
            c.inc(f"a0t_{M}z", h, 1)

        @block.scalar
        def _(scalar, tt=tt, dc=dc, k=k):
            c.wait(scalar, f"a0t_{M}z", k + 1)
            if k >= 1:
                c.wait(scalar, f"a0v_{M}z", k)
            slot = (dc % 2) * TT5
            c.inc(f"a0s_{M}z", scalar.activation(
                out=zex[:, :], in_=ps_a[:, slot:slot + TT5], func=AF.Exp,
                scale=-1.0), 1)

        @block.vector
        def _(vector, tt=tt, dc=dc, k=k):
            c.wait(vector, f"a0s_{M}z", k + 1)
            slot = (dc % 2) * TT5
            c.seq(vector, vector.tensor_copy(out=zs[:, :],
                                             in_=ps_a[:, slot:slot + TT5]), "V")
            c.seq(vector, vector.tensor_scalar(
                out=zex[:, :], in0=zex[:, :], scalar1=1.0, scalar2=None,
                op0=OP.add), "V")
            c.seq(vector, vector.reciprocal(out=zex[:, :], in_=zex[:, :]), "V")
            c.seq(vector, vector.tensor_mul(out=zs[:, :], in0=zs[:, :],
                                            in1=zex[:, :]), "V")
            if M == "A":
                c.seq(vector, vector.scalar_tensor_tensor(
                    out=gb[:, 0:TT5], in0=xcpv[:, dc, 3:3 + TT5],
                    scalar=dpc[:, dc:dc + 1], in1=ydc[:, :],
                    op0=OP.mult, op1=OP.add), "V")
                h = vector.tensor_mul(out=y2v[:, dc, :], in0=gb[:, 0:TT5], in1=zs[:, :])
            else:
                h = vector.tensor_mul(out=y2v[:, dc, :], in0=ycor[:, :], in1=zs[:, :])
            nn = c.inc(f"a0v_{M}z", h, 1)
            c.wait(vector, f"a0v_{M}z", nn)

    def tail_wout_x(tt, M):
        for oc in range(8):
            k = 8 * tt + oc

            @block.gpsimd
            def _(gpsimd, tt=tt, oc=oc, k=k):
                if k >= 2:
                    c.wait(gpsimd, f"a0t_{M}o", k - 1)
                c.inc(f"a0g_{M}o{oc % 2}", gpsimd.dma_start(
                    out=woutb[oc % 2][:].rearrange("p (k f) -> p k f", k=16),
                    in_=wt_kpf(wdram["L0_W_out"])[:, :, oc * 128:(oc + 1) * 128]), 16)

            @block.tensor
            def _(tensor, tt=tt, oc=oc, k=k):
                c.wait(tensor, f"a0g_{M}o{oc % 2}", 16 * (k // 2 + 1))
                c.wait(tensor, f"a0v_{M}z", 16 * (tt + 1))
                if k >= 2:
                    c.wait(tensor, f"a0v_{M}o2", k - 1)
                slot = (oc % 2) * TT5
                for dc in range(16):
                    h = tensor.matmul(ps_a[:, slot:slot + TT5],
                                      woutb[oc % 2][:, dc * 128:(dc + 1) * 128],
                                      y2v[:, dc, :],
                                      start=(dc == 0), stop=(dc == 15))
                c.inc(f"a0t_{M}o", h, 1)

            @block.vector
            def _(vector, tt=tt, oc=oc, k=k):
                c.wait(vector, f"a0t_{M}o", k + 1)
                slot = (oc % 2) * TT5
                tb = 3 + tt * TT5
                c.seq(vector, vector.tensor_mul(
                    out=gb[:, TT5:2 * TT5],
                    in0=gv[:, oc, tt * TT5:(tt + 1) * TT5],
                    in1=ps_a[:, slot:slot + TT5]), "V")
                h = vector.tensor_add(out=xv[:, oc, tb:tb + TT5],
                                      in0=xv[:, oc, tb:tb + TT5],
                                      in1=gb[:, TT5:2 * TT5])
                nn = c.inc(f"a0v_{M}o2", h, 1)
                c.wait(vector, f"a0v_{M}o2", nn)

    for tt in range(NTT):
        tb = 3 + tt * TT5

        for oc in range(16):
            wb = winb[oc % 2]
            k = 16 * tt + oc

            @block.gpsimd
            def _(gpsimd, tt=tt, oc=oc, wb=wb, k=k):
                if tt > 0 and oc < 2:
                    c.wait(gpsimd, "a0t_Az", 16 * tt)
                if k >= 2:
                    c.wait(gpsimd, "a0t_xz", k - 1)
                c.inc(f"a0g_w{oc % 2}", gpsimd.dma_start(
                    out=wb[:].rearrange("p (k f) -> p k f", k=8),
                    in_=wt_kpf(wdram["L0_W_in"])[:, :, oc * 128:(oc + 1) * 128]), 16)

            @block.tensor
            def _(tensor, tt=tt, oc=oc, wb=wb, tb=tb, k=k):
                c.wait(tensor, f"a0g_w{oc % 2}", 16 * (k // 2 + 1))
                if k >= 2:
                    c.wait(tensor, "a0v_xz", k - 1)
                if tt == 0 and oc == 0:
                    c.wait(tensor, "a0v0", 1)
                if tt > 0 and oc == 0:
                    c.wait(tensor, "a0v_ctx", tt)
                if tt > 0 and oc == 0:
                    c.wait(tensor, "a0v_bc", 128 * tt)
                if tt > 0 and oc < 2:
                    c.wait(tensor, "a0v_Ao2", 8 * tt)
                slot = (oc % 2) * TT5
                for kc in range(8):
                    h = tensor.matmul(ps_a[:, slot:slot + TT5],
                                      wb[:, kc * 128:(kc + 1) * 128],
                                      xnv[:, kc, tb:tb + TT5],
                                      start=(kc == 0), stop=(kc == 7))
                if tt == 0:
                    for kc in range(8):
                        h = tensor.matmul(ps_bc[:, slot:slot + 3],
                                          wb[:, kc * 128:(kc + 1) * 128],
                                          xnv[:, kc, 0:3],
                                          start=(kc == 0), stop=(kc == 7))
                c.inc("a0t_xz", h, 1)

            @block.vector
            def _(vector, tt=tt, oc=oc, k=k):
                c.wait(vector, "a0t_xz", k + 1)
                slot = (oc % 2) * TT5
                if tt == 0:
                    c.seq(vector, vector.tensor_copy(out=xcpv[:, oc, 0:3],
                                                     in_=ps_bc[:, slot:slot + 3]), "V")
                else:
                    c.seq(vector, vector.tensor_copy(
                        out=xcpv[:, oc, 0:3],
                        in_=ctx3[:].rearrange("p (k t) -> p k t", k=16)[:, oc, :]), "V")
                h = vector.tensor_copy(out=xcpv[:, oc, 3:3 + TT5],
                                       in_=ps_a[:, slot:slot + TT5])
                c.inc("a0v_xz", h, 1)

        @block.vector
        def _(vector, tt=tt):
            c.wait(vector, "a0v_xz", 16 * (tt + 1))
            h = vector.tensor_copy(out=ctx3[:].rearrange("p (k t) -> p k t", k=16),
                                   in_=xcpv[:, :, TT5:TT5 + 3])
            nn = c.inc("a0v_ctx", h, 1)
            c.wait(vector, "a0v_ctx", nn)

        for dc in range(16):
            k = 16 * tt + dc

            @block.vector
            def _(vector, tt=tt, dc=dc, k=k):
                if k >= 2:
                    c.wait(vector, "a0s_cv", k - 1)
                sch = schb[dc % 2]
                c.seq(vector, vector.tensor_scalar(
                    out=sch[:, :], in0=xcpv[:, dc, 0:TT5],
                    scalar1=convw[:, dc * 4:dc * 4 + 1], scalar2=None,
                    op0=OP.mult), "V")
                for j in range(1, 4):
                    h = vector.scalar_tensor_tensor(
                        out=sch[:, :], in0=xcpv[:, dc, j:j + TT5],
                        scalar=convw[:, dc * 4 + j:dc * 4 + j + 1],
                        in1=sch[:, :], op0=OP.mult, op1=OP.add)
                    if j < 3:
                        c.seq(vector, h, "V")
                c.inc("a0v_cv", h, 1)

            @block.scalar
            def _(scalar, tt=tt, dc=dc, k=k):
                c.wait(scalar, "a0v_cv", k + 1)
                if k >= 1:
                    c.wait(scalar, "a0v_cv2", k)
                h = scalar.activation(out=vpb[:, :], in_=schb[dc % 2][:, :],
                                      func=AF.Exp, scale=-1.0, bias=convb[:, dc:dc + 1])
                c.inc("a0s_cv", h, 1)

            @block.vector
            def _(vector, tt=tt, dc=dc, k=k):
                c.wait(vector, "a0s_cv", k + 1)
                c.seq(vector, vector.tensor_scalar(
                    out=vpb[:, :], in0=vpb[:, :], scalar1=1.0, scalar2=None,
                    op0=OP.add), "V")
                c.seq(vector, vector.reciprocal(out=vpb[:, :], in_=vpb[:, :]), "V")
                c.seq(vector, vector.tensor_scalar(
                    out=gb[:, 0:TT5], in0=schb[dc % 2][:, :],
                    scalar1=convb[:, dc:dc + 1], scalar2=None, op0=OP.subtract), "V")
                h = vector.tensor_mul(out=xcpv[:, dc, 3:3 + TT5], in0=gb[:, 0:TT5],
                                      in1=vpb[:, :])
                c.inc("a0v_cv2", h, 1)

        @block.tensor
        def _(tensor, tt=tt):
            c.wait(tensor, "a0v_cv2", 16 * (tt + 1))
            if tt >= 1:
                c.wait(tensor, "a0v_xd", tt)
            for kc in range(16):
                h = tensor.matmul(ps_xd[:, :], wxp[:, kc * 33:(kc + 1) * 33],
                                  xcpv[:, kc, 3:3 + TT5],
                                  start=(kc == 0), stop=(kc == 15))
            c.inc("a0t_xd", h, 1)

        @block.vector
        def _(vector, tt=tt):
            c.wait(vector, "a0t_xd", tt + 1)
            c.inc("a0v_xd", vector.tensor_copy(
                out=xdblk[:, tt * TT5:(tt + 1) * TT5], in_=ps_xd[:, :]), 1)

        @block.tensor
        def _(tensor, tt=tt):
            c.wait(tensor, "a0v_xd", tt + 1)
            if tt >= 1:
                c.wait(tensor, "a0v_xd0", tt)
                c.wait(tensor, "a0v_bc", 128 * tt)
            c.inc("a0t_xd0", tensor.matmul(
                ps_bc[:, TT5:2 * TT5], sel33[:, 0:128],
                xdblk[:, tt * TT5:(tt + 1) * TT5], start=True, stop=True), 1)

        @block.vector
        def _(vector, tt=tt):
            c.wait(vector, "a0t_xd0", tt + 1)
            c.inc("a0v_xd0", vector.tensor_copy(out=xd0bc[:, :],
                                                in_=ps_bc[:, TT5:2 * TT5]), 1)

        for dc in range(16):
            k = 16 * tt + dc

            @block.vector
            def _(vector, tt=tt, dc=dc, k=k):
                c.wait(vector, "a0v_xd0", tt + 1)
                if k >= 1:
                    c.wait(vector, "a0s_dl", 2 * k)
                h = vector.tensor_scalar(
                    out=vpb[:, :], in0=xd0bc[:, :],
                    scalar1=wdtc[:, dc:dc + 1], scalar2=bdtc[:, dc:dc + 1],
                    op0=OP.mult, op1=OP.add)
                c.inc("a0v_dl1", h, 1)

            @block.scalar
            def _(scalar, tt=tt, dc=dc, k=k):
                c.wait(scalar, "a0v_dl1", k + 1)
                if k >= 1:
                    c.wait(scalar, "a0v_dl3", k)
                c.seq(scalar, scalar.activation(out=schb[0][:, :], in_=vpb[:, :],
                                                func=AF.Abs), "S")
                c.seq(scalar, scalar.activation(out=schb[0][:, :], in_=schb[0][:, :],
                                                func=AF.Exp, scale=-1.0), "S")
                h = scalar.activation(out=schb[1][:, :], in_=schb[0][:, :],
                                      func=AF.Ln, bias=1.0)
                c.inc("a0s_dl", h, 2)

            @block.vector
            def _(vector, tt=tt, dc=dc, k=k):
                c.wait(vector, "a0s_dl", 2 * (k + 1))
                if k >= 1:
                    c.wait(vector, "a0s_da", 8 * k)
                h = vector.scalar_tensor_tensor(
                    out=delta[:, :], in0=vpb[:, :], scalar=0.0, in1=schb[1][:, :],
                    op0=OP.max, op1=OP.add,
                    accum_out=dsacc[:, tt * 16 + dc:tt * 16 + dc + 1])
                nn = c.inc("a0v_dl2", h, 1)
                c.wait(vector, "a0v_dl2", nn)
                h = vector.tensor_mul(out=wdc[:, :], in0=delta[:, :],
                                      in1=xcpv[:, dc, 3:3 + TT5])
                nn = c.inc("a0v_dl3", h, 1)
                c.wait(vector, "a0v_dl3", nn)

            for sp in range(8):
                sidx = (16 * tt + dc) * 8 + sp
                par = sp % 2

                @block.tensor
                def _(tensor, tt=tt, dc=dc, sp=sp, sidx=sidx):
                    c.wait(tensor, "a0v_dl3", 16 * tt + dc + 1)
                    if sidx >= 1:
                        c.wait(tensor, "a0v_bc", sidx)
                    for half in range(2):
                        s = 2 * sp + half
                        tensor.matmul(ps_bc[:, half * TT5:(half + 1) * TT5],
                                      sel33[:, (1 + s) * 128:(2 + s) * 128],
                                      xdblk[:, tt * TT5:(tt + 1) * TT5],
                                      start=True, stop=True)
                        h = tensor.matmul(ps_a[:, half * TT5:(half + 1) * TT5],
                                          sel33[:, (17 + s) * 128:(18 + s) * 128],
                                          xdblk[:, tt * TT5:(tt + 1) * TT5],
                                          start=True, stop=True)
                    c.inc("a0t_bc", h, 1)

                @block.vector
                def _(vector, tt=tt, dc=dc, sp=sp, sidx=sidx, par=par):
                    c.wait(vector, "a0t_bc", sidx + 1)
                    c.seq(vector, vector.tensor_copy(out=bbc[:, :], in_=ps_bc[:, :]), "V")
                    h = vector.tensor_copy(out=cbc2[:, :], in_=ps_a[:, :])
                    nn = c.inc("a0v_bc", h, 1)
                    c.wait(vector, "a0v_bc", nn)
                    ubv = ub[par][:].rearrange("p (a b) -> p a b", a=2)
                    c.seq(vector, vector.tensor_tensor(
                        out=ubv[:, :, 1:1 + TT5],
                        in0=wdc[:].unsqueeze(1).broadcast_to((128, 2, TT5)),
                        in1=bbc[:].rearrange("p (a b) -> p a b", a=2),
                        op=OP.mult), "V")
                    c.seq(vector, vector.tensor_copy(
                        out=ubv[:, :, 0:1],
                        in_=carry[:, dc * S0 + 2 * sp:dc * S0 + 2 * sp + 2]
                        .rearrange("p (a b) -> p a b", a=2)), "V")
                    c.wait(vector, "a0s_da", sidx + 1)
                    h = vector.tensor_tensor_scan(
                        out=hb[par][:, :], data0=dab[par][:, :], data1=ub[par][:, :],
                        initial=0.0, op0=OP.mult, op1=OP.add)
                    c.seq(vector, h, "V")
                    hbv = hb[par][:].rearrange("p (a b) -> p a b", a=2)
                    c.seq(vector, vector.tensor_copy(
                        out=carry[:, dc * S0 + 2 * sp:dc * S0 + 2 * sp + 2]
                        .rearrange("p (a b) -> p a b", a=2),
                        in_=hbv[:, :, TT5:TT5 + 1]), "V")
                    c.seq(vector, vector.tensor_tensor(
                        out=gb[:].rearrange("p (a b) -> p a b", a=2),
                        in0=hbv[:, :, 1:1 + TT5],
                        in1=cbc2[:].rearrange("p (a b) -> p a b", a=2),
                        op=OP.mult), "V")
                    if sp == 0:
                        h = vector.tensor_add(out=ydc[:, :], in0=gb[:, 0:TT5],
                                              in1=gb[:, TT5:2 * TT5])
                    else:
                        c.seq(vector, vector.tensor_add(
                            out=ydc[:, :], in0=ydc[:, :], in1=gb[:, 0:TT5]), "V")
                        h = vector.tensor_add(out=ydc[:, :], in0=ydc[:, :],
                                              in1=gb[:, TT5:2 * TT5])
                    nn = c.inc("a0v_scan", h, 1)
                    c.wait(vector, "a0v_scan", nn)

                @block.scalar
                def _(scalar, tt=tt, dc=dc, sp=sp, sidx=sidx, par=par):
                    c.wait(scalar, "a0v_dl2", 16 * tt + dc + 1)
                    if sidx >= 2:
                        c.wait(scalar, "a0v_scan", sidx - 1)
                    dav = dab[par][:].rearrange("p (a b) -> p a b", a=2)
                    c.seq(scalar, scalar.activation(
                        out=dav[:, 0, 1:1 + TT5], in_=delta[:, :], func=AF.Exp,
                        scale=As[:, dc * S0 + 2 * sp:dc * S0 + 2 * sp + 1]), "S")
                    h = scalar.activation(
                        out=dav[:, 1, 1:1 + TT5], in_=delta[:, :], func=AF.Exp,
                        scale=As[:, dc * S0 + 2 * sp + 1:dc * S0 + 2 * sp + 2])
                    c.inc("a0s_da", h, 1)

            tail_z_y2(tt, dc, "A")

        tail_wout_x(tt, "A")

    barrier(c, block, sems=["a0v_Az", "a0v_Ao2", "a0t_Ao", "a0s_Az", "a0v_scan",
                            "a0s_da", "a0v_bc", "a0t_bc", "a0s_cv", "a0v_cv",
                            "a0v_xz", "a0t_xz", "a0v_dl2", "a0v_dl3", "a0s_dl", "a0v_cv2"],
            release=["a0g_w", "a0t_xz", "a0v_xz", "a0v_ctx", "a0v_cv", "a0s_cv", "a0v_cv2",
                     "a0t_xd", "a0v_xd", "a0v_dl", "a0s_dl", "a0t_bc", "a0v_bc",
                     "a0s_da", "a0v_scan", "a0g_Az", "a0t_Az", "a0s_Az", "a0v_Az",
                     "a0g_Ao", "a0t_Ao", "a0v_Ao2"])
    stPA.close()
    cbcall = c.sb("a0_cbcall", [128, 16 * TT5], BF16, stP)
    cbav = cbcall[:].rearrange("p (s t) -> p s t", s=16)

    @block.vector
    def _(vector):
        c.seq(vector, vector.tensor_copy(out=dsum[:, :], in_=dsacc[:, 0:16]), "V")
        for _tt in range(1, NTT):
            c.seq(vector, vector.tensor_add(
                out=dsum[:, :], in0=dsum[:, :],
                in1=dsacc[:, _tt * 16:(_tt + 1) * 16]), "V")
        for dc in range(16):
            h = vector.tensor_scalar(
                out=Pacc[:, dc * S0:(dc + 1) * S0],
                in0=As[:, dc * S0:(dc + 1) * S0],
                scalar1=dsum[:, dc:dc + 1], scalar2=None, op0=OP.mult)
        c.inc("a0v_dec1", h, 1)

    @block.scalar
    def _(scalar):
        c.wait(scalar, "a0v_dec1", 1)
        c.inc("a0s_dec", scalar.activation(out=Pacc[:], in_=Pacc[:], func=AF.Exp), 1)

    @block.sync
    def _(sync):
        c.wait(sync, "a0s_dec", 1)
        c.inc("a0q_cc", sync.dma_start(out=cc0_in[:, 0:256], in_=Pacc[:]), 16)
        c.inc("a0q_cc", sync.dma_start(out=cc0_in[:, 256:512], in_=carry[:]), 16)

    @block.gpsimd
    def _(gpsimd):
        c.wait(gpsimd, "a0q_cc", 32)
        c.inc("a0g_cc", gpsimd.collective_compute(
            "AllGather", OP.bypass, ins=[cc0_in], outs=[cc0_out],
            replica_groups=[[0, 1, 2, 3], [4, 5, 6, 7]]), 1)

    @block.gpsimd
    def _(gpsimd):
        c.wait(gpsimd, "a0g_cc", 1)
        gpsimd.memset(Hbuf[:], 0.0)
        c.inc("a0g_hm", gpsimd.memset(Prun[:], 1.0), 1)

    for j in [2, 1, 0]:
        @block.sync
        def _(sync, j=j):
            c.wait(sync, "a0g_cc", 1)
            if j < 2:
                c.wait(sync, "a0v_cmb", 2 - j)
            c.inc("a0q_ag", sync.dma_start(
                out=agj[:], in_=cc0_out[j * 128:(j + 1) * 128, :]), 16)

        @block.vector
        def _(vector, j=j):
            c.wait(vector, "a0q_ag", 16 * (3 - j))
            c.wait(vector, "a0g_hm", 1)
            m = chm[:, j:j + 1]
            c.seq(vector, vector.tensor_mul(out=tcmb[:, :], in0=agj[:, 256:512],
                                            in1=Prun[:, :]), "V")
            c.seq(vector, vector.tensor_scalar(out=tcmb[:, :], in0=tcmb[:, :],
                                               scalar1=m, scalar2=None,
                                               op0=OP.mult), "V")
            c.seq(vector, vector.tensor_add(out=Hbuf[:, :], in0=Hbuf[:, :],
                                            in1=tcmb[:, :]), "V")
            c.seq(vector, vector.tensor_scalar(out=tcmb[:, :], in0=agj[:, 0:256],
                                               scalar1=1.0, scalar2=m,
                                               op0=OP.subtract, op1=OP.mult), "V")
            c.seq(vector, vector.tensor_scalar(out=tcmb[:, :], in0=tcmb[:, :],
                                               scalar1=1.0, scalar2=None,
                                               op0=OP.add), "V")
            h = vector.tensor_mul(out=Prun[:, :], in0=Prun[:, :], in1=tcmb[:, :])
            nn = c.inc("a0v_cmb", h, 1)
            c.wait(vector, "a0v_cmb", nn)

    barrier(c, block, sems=["a0v_cmb"])

    for tt in range(NTT):
        tb = 3 + tt * TT5

        @block.tensor
        def _(tensor, tt=tt):
            if tt >= 1:
                c.wait(tensor, "a0v_xd0B", tt)
                c.wait(tensor, "a0v_cbl", 16 * tt)
            c.inc("a0t_xd0B", tensor.matmul(
                ps_bc[:, 0:TT5], sel33[:, 0:128],
                xdblk[:, tt * TT5:(tt + 1) * TT5], start=True, stop=True), 1)

        @block.vector
        def _(vector, tt=tt):
            c.wait(vector, "a0t_xd0B", tt + 1)
            c.inc("a0v_xd0B", vector.tensor_copy(out=xd0bc[:, :],
                                                 in_=ps_bc[:, 0:TT5]), 1)

        for s in range(16):
            @block.tensor
            def _(tensor, tt=tt, s=s):
                c.wait(tensor, "a0v_xd0B", tt + 1)
                if tt * 16 + s >= 2:
                    c.wait(tensor, "a0v_cbl", 16 * tt + s - 1)
                c.inc("a0t_cbl", tensor.matmul(
                    ps_bc[:, (s % 2) * TT5:(s % 2 + 1) * TT5],
                    sel33[:, (17 + s) * 128:(18 + s) * 128],
                    xdblk[:, tt * TT5:(tt + 1) * TT5], start=True, stop=True), 1)

            @block.vector
            def _(vector, tt=tt, s=s):
                c.wait(vector, "a0t_cbl", 16 * tt + s + 1)
                c.inc("a0v_cbl", vector.tensor_copy(
                    out=cbav[:, s, :], in_=ps_bc[:, (s % 2) * TT5:(s % 2 + 1) * TT5]), 1)

        for dc in range(16):
            k = 16 * tt + dc

            @block.vector
            def _(vector, tt=tt, dc=dc, k=k):
                c.wait(vector, "a0v_cbl", 16 * (tt + 1))
                if k >= 1:
                    c.wait(vector, "a0s_dlB", 2 * k)
                h = vector.tensor_scalar(
                    out=vpb[:, :], in0=xd0bc[:, :],
                    scalar1=wdtc[:, dc:dc + 1], scalar2=bdtc[:, dc:dc + 1],
                    op0=OP.mult, op1=OP.add)
                c.inc("a0v_dl1B", h, 1)

            @block.scalar
            def _(scalar, tt=tt, dc=dc, k=k):
                c.wait(scalar, "a0v_dl1B", k + 1)
                if k >= 1:
                    c.wait(scalar, "a0v_dl2B", k)
                    c.wait(scalar, "a0v_pb", 16 * k)
                c.seq(scalar, scalar.activation(out=schb[0][:, :], in_=vpb[:, :],
                                                func=AF.Abs), "S")
                c.seq(scalar, scalar.activation(out=schb[0][:, :], in_=schb[0][:, :],
                                                func=AF.Exp, scale=-1.0), "S")
                h = scalar.activation(out=schb[1][:, :], in_=schb[0][:, :],
                                      func=AF.Ln, bias=1.0)
                c.inc("a0s_dlB", h, 2)

            @block.vector
            def _(vector, tt=tt, dc=dc, k=k):
                c.wait(vector, "a0s_dlB", 2 * (k + 1))
                if k >= 1:
                    c.wait(vector, "a0s_pb", 16 * k)
                c.seq(vector, vector.scalar_tensor_tensor(
                    out=delta[:, :], in0=vpb[:, :], scalar=0.0, in1=schb[1][:, :],
                    op0=OP.max, op1=OP.add), "V")
                c.seq(vector, vector.tensor_tensor_scan(
                    out=cum[:, :], data0=ones512[:, :], data1=delta[:, :],
                    initial=cdel[:, dc:dc + 1], op0=OP.mult, op1=OP.add), "V")
                h = vector.tensor_copy(out=cdel[:, dc:dc + 1], in_=cum[:, TT5 - 1:TT5])
                nn = c.inc("a0v_dl2B", h, 1)
                c.wait(vector, "a0v_dl2B", nn)

            for s in range(16):
                pidx = (16 * tt + dc) * 16 + s

                @block.scalar
                def _(scalar, tt=tt, dc=dc, s=s, pidx=pidx):
                    c.wait(scalar, "a0v_dl2B", 16 * tt + dc + 1)
                    if pidx >= 2:
                        c.wait(scalar, "a0v_pb", pidx - 1)
                    h = scalar.activation(
                        out=schb[s % 2][:, :], in_=cum[:, :], func=AF.Exp,
                        scale=As[:, dc * S0 + s:dc * S0 + s + 1])
                    c.inc("a0s_pb", h, 1)

                @block.vector
                def _(vector, tt=tt, dc=dc, s=s, pidx=pidx):
                    c.wait(vector, "a0s_pb", pidx + 1)
                    c.seq(vector, vector.tensor_mul(
                        out=gb[:, 0:TT5], in0=schb[s % 2][:, :], in1=cbav[:, s, :]), "V")
                    if s == 0:
                        h = vector.tensor_scalar(
                            out=ycor[:, :], in0=gb[:, 0:TT5],
                            scalar1=Hbuf[:, dc * S0 + s:dc * S0 + s + 1],
                            scalar2=None, op0=OP.mult)
                    else:
                        h = vector.scalar_tensor_tensor(
                            out=ycor[:, :], in0=gb[:, 0:TT5],
                            scalar=Hbuf[:, dc * S0 + s:dc * S0 + s + 1],
                            in1=ycor[:, :], op0=OP.mult, op1=OP.add)
                    nn = c.inc("a0v_pb", h, 1)
                    c.wait(vector, "a0v_pb", nn)

            tail_z_y2(tt, dc, "B")

        tail_wout_x(tt, "B")

    barrier(c, block, sems=["a0v_Bz", "a0v_Bo2", "a0t_Bo", "a0s_Bz", "a0v_pb",
                            "a0s_pb", "a0v_cbl", "a0t_cbl", "a0v_dl2B", "a0s_dlB"],
            release=["a0"])
    stP.close()


def ph_edge_exchange(c, block, x, chm, prevsel, ccs):
    nc = c.nc
    ccx_in, ccx_out = ccs
    st = ExitStack()
    ed = c.sb("ed_buf", [128, 8 * 3], F32, st)
    eda = c.sb("ed_all", [128, 4 * 24], F32, st)
    xv = x[:].rearrange("p (d t) -> p d t", d=8)

    @block.vector
    def _(vector):
        c.inc("edv", vector.tensor_copy(
            out=ed[:].rearrange("p (d t) -> p d t", d=8),
            in_=xv[:, :, TE - 3:TE]), 1)

    @block.sync
    def _(sync):
        c.wait(sync, "edv", 1)
        c.inc("edq", sync.dma_start(out=ccx_in[:, :], in_=ed[:]), 16)

    @block.gpsimd
    def _(gpsimd):
        c.wait(gpsimd, "edq", 16)
        c.inc("edg", gpsimd.collective_compute(
            "AllGather", OP.bypass, ins=[ccx_in], outs=[ccx_out],
            replica_groups=[[0, 1, 2, 3], [4, 5, 6, 7]]), 1)

    @block.sync
    def _(sync):
        c.wait(sync, "edg", 1)
        c.inc("edq2", sync.dma_start(
            out=eda[:].rearrange("p (j t) -> p j t", j=4),
            in_=ccx_out.rearrange("(j p) t -> j p t", p=128).transpose([1, 0, 2])), 16)

    @block.vector
    def _(vector):
        c.wait(vector, "edq2", 16)
        c.seq(vector, vector.tensor_scalar(
            out=ed[:, :], in0=eda[:, 0:24], scalar1=prevsel[:, 0:1],
            scalar2=None, op0=OP.mult), "V")
        for j in range(1, 4):
            c.seq(vector, vector.scalar_tensor_tensor(
                out=ed[:, :], in0=eda[:, j * 24:(j + 1) * 24],
                scalar=prevsel[:, j:j + 1], in1=ed[:, :],
                op0=OP.mult, op1=OP.add), "V")
        h = vector.tensor_copy(out=xv[:, :, 0:3],
                               in_=ed[:].rearrange("p (d t) -> p d t", d=8))
        nn = c.inc("edv2", h, 1)
        c.wait(vector, "edv2", nn)

    barrier(c, block, sems=["edv2"], release=["ed"])
    st.close()


def ph_ffn1(c, block, xn, g, wdram, ones128, ones_row, ones_rowf, ident, identf):
    nc = c.nc
    NT5 = TC // 512
    xnv = xn[:].rearrange("p (d t) -> p d t", d=8)
    gv = g[:].rearrange("p (d t) -> p d t", d=8)
    st = ExitStack()
    qk = c.sb("f1_qk", [128, 8 * 1024], BF16, st)
    qf = c.sb("f1_qf", [128, 4 * 1024], F32, st)
    qh = c.sb("f1_qh", [128, 4 * 1024], F32, st)
    at = c.sb("f1_a", [128, 8 * 16], BF16, st)
    bt = c.sb("f1_b", [16, 1024], BF16, st)
    bias = c.sb("f1_bias", [128, 8], F32, st)
    mx = c.sb("f1_mx", [128, 2], F32, st)
    mxr = c.sb("f1_mxr", [1, 128], F32, st)
    mxc = c.sb("f1_mxc", [128, 1], F32, st)
    u16 = c.sb("f1_u", [16, TC], BF16, st)
    ps_m = c.ps("f1_ps", [128, 2 * 512], F32, st)
    ps_u = c.ps("f1_psu", [16, 512], F32, st)
    ps_t = c.ps("f1_pst", [1, 128], F32, st)
    mxf = c.sb("f1_mxf", [128, 1], F32, st)

    @block.gpsimd
    def _(gpsimd):
        c.inc("f1g", gpsimd.dma_start(
            out=at[:].rearrange("p (k r) -> p k r", k=8), in_=wt_kpf(wdram["Q_a"])), 16)
        c.inc("f1g", gpsimd.dma_start(out=bt[:], in_=wdram["Q_b"]), 16)
        c.inc("f1g", gpsimd.dma_start(
            out=bias[:], in_=wdram["Q_bias"].rearrange("(k p) -> k p", p=128)
            .transpose([1, 0]), allow_slow_non_contiguous=True), 16)

    for hh in range(2):
        @block.gpsimd
        def _(gpsimd, hh=hh):
            if hh == 1:
                c.wait(gpsimd, "f1v_mx", 1)
            c.inc("f1g_q", gpsimd.dma_start(
                out=qf[:].rearrange("p (k f) -> p k f", k=4),
                in_=wt_kpf(wdram["Q_kernel"])[:, hh * 4:(hh + 1) * 4, :]), 16)

        @block.vector
        def _(vector, hh=hh):
            c.wait(vector, "f1g_q", 16 * (hh + 1))
            h = vector.tensor_reduce(out=mx[:, hh:hh + 1], in_=qf[:, :],
                                     axis=AX.X, op=OP.max, apply_absolute_value=True)
            c.inc("f1v_mx", h, 1)

    @block.vector
    def _(vector):
        c.wait(vector, "f1v_mx", 2)
        c.seq(vector, vector.tensor_reduce(out=mx[:, 0:1], in_=mx[:, 0:2],
                                           axis=AX.X, op=OP.max), "V")
        c.inc("f1v_mx2", vector.tensor_copy(out=mxf[:, 0:1], in_=mx[:, 0:1]), 1)

    @block.tensor
    def _(tensor):
        c.wait(tensor, "f1v_mx2", 1)
        c.inc("f1t_mx", tensor.transpose(ps_t[0:1, 0:128], mxf[:, 0:1], identf[:]), 1)

    @block.vector
    def _(vector):
        c.wait(vector, "f1t_mx", 1)
        c.seq(vector, vector.tensor_copy(out=mxr[0:1, :], in_=ps_t[0:1, 0:128]), "V")
        c.seq(vector, vector.tensor_reduce(out=mx[0:1, 0:1], in_=mxr[0:1, :],
                                           axis=AX.X, op=OP.max), "V")
        c.inc("f1v_mx3", vector.engine_nop(), 1)

    @block.tensor
    def _(tensor):
        c.wait(tensor, "f1v_mx3", 1)
        c.inc("f1t_bc", tensor.matmul(ps_m[:, 0:1], ones_rowf[0:1, :],
                                      mx[0:1, 0:1], start=True, stop=True), 1)

    @block.vector
    def _(vector):
        c.wait(vector, "f1t_bc", 1)
        c.seq(vector, vector.tensor_scalar_mul(out=mxc[:, 0:1], in0=ps_m[:, 0:1],
                                               scalar1=1.0 / 7.0), "V")
        c.seq(vector, vector.reciprocal(out=mx[:, 0:1], in_=mxc[:, 0:1]), "V")
        c.inc("f1v_sc", vector.engine_nop(), 1)

    for hh in range(2):
        @block.gpsimd
        def _(gpsimd, hh=hh):
            c.wait(gpsimd, "f1v_mx", 2)
            if hh == 1:
                c.wait(gpsimd, "f1v_qz", 1)
            c.inc("f1g_q2", gpsimd.dma_start(
                out=qf[:].rearrange("p (k f) -> p k f", k=4),
                in_=wt_kpf(wdram["Q_kernel"])[:, hh * 4:(hh + 1) * 4, :]), 16)

        @block.vector
        def _(vector, hh=hh):
            c.wait(vector, "f1g_q2", 16 * (hh + 1))
            c.wait(vector, "f1v_sc", 1)
            c.seq(vector, vector.tensor_scalar(
                out=qf[:, :], in0=qf[:, :], scalar1=mx[:, 0:1], scalar2=None,
                op0=OP.mult), "V")
            c.seq(vector, vector.tensor_single_scalar(
                out=qh[:, :], in_=qf[:, :], scalar=-7.5, op=OP.is_ge), "V")
            for kk in range(-6, 8):
                c.seq(vector, vector.scalar_tensor_tensor(
                    out=qh[:, :], in0=qf[:, :], scalar=float(kk) - 0.5,
                    in1=qh[:, :], op0=OP.is_ge, op1=OP.add), "V")
            c.seq(vector, vector.tensor_scalar(
                out=qf[:, :], in0=qh[:, :], scalar1=-8.0, scalar2=None,
                op0=OP.add), "V")
            h = vector.tensor_scalar(
                out=qk[:, hh * 4096:(hh + 1) * 4096], in0=qf[:, :],
                scalar1=mxc[:, 0:1], scalar2=None, op0=OP.mult)
            nn = c.inc("f1v_qz", h, 1)
            c.wait(vector, "f1v_qz", nn)

    for tg in range(NT5):
        @block.tensor
        def _(tensor, tg=tg):
            c.wait(tensor, "f1g", 48)
            if tg >= 1:
                c.wait(tensor, "f1v_u", tg)
            for kc in range(8):
                h = tensor.matmul(ps_u[:, :], at[:, kc * 16:(kc + 1) * 16],
                                  xnv[:, kc, 3 + tg * 512:3 + (tg + 1) * 512],
                                  start=(kc == 0), stop=(kc == 7))
            c.inc("f1t_u", h, 1)

        @block.vector
        def _(vector, tg=tg):
            c.wait(vector, "f1t_u", tg + 1)
            c.inc("f1v_u", vector.tensor_copy(
                out=u16[:, tg * 512:(tg + 1) * 512], in_=ps_u[:, :]), 1)

    @block.scalar
    def _(scalar):
        c.table(scalar, SILU)
        c.inc("f1s0", scalar.nop(), 1)

    for oc in range(8):
        for tg in range(NT5):
            k = NT5 * oc + tg
            slot = (k % 2) * 512

            @block.tensor
            def _(tensor, oc=oc, tg=tg, k=k, slot=slot):
                c.wait(tensor, "f1v_qz", 2)
                c.wait(tensor, "f1v_u", NT5)
                if k >= 2:
                    c.wait(tensor, "f1s_g", k - 1)
                for kc in range(8):
                    tensor.matmul(ps_m[:, slot:slot + 512],
                                  qk[:, (kc * 8 + oc) * 128:(kc * 8 + oc) * 128 + 128],
                                  xnv[:, kc, 3 + tg * 512:3 + (tg + 1) * 512],
                                  start=(kc == 0), stop=False)
                h = tensor.matmul(ps_m[:, slot:slot + 512],
                                  bt[:, oc * 128:(oc + 1) * 128],
                                  u16[:, tg * 512:(tg + 1) * 512],
                                  start=False, stop=True)
                c.inc("f1t_g", h, 1)

            @block.scalar
            def _(scalar, oc=oc, tg=tg, k=k, slot=slot):
                c.wait(scalar, "f1t_g", k + 1)
                c.wait(scalar, "f1s0", 1)
                h = scalar.activation(out=gv[:, oc, tg * 512:(tg + 1) * 512],
                                      in_=ps_m[:, slot:slot + 512], func=AF.Silu,
                                      bias=bias[:, oc:oc + 1])
                c.inc("f1s_g", h, 1)

    barrier(c, block, sems=["f1s_g", "f1t_g", "f1v_qz", "f1v_u", "f1v_sc",
                            "f1t_u", "f1g", "f1s0"], release=["f1"])
    st.close()


def ph_attn1(c, block, x, xn, g, wdram, ones128, ones_row, ident, sel32_in,
             sel128_in, chm, ccs):
    nc = c.nc
    TT5 = 512
    NTT = TC // TT5
    S1L = 64
    xv = x[:].rearrange("p (d t) -> p d t", d=8)
    xnv = xn[:].rearrange("p (d t) -> p d t", d=8)
    gv = g[:].rearrange("p (d t) -> p d t", d=8)
    cc1_in, cc1_out = ccs

    stP = ExitStack()
    sel32 = c.sb("a1_sel32", [32, 16 * 128], F32, stP)[:]
    sel128 = c.sb("a1_sel128", [128, 64 * 128], BF16, stP)[:]
    Arep = c.sb("a1_Arep", [128, 16 * S1L], F32, stP)
    Atmp = c.sb("a1_Atmp", [32, S1L], F32, stP)
    Dsrc = c.sb("a1_Dsrc", [32, 1], F32, stP)
    Drep = c.sb("a1_Drep", [128, 16], F32, stP)
    dtb = c.sb("a1_dtb", [32, 1], F32, stP)
    dtw = c.sb("a1_dtw", [128, 8 * 32], BF16, stP)
    convw = c.sb("a1_convw", [128, 16 * 4], F32, stP)
    convb = c.sb("a1_convb", [128, 16], F32, stP)
    carry = c.sb("a1_carry", [128, 16 * S1L], BF16, stP)
    dsacc = c.sb("a1_dsacc", [32, 4], F32, stP)
    dsum32 = c.sb("a1_dsum32", [32, 1], F32, stP)
    cdel32 = c.sb("a1_cdel32", [32, 1], F32, stP)
    dtt = c.sb("a1_dtt", [32, TT5], F32, stP)
    dts1 = c.sb("a1_dts1", [32, TT5], F32, stP)
    dts2 = c.sb("a1_dts2", [32, TT5], F32, stP)
    y2t = c.sb("a1_y2", [128, 16 * TT5], BF16, stP)
    winb = [c.sb(f"a1_win{i}", [128, 8 * 128], BF16, stP) for i in range(2)]
    woutb = [c.sb("a1_wout", [128, 16 * 128], BF16, stP)] * 2
    y2v = y2t[:].rearrange("p (k t) -> p k t", k=16)

    stPA = ExitStack()
    bcw = [c.sb("a1_bcw", [128, 8 * 256], BF16, stPA)] * 2
    xh = c.sb("a1_xh", [128, TT5 + 3], BF16, stPA)
    ctx3 = c.sb("a1_ctx", [128, 16 * 3], BF16, stPA)
    Bt = c.sb("a1_Bt", [128, TT5], BF16, stPA)
    Ct = c.sb("a1_Ct", [128, TT5], BF16, stPA)
    drep = c.sb("a1_drep", [128, TT5], F32, stPA)
    wdc = c.sb("a1_wdc", [128, TT5], BF16, stPA)
    sch1 = c.sb("a1_sch1", [128, TT5], F32, stPA)
    vpb = c.sb("a1_vpb", [128, TT5], F32, stPA)
    dab = [c.sb(f"a1_dab{i}", [128, 2 * (TT5 + 1)], F32, stPA) for i in range(2)]
    ubb = c.sb("a1_ub", [128, 4 * (TT5 + 1)], BF16, stPA)
    hbb = c.sb("a1_hb", [128, 4 * (TT5 + 1)], BF16, stPA)
    t1b = c.sb("a1_t1", [128, 2 * TT5], BF16, stPA)
    ydc = c.sb("a1_ydc", [128, TT5], BF16, stPA)
    gb = c.sb("a1_gb", [128, 2 * TT5], BF16, stPA)
    zs = c.sb("a1_zs", [128, TT5], BF16, stPA)
    zex = c.sb("a1_zex", [128, TT5], F32, stPA)

    ps_a = c.ps("a1_ps_a", [128, 2 * TT5], F32, stP)
    ps_bc = c.ps("a1_ps_bc", [128, 2 * TT5], F32, stP)
    ps_c = c.ps("a1_ps_c", [128, 2 * TT5], F32, stP)
    ps_dr = c.ps("a1_ps_dr", [128, TT5], F32, stP)
    ps_K = c.ps("a1_ps_K", [32, TT5], F32, stP)

    WIN = wdram["L1_W_inT"]
    WOUT = wdram["L1_W_outT"]

    @block.gpsimd
    def _(gpsimd):
        c.inc("a1g", gpsimd.dma_start(out=sel32, in_=sel32_in), 16)
        c.inc("a1g", gpsimd.dma_start(out=sel128, in_=sel128_in), 16)
        c.inc("a1g", gpsimd.dma_start(out=Atmp[:], in_=wdram["L1_A_log"]), 16)
        c.inc("a1g", gpsimd.dma_start(
            out=Dsrc[:], in_=wdram["L1_D"].rearrange("(n o) -> n o", o=1)), 16)
        c.inc("a1g", gpsimd.dma_start(
            out=dtb[:], in_=wdram["L1_dt_bias"].rearrange("(n o) -> n o", o=1)), 16)
        c.inc("a1g", gpsimd.dma_start(out=convw[:].rearrange("p (k s) -> p k s", k=16),
                                      in_=wt_kpf(wdram["L1_conv_w"])), 16)
        c.inc("a1g", gpsimd.dma_start(
            out=convb[:], in_=wdram["L1_conv_b"].rearrange("(k p) -> k p", p=128)
            .transpose([1, 0]), allow_slow_non_contiguous=True), 16)
        for kc in range(8):
            c.inc("a1g", gpsimd.dma_start(
                out=dtw[:, kc * 32:(kc + 1) * 32],
                in_=wt_kpf(WIN)[:, kc, 4096:8224].rearrange(
                    "p (n r) -> p n r", r=129)[:, :, 128],
                allow_slow_non_contiguous=True), 16)
        gpsimd.memset(carry[:], 0.0)
        gpsimd.memset(cdel32[:], 0.0)
        gpsimd.memset(dab[0][:], 0.0)
        c.inc("a1g2", gpsimd.memset(dab[1][:], 0.0), 1)

    @block.scalar
    def _(scalar):
        c.wait(scalar, "a1g", c.val("a1g"))
        c.wait(scalar, "a1g2", 1)
        c.table(scalar, NLEXP)
        c.inc("a1s0", scalar.activation(out=Atmp[:], in_=Atmp[:], func=AF.Exp), 1)

    @block.vector
    def _(vector):
        c.wait(vector, "a1s0", 1)
        c.seq(vector, vector.tensor_scalar_mul(out=Atmp[:], in0=Atmp[:],
                                               scalar1=-1.0), "V")
        c.inc("a1v0", vector.tensor_scalar_mul(out=convb[:], in0=convb[:],
                                               scalar1=-1.0), 1)

    for dc in range(16):
        @block.tensor
        def _(tensor, dc=dc):
            c.wait(tensor, "a1v0", 1)
            if dc >= 1:
                c.wait(tensor, "a1v_ar", dc)
            tensor.matmul(ps_dr[:, 0:S1L], sel32[:, dc * 128:(dc + 1) * 128],
                          Atmp[:], start=True, stop=True)
            c.inc("a1t_ar", tensor.matmul(
                ps_dr[:, S1L:S1L + 1], sel32[:, dc * 128:(dc + 1) * 128],
                Dsrc[:], start=True, stop=True), 1)

        @block.vector
        def _(vector, dc=dc):
            c.wait(vector, "a1t_ar", dc + 1)
            c.seq(vector, vector.tensor_copy(
                out=Arep[:, dc * S1L:(dc + 1) * S1L], in_=ps_dr[:, 0:S1L]), "V")
            h = vector.tensor_copy(out=Drep[:, dc:dc + 1],
                                   in_=ps_dr[:, S1L:S1L + 1])
            nn = c.inc("a1v_ar", h, 1)
            c.wait(vector, "a1v_ar", nn)

    def tail_z_y2(tt, dc, M):
        tb = 3 + tt * TT5
        k = 16 * tt + dc

        @block.gpsimd
        def _(gpsimd, tt=tt, dc=dc, k=k):
            if M == "A":
                c.wait(gpsimd, "a1t_pr", k + 1)
            if k >= 2:
                c.wait(gpsimd, f"a1t_{M}z", k - 1)
            c.inc(f"a1g_{M}z{dc % 2}", gpsimd.dma_start(
                out=winb[dc % 2][:].rearrange("p (k f) -> p k f", k=8),
                in_=wt_kpf(WIN)[:, :, 2048 + dc * 128:2048 + (dc + 1) * 128]), 16)

        @block.tensor
        def _(tensor, tt=tt, dc=dc, k=k):
            c.wait(tensor, f"a1g_{M}z{dc % 2}", 16 * (k // 2 + 1))
            if M == "A":
                c.wait(tensor, "a1v_yD", k + 1)
            else:
                c.wait(tensor, "a1v_pb", S1L // 2 * (k + 1))
            if k >= 2:
                c.wait(tensor, f"a1v_{M}z", k - 1)
            for kc in range(8):
                h = tensor.matmul(ps_a[:, TT5:2 * TT5],
                                  winb[dc % 2][:, kc * 128:(kc + 1) * 128],
                                  xnv[:, kc, tb:tb + TT5],
                                  start=(kc == 0), stop=(kc == 7))
            c.inc(f"a1t_{M}z", h, 1)

        @block.scalar
        def _(scalar, tt=tt, dc=dc, k=k):
            c.wait(scalar, f"a1t_{M}z", k + 1)
            if k >= 1:
                c.wait(scalar, f"a1v_{M}z", k)
            c.inc(f"a1s_{M}z", scalar.activation(
                out=zex[:, :], in_=ps_a[:, TT5:2 * TT5], func=AF.Exp,
                scale=-1.0), 1)

        @block.vector
        def _(vector, tt=tt, dc=dc, k=k):
            c.wait(vector, f"a1s_{M}z", k + 1)
            c.seq(vector, vector.tensor_copy(out=zs[:, :],
                                             in_=ps_a[:, TT5:2 * TT5]), "V")
            c.seq(vector, vector.tensor_scalar(
                out=zex[:, :], in0=zex[:, :], scalar1=1.0, scalar2=None,
                op0=OP.add), "V")
            c.seq(vector, vector.reciprocal(out=zex[:, :], in_=zex[:, :]), "V")
            c.seq(vector, vector.tensor_mul(out=zs[:, :], in0=zs[:, :],
                                            in1=zex[:, :]), "V")
            if M == "A":
                h = vector.tensor_mul(out=y2v[:, dc, :], in0=ydc[:, :], in1=zs[:, :])
            else:
                h = vector.tensor_mul(out=y2v[:, dc, :], in0=gb[:, 0:TT5],
                                      in1=zs[:, :])
            nn = c.inc(f"a1v_{M}z", h, 1)
            c.wait(vector, f"a1v_{M}z", nn)

    def tail_wout_x(tt, M):
        for oc in range(8):
            k = 8 * tt + oc

            @block.gpsimd
            def _(gpsimd, tt=tt, oc=oc, k=k):
                if k >= 1:
                    c.wait(gpsimd, f"a1t_{M}o", k)
                c.inc(f"a1g_{M}o{oc % 2}", gpsimd.dma_start(
                    out=woutb[oc % 2][:].rearrange("p (k f) -> p k f", k=16),
                    in_=wt_kpf(WOUT)[:, :, oc * 128:(oc + 1) * 128]), 16)

            @block.tensor
            def _(tensor, tt=tt, oc=oc, k=k):
                c.wait(tensor, f"a1g_{M}o{oc % 2}", 16 * (k // 2 + 1))
                c.wait(tensor, f"a1v_{M}z", 16 * (tt + 1))
                if k >= 2:
                    c.wait(tensor, f"a1v_{M}o2", k - 1)
                slot = (oc % 2) * TT5
                for dc in range(16):
                    h = tensor.matmul(ps_c[:, slot:slot + TT5],
                                      woutb[oc % 2][:, dc * 128:(dc + 1) * 128],
                                      y2v[:, dc, :],
                                      start=(dc == 0), stop=(dc == 15))
                c.inc(f"a1t_{M}o", h, 1)

            @block.vector
            def _(vector, tt=tt, oc=oc, k=k):
                c.wait(vector, f"a1t_{M}o", k + 1)
                slot = (oc % 2) * TT5
                tb = 3 + tt * TT5
                c.seq(vector, vector.tensor_mul(
                    out=gb[:, TT5:2 * TT5],
                    in0=gv[:, oc, tt * TT5:(tt + 1) * TT5],
                    in1=ps_c[:, slot:slot + TT5]), "V")
                h = vector.tensor_add(out=xv[:, oc, tb:tb + TT5],
                                      in0=xv[:, oc, tb:tb + TT5],
                                      in1=gb[:, TT5:2 * TT5])
                nn = c.inc(f"a1v_{M}o2", h, 1)
                c.wait(vector, f"a1v_{M}o2", nn)

    def dt_block(tt, M):
        tb = 3 + tt * TT5
        ti = (0 if M == "A" else NTT) + tt

        @block.tensor
        def _(tensor, tt=tt, ti=ti):
            if ti == 0:
                c.wait(tensor, "a1v_ar", 16)
            else:
                c.wait(tensor, f"a1v_dt", ti)
            for kc in range(8):
                h = tensor.matmul(ps_K[:, :], dtw[:, kc * 32:(kc + 1) * 32],
                                  xnv[:, kc, tb:tb + TT5],
                                  start=(kc == 0), stop=(kc == 7))
            c.inc("a1t_dt", h, 1)

        @block.scalar
        def _(scalar, tt=tt, ti=ti):
            c.wait(scalar, "a1t_dt", ti + 1)
            if ti >= 1:
                c.wait(scalar, "a1v_dt", ti)
            c.seq(scalar, scalar.activation(out=dts1[:, :], in_=ps_K[:, :],
                                            func=AF.Abs, bias=dtb[:, 0:1]), "S")
            c.seq(scalar, scalar.activation(out=dts1[:, :], in_=dts1[:, :],
                                            func=AF.Exp, scale=-1.0), "S")
            h = scalar.activation(out=dts1[:, :], in_=dts1[:, :],
                                  func=AF.Ln, bias=1.0)
            c.inc("a1s_dt", h, 1)

        @block.vector
        def _(vector, tt=tt, ti=ti, M=M):
            c.wait(vector, "a1s_dt", ti + 1)
            if M == "A" and tt >= 1:
                c.wait(vector, "a1t_pr", 16 * tt)
            if M == "B" and tt == 0:
                c.wait(vector, "a1g_o5", 1)
            if M == "B" and tt >= 1:
                c.wait(vector, "a1t_prB", 16 * tt)
            c.seq(vector, vector.tensor_scalar(
                out=dts2[:, :], in0=ps_K[:, :], scalar1=dtb[:, 0:1],
                scalar2=None, op0=OP.add), "V")
            if M == "A":
                h = vector.scalar_tensor_tensor(
                    out=dtt[:, :], in0=dts2[:, :], scalar=0.0, in1=dts1[:, :],
                    op0=OP.max, op1=OP.add,
                    accum_out=dsacc[:, tt:tt + 1])
            else:
                c.seq(vector, vector.scalar_tensor_tensor(
                    out=dtt[:, :], in0=dts2[:, :], scalar=0.0, in1=dts1[:, :],
                    op0=OP.max, op1=OP.add), "V")
                c.seq(vector, vector.tensor_tensor_scan(
                    out=cum32[:, :], data0=ones512[0:32, :], data1=dtt[:, :],
                    initial=cdel32[:, 0:1], op0=OP.mult, op1=OP.add), "V")
                h = vector.tensor_copy(out=cdel32[:, 0:1],
                                       in_=cum32[:, TT5 - 1:TT5])
            nn = c.inc("a1v_dt", h, 1)
            c.wait(vector, "a1v_dt", nn)

    for tt in range(NTT):
        tb = 3 + tt * TT5
        dt_block(tt, "A")

        for dc in range(16):
            k = 16 * tt + dc

            @block.gpsimd
            def _(gpsimd, tt=tt, dc=dc, k=k):
                if k >= 1:
                    c.wait(gpsimd, "a1t_pr", k)
                if k >= 2:
                    c.wait(gpsimd, "a1t_Az", k - 1)
                c.inc(f"a1g_x{dc % 2}", gpsimd.dma_start(
                    out=winb[dc % 2][:].rearrange("p (k f) -> p k f", k=8),
                    in_=wt_kpf(WIN)[:, :, dc * 128:(dc + 1) * 128]), 16)
                for q in range(2):
                    n = 2 * dc + q
                    c.inc(f"a1g_b{dc % 2}", gpsimd.dma_start(
                        out=bcw[dc % 2][:].rearrange("p (k f) -> p k f", k=8)
                        [:, :, q * 128:(q + 1) * 128],
                        in_=wt_kpf(WIN)[:, :, 4096 + n * 129:4096 + n * 129 + 128]), 16)

            @block.tensor
            def _(tensor, tt=tt, dc=dc, k=k):
                c.wait(tensor, f"a1g_x{dc % 2}", 16 * (k // 2 + 1))
                c.wait(tensor, f"a1g_b{dc % 2}", 32 * (k // 2 + 1))
                c.wait(tensor, "a1v_dt", tt + 1)
                if k >= 1:
                    c.wait(tensor, "a1v_pr", k)
                if k >= 1:
                    c.wait(tensor, "a1v_scan", 16 * k)
                if tt >= 1:
                    c.wait(tensor, "a1v_Ao2", 8 * tt)
                for kc in range(8):
                    tensor.matmul(ps_a[:, 0:TT5],
                                  winb[dc % 2][:, kc * 128:(kc + 1) * 128],
                                  xnv[:, kc, tb:tb + TT5],
                                  start=(kc == 0), stop=(kc == 7))
                if tt == 0:
                    for kc in range(8):
                        tensor.matmul(ps_dr[:, 0:3],
                                      winb[dc % 2][:, kc * 128:(kc + 1) * 128],
                                      xnv[:, kc, 0:3],
                                      start=(kc == 0), stop=(kc == 7))
                for q in range(2):
                    for kc in range(8):
                        tensor.matmul(ps_bc[:, q * TT5:(q + 1) * TT5],
                                      bcw[dc % 2][:, kc * 256 + q * 128:
                                                  kc * 256 + (q + 1) * 128],
                                      xnv[:, kc, tb:tb + TT5],
                                      start=(kc == 0), stop=(kc == 7))
                h = tensor.matmul(ps_c[:, 0:TT5],
                                  sel32[:, dc * 128:(dc + 1) * 128],
                                  dtt[:, :], start=True, stop=True)
                c.inc("a1t_pr", h, 1)

            @block.vector
            def _(vector, tt=tt, dc=dc, k=k):
                c.wait(vector, "a1t_pr", k + 1)
                if k >= 1:
                    c.wait(vector, "a1t_C", 16 * k)
                    c.wait(vector, "a1s_da", 32 * k)
                if tt == 0:
                    c.seq(vector, vector.tensor_copy(out=xh[:, 0:3],
                                                     in_=ps_dr[:, 0:3]), "V")
                else:
                    c.seq(vector, vector.tensor_copy(
                        out=xh[:, 0:3],
                        in_=ctx3[:].rearrange("p (k t) -> p k t", k=16)[:, dc, :]), "V")
                c.seq(vector, vector.tensor_copy(out=xh[:, 3:3 + TT5],
                                                 in_=ps_a[:, 0:TT5]), "V")
                c.seq(vector, vector.tensor_copy(
                    out=ctx3[:].rearrange("p (k t) -> p k t", k=16)[:, dc, :],
                    in_=xh[:, TT5:TT5 + 3]), "V")
                for q in range(2):
                    c.seq(vector, vector.tensor_copy(
                        out=Bt[q * 64:(q + 1) * 64, :],
                        in_=ps_bc[0:64, q * TT5:(q + 1) * TT5]), "V")
                    c.seq(vector, vector.tensor_copy(
                        out=Ct[q * 64:(q + 1) * 64, :],
                        in_=ps_bc[64:128, q * TT5:(q + 1) * TT5]), "V")
                h = vector.tensor_copy(out=drep[:, :], in_=ps_c[:, 0:TT5])
                nn = c.inc("a1v_pr", h, 1)
                c.wait(vector, "a1v_pr", nn)

            @block.vector
            def _(vector, tt=tt, dc=dc, k=k):
                if k >= 1:
                    c.wait(vector, "a1s_cv", k)
                c.seq(vector, vector.tensor_scalar(
                    out=sch1[:, :], in0=xh[:, 0:TT5],
                    scalar1=convw[:, dc * 4:dc * 4 + 1], scalar2=None,
                    op0=OP.mult), "V")
                for j in range(1, 4):
                    h = vector.scalar_tensor_tensor(
                        out=sch1[:, :], in0=xh[:, j:j + TT5],
                        scalar=convw[:, dc * 4 + j:dc * 4 + j + 1],
                        in1=sch1[:, :], op0=OP.mult, op1=OP.add)
                    if j < 3:
                        c.seq(vector, h, "V")
                c.inc("a1v_cv", h, 1)

            @block.scalar
            def _(scalar, tt=tt, dc=dc, k=k):
                c.wait(scalar, "a1v_cv", k + 1)
                if k >= 1:
                    c.wait(scalar, "a1v_cv2", k)
                h = scalar.activation(out=vpb[:, :], in_=sch1[:, :],
                                      func=AF.Exp, scale=-1.0,
                                      bias=convb[:, dc:dc + 1])
                c.inc("a1s_cv", h, 1)

            @block.vector
            def _(vector, tt=tt, dc=dc, k=k):
                c.wait(vector, "a1s_cv", k + 1)
                c.seq(vector, vector.tensor_scalar(
                    out=vpb[:, :], in0=vpb[:, :], scalar1=1.0, scalar2=None,
                    op0=OP.add), "V")
                c.seq(vector, vector.reciprocal(out=vpb[:, :], in_=vpb[:, :]), "V")
                c.seq(vector, vector.tensor_scalar(
                    out=gb[:, 0:TT5], in0=sch1[:, :],
                    scalar1=convb[:, dc:dc + 1], scalar2=None, op0=OP.subtract), "V")
                c.seq(vector, vector.tensor_mul(out=xh[:, 3:3 + TT5],
                                                in0=gb[:, 0:TT5], in1=vpb[:, :]), "V")
                h = vector.tensor_mul(out=wdc[:, :], in0=drep[:, :],
                                      in1=xh[:, 3:3 + TT5])
                nn = c.inc("a1v_cv2", h, 1)
                c.wait(vector, "a1v_cv2", nn)

            for sb in range(16):
                bidx = (16 * tt + dc) * 16 + sb

                @block.tensor
                def _(tensor, tt=tt, dc=dc, sb=sb, bidx=bidx):
                    c.wait(tensor, "a1v_cv2", 16 * tt + dc + 1)
                    if bidx >= 1:
                        c.wait(tensor, "a1v_u", 2 * bidx)
                        c.wait(tensor, "a1v_gb", bidx)
                    if tt >= 1:
                        c.wait(tensor, "a1v_Ao2", 8 * tt)
                    for q4 in range(4):
                        s = 4 * sb + q4
                        dst = ps_bc if q4 < 2 else ps_c
                        h = tensor.matmul(dst[:, (q4 % 2) * TT5:(q4 % 2 + 1) * TT5],
                                          sel128[:, s * 128:(s + 1) * 128],
                                          Bt[:, :], start=True, stop=True)
                    c.inc("a1t_B", h, 1)

                @block.tensor
                def _(tensor, tt=tt, dc=dc, sb=sb, bidx=bidx):
                    c.wait(tensor, "a1v_u", 2 * (bidx + 1))
                    for q4 in range(4):
                        s = 4 * sb + q4
                        dst = ps_bc if q4 < 2 else ps_c
                        h = tensor.matmul(dst[:, (q4 % 2) * TT5:(q4 % 2 + 1) * TT5],
                                          sel128[:, s * 128:(s + 1) * 128],
                                          Ct[:, :], start=True, stop=True)
                    c.inc("a1t_C", h, 1)

                @block.scalar
                def _(scalar, tt=tt, dc=dc, sb=sb, bidx=bidx):
                    c.wait(scalar, "a1v_cv2", 16 * tt + dc + 1)
                    if bidx >= 1:
                        c.wait(scalar, "a1v_sc2", bidx)
                    for half in range(2):
                        dav = dab[half][:].rearrange("p (a b) -> p a b", a=2)
                        for a in range(2):
                            s = 4 * sb + 2 * half + a
                            h = scalar.activation(
                                out=dav[:, a, 1:1 + TT5], in_=drep[:, :],
                                func=AF.Exp,
                                scale=Arep[:, dc * S1L + s:dc * S1L + s + 1])
                            if a == 0:
                                c.seq(scalar, h, "S")
                        c.inc("a1s_da", h, 1)

                @block.vector
                def _(vector, tt=tt, dc=dc, sb=sb, bidx=bidx):
                    c.wait(vector, "a1t_B", bidx + 1)
                    ubv = ubb[:].rearrange("p (a b) -> p a b", a=4)
                    h = vector.tensor_tensor(
                        out=ubv[:, 0:2, 1:1 + TT5],
                        in0=wdc[:].unsqueeze(1).broadcast_to((128, 2, TT5)),
                        in1=ps_bc[:].rearrange("p (a b) -> p a b", a=2),
                        op=OP.mult)
                    c.inc("a1v_u", h, 1)
                    h = vector.tensor_tensor(
                        out=ubv[:, 2:4, 1:1 + TT5],
                        in0=wdc[:].unsqueeze(1).broadcast_to((128, 2, TT5)),
                        in1=ps_c[:].rearrange("p (a b) -> p a b", a=2),
                        op=OP.mult)
                    c.inc("a1v_u", h, 1)
                    c.seq(vector, vector.tensor_copy(
                        out=ubv[:, :, 0:1],
                        in_=carry[:, dc * S1L + 4 * sb:dc * S1L + 4 * sb + 4]
                        .rearrange("p (a b) -> p a b", a=4)), "V")
                    c.wait(vector, "a1s_da", 2 * bidx + 1)
                    c.seq(vector, vector.tensor_tensor_scan(
                        out=hbb[:, 0:2 * (TT5 + 1)], data0=dab[0][:, :],
                        data1=ubb[:, 0:2 * (TT5 + 1)],
                        initial=0.0, op0=OP.mult, op1=OP.add), "V")
                    c.wait(vector, "a1s_da", 2 * bidx + 2)
                    h = vector.tensor_tensor_scan(
                        out=hbb[:, 2 * (TT5 + 1):4 * (TT5 + 1)], data0=dab[1][:, :],
                        data1=ubb[:, 2 * (TT5 + 1):4 * (TT5 + 1)],
                        initial=0.0, op0=OP.mult, op1=OP.add)
                    c.inc("a1v_sc2", h, 1)
                    hbv = hbb[:].rearrange("p (a b) -> p a b", a=4)
                    c.wait(vector, "a1t_C", bidx + 1)
                    c.seq(vector, vector.tensor_tensor(
                        out=gb[:].rearrange("p (a b) -> p a b", a=2),
                        in0=hbv[:, 0:2, 1:1 + TT5],
                        in1=ps_bc[:].rearrange("p (a b) -> p a b", a=2),
                        op=OP.mult), "V")
                    c.seq(vector, vector.tensor_tensor(
                        out=t1b[:].rearrange("p (a b) -> p a b", a=2),
                        in0=hbv[:, 2:4, 1:1 + TT5],
                        in1=ps_c[:].rearrange("p (a b) -> p a b", a=2),
                        op=OP.mult), "V")
                    h = vector.tensor_add(out=t1b[:, :], in0=t1b[:, :],
                                          in1=gb[:, 0:2 * TT5])
                    c.inc("a1v_gb", h, 1)
                    c.seq(vector, vector.tensor_copy(
                        out=carry[:, dc * S1L + 4 * sb:dc * S1L + 4 * sb + 4]
                        .rearrange("p (a b) -> p a b", a=4),
                        in_=hbv[:, :, TT5:TT5 + 1]), "V")
                    if sb == 0:
                        h = vector.tensor_add(out=ydc[:, :], in0=t1b[:, 0:TT5],
                                              in1=t1b[:, TT5:2 * TT5])
                    else:
                        c.seq(vector, vector.tensor_add(
                            out=ydc[:, :], in0=ydc[:, :], in1=t1b[:, 0:TT5]), "V")
                        h = vector.tensor_add(out=ydc[:, :], in0=ydc[:, :],
                                              in1=t1b[:, TT5:2 * TT5])
                    nn = c.inc("a1v_scan", h, 1)
                    c.wait(vector, "a1v_scan", nn)

            @block.vector
            def _(vector, tt=tt, dc=dc, k=k):
                c.wait(vector, "a1v_scan", 16 * (k + 1))
                h = vector.scalar_tensor_tensor(
                    out=ydc[:, :], in0=xh[:, 3:3 + TT5],
                    scalar=Drep[:, dc:dc + 1], in1=ydc[:, :],
                    op0=OP.mult, op1=OP.add)
                nn = c.inc("a1v_yD", h, 1)
                c.wait(vector, "a1v_yD", nn)

            tail_z_y2(tt, dc, "A")

        tail_wout_x(tt, "A")

    barrier(c, block, sems=["a1v_Az", "a1v_Ao2", "a1t_Ao", "a1s_Az", "a1v_scan",
                            "a1s_da", "a1t_B", "a1t_C", "a1v_u", "a1v_gb",
                            "a1v_sc2", "a1s_cv", "a1v_cv",
                            "a1v_cv2", "a1t_pr", "a1v_pr", "a1v_yD", "a1v_dt",
                            "a1s_dt", "a1t_dt"])
    stPA.close()
    Hbuf = c.sb("a1_H", [128, 16 * S1L], F32, stP)
    cum32 = c.sb("a1_cum32", [32, TT5], F32, stP)
    ones512 = c.sb("a1_ones512", [32, TT5], BF16, stP)

    @block.gpsimd
    def _(gpsimd):
        c.inc("a1g_o5", gpsimd.memset(ones512[:], 1.0), 1)

    stE = ExitStack()
    Pacc = c.sb("a1_Pacc", [128, 16 * S1L], F32, stE)
    Prun = c.sb("a1_Prun", [128, 16 * S1L], F32, stE)
    tcmb = c.sb("a1_tcmb", [128, 16 * S1L], F32, stE)
    agj = c.sb("a1_agj", [128, 2 * 16 * S1L], F32, stE)
    dsr = c.sb("a1_dsr", [128, 16], F32, stE)

    @block.vector
    def _(vector):
        h = vector.tensor_copy(out=dsum32[:, :], in_=dsacc[:, 0:1])
        for _tt in range(1, NTT):
            c.seq(vector, h, "V")
            h = vector.tensor_add(out=dsum32[:, :], in0=dsum32[:, :],
                                  in1=dsacc[:, _tt:_tt + 1])
        nn = c.inc("a1v_ds", h, 1)
        c.wait(vector, "a1v_ds", nn)

    for dc in range(16):
        @block.tensor
        def _(tensor, dc=dc):
            c.wait(tensor, "a1v_ds", 1)
            if dc >= 1:
                c.wait(tensor, "a1v_dsr", dc)
            c.inc("a1t_dsr", tensor.matmul(
                ps_dr[:, 0:1], sel32[:, dc * 128:(dc + 1) * 128],
                dsum32[:, :], start=True, stop=True), 1)

        @block.vector
        def _(vector, dc=dc):
            c.wait(vector, "a1t_dsr", dc + 1)
            h = vector.tensor_copy(out=dsr[:, dc:dc + 1], in_=ps_dr[:, 0:1])
            nn = c.inc("a1v_dsr", h, 1)
            c.wait(vector, "a1v_dsr", nn)

    @block.vector
    def _(vector):
        c.wait(vector, "a1v_dsr", 16)
        for dc in range(16):
            h = vector.tensor_scalar(
                out=Pacc[:, dc * S1L:(dc + 1) * S1L],
                in0=Arep[:, dc * S1L:(dc + 1) * S1L],
                scalar1=dsr[:, dc:dc + 1], scalar2=None, op0=OP.mult)
        c.inc("a1v_dec1", h, 1)

    @block.scalar
    def _(scalar):
        c.wait(scalar, "a1v_dec1", 1)
        c.inc("a1s_dec", scalar.activation(out=Pacc[:], in_=Pacc[:], func=AF.Exp), 1)

    @block.sync
    def _(sync):
        c.wait(sync, "a1s_dec", 1)
        c.inc("a1q_cc", sync.dma_start(out=cc1_in[:, 0:1024], in_=Pacc[:]), 16)

    @block.gpsimd
    def _(gpsimd):
        c.wait(gpsimd, "a1s_dec", 1)
        c.inc("a1q_cc2", gpsimd.dma_start(out=cc1_in[:, 1024:2048],
                                          in_=carry[:]), 16)

    @block.gpsimd
    def _(gpsimd):
        c.wait(gpsimd, "a1q_cc", 16)
        c.wait(gpsimd, "a1q_cc2", 16)
        c.inc("a1g_cc", gpsimd.collective_compute(
            "AllGather", OP.bypass, ins=[cc1_in], outs=[cc1_out],
            replica_groups=[[0, 1, 2, 3], [4, 5, 6, 7]]), 1)

    @block.gpsimd
    def _(gpsimd):
        c.wait(gpsimd, "a1g_cc", 1)
        gpsimd.memset(Hbuf[:], 0.0)
        c.inc("a1g_hm", gpsimd.memset(Prun[:], 1.0), 1)

    for j in [2, 1, 0]:
        @block.sync
        def _(sync, j=j):
            c.wait(sync, "a1g_cc", 1)
            if j < 2:
                c.wait(sync, "a1v_cmb", 2 - j)
            c.inc("a1q_ag", sync.dma_start(
                out=agj[:], in_=cc1_out[j * 128:(j + 1) * 128, :]), 16)

        @block.vector
        def _(vector, j=j):
            c.wait(vector, "a1q_ag", 16 * (3 - j))
            c.wait(vector, "a1g_hm", 1)
            m = chm[:, j:j + 1]
            c.seq(vector, vector.tensor_mul(out=tcmb[:, :], in0=agj[:, 1024:2048],
                                            in1=Prun[:, :]), "V")
            c.seq(vector, vector.tensor_scalar(out=tcmb[:, :], in0=tcmb[:, :],
                                               scalar1=m, scalar2=None,
                                               op0=OP.mult), "V")
            c.seq(vector, vector.tensor_add(out=Hbuf[:, :], in0=Hbuf[:, :],
                                            in1=tcmb[:, :]), "V")
            c.seq(vector, vector.tensor_scalar(out=tcmb[:, :], in0=agj[:, 0:1024],
                                               scalar1=1.0, scalar2=m,
                                               op0=OP.subtract, op1=OP.mult), "V")
            c.seq(vector, vector.tensor_scalar(out=tcmb[:, :], in0=tcmb[:, :],
                                               scalar1=1.0, scalar2=None,
                                               op0=OP.add), "V")
            h = vector.tensor_mul(out=Prun[:, :], in0=Prun[:, :], in1=tcmb[:, :])
            nn = c.inc("a1v_cmb", h, 1)
            c.wait(vector, "a1v_cmb", nn)

    barrier(c, block, sems=["a1v_cmb", "a1v_ds", "a1t_dsr", "a1v_dsr",
                            "a1v_dec1", "a1s_dec", "a1q_cc", "a1q_cc2", "a1g_cc",
                            "a1g_hm", "a1q_ag"])
    stE.close()

    stPB = ExitStack()
    bcwB = [c.sb("a1_bcwB", [128, 8 * 256], BF16, stPB)] * 2
    CtB = c.sb("a1_CtB", [128, TT5], BF16, stPB)
    cumrep = c.sb("a1_cumrep", [128, TT5], F32, stPB)
    schB2p = [c.sb(f"a1_schB2{i}", [128, 2 * TT5], F32, stPB) for i in range(2)]
    ycor = c.sb("a1_ycor", [128, TT5], F32, stPB)
    gbB = c.sb("a1_gbB", [128, 2 * TT5], BF16, stPB)
    zsB = c.sb("a1_zsB", [128, TT5], BF16, stPB)
    zexB = c.sb("a1_zexB", [128, TT5], F32, stPB)

    gb = gbB
    zs = zsB
    zex = zexB

    for tt in range(NTT):
        tb = 3 + tt * TT5
        dt_block(tt, "B")

        for dc in range(16):
            k = 16 * tt + dc

            @block.gpsimd
            def _(gpsimd, tt=tt, dc=dc, k=k):
                if k >= 1:
                    c.wait(gpsimd, "a1t_prB", k)
                for q in range(2):
                    n = 2 * dc + q
                    c.inc(f"a1g_bB{dc % 2}", gpsimd.dma_start(
                        out=bcwB[dc % 2][:].rearrange("p (k f) -> p k f", k=8)
                        [:, :, q * 128:(q + 1) * 128],
                        in_=wt_kpf(WIN)[:, :, 4096 + n * 129:4096 + n * 129 + 128]), 16)

            @block.tensor
            def _(tensor, tt=tt, dc=dc, k=k):
                c.wait(tensor, f"a1g_bB{dc % 2}", 32 * (k // 2 + 1))
                c.wait(tensor, "a1v_dt", NTT + tt + 1)
                if k >= 1:
                    c.wait(tensor, "a1v_prB", k)
                    c.wait(tensor, "a1v_pb", S1L // 2 * k)
                for q in range(2):
                    for kc in range(8):
                        tensor.matmul(ps_bc[:, q * TT5:(q + 1) * TT5],
                                      bcwB[dc % 2][:, kc * 256 + q * 128:
                                                   kc * 256 + (q + 1) * 128],
                                      xnv[:, kc, tb:tb + TT5],
                                      start=(kc == 0), stop=(kc == 7))
                h = tensor.matmul(ps_dr[:, 0:TT5],
                                  sel32[:, dc * 128:(dc + 1) * 128],
                                  cum32[:, :], start=True, stop=True)
                c.inc("a1t_prB", h, 1)

            @block.vector
            def _(vector, tt=tt, dc=dc, k=k):
                c.wait(vector, "a1t_prB", k + 1)
                if k >= 1:
                    c.wait(vector, "a1t_cb", S1L // 2 * k)
                    c.wait(vector, "a1s_pb", S1L // 2 * k)
                for q in range(2):
                    c.seq(vector, vector.tensor_copy(
                        out=CtB[q * 64:(q + 1) * 64, :],
                        in_=ps_bc[64:128, q * TT5:(q + 1) * TT5]), "V")
                h = vector.tensor_copy(out=cumrep[:, :], in_=ps_dr[:, 0:TT5])
                nn = c.inc("a1v_prB", h, 1)
                c.wait(vector, "a1v_prB", nn)

            for sq in range(S1L // 2):
                pidx = (16 * tt + dc) * (S1L // 2) + sq

                @block.tensor
                def _(tensor, tt=tt, dc=dc, sq=sq, pidx=pidx):
                    c.wait(tensor, "a1v_prB", k + 1)
                    if pidx >= 2:
                        c.wait(tensor, "a1v_pb", pidx - 1)
                    if tt >= 1:
                        c.wait(tensor, "a1v_Bo2", 8 * tt)
                    pdst = ps_c if sq % 2 == 0 else ps_bc
                    for a in range(2):
                        s = 2 * sq + a
                        h = tensor.matmul(pdst[:, a * TT5:(a + 1) * TT5],
                                          sel128[:, s * 128:(s + 1) * 128],
                                          CtB[:, :], start=True, stop=True)
                    c.inc("a1t_cb", h, 1)

                @block.scalar
                def _(scalar, tt=tt, dc=dc, sq=sq, pidx=pidx):
                    c.wait(scalar, "a1v_prB", k + 1)
                    if pidx >= 2:
                        c.wait(scalar, "a1v_pb", pidx - 1)
                    schB2 = schB2p[pidx % 2]
                    for a in range(2):
                        s = 2 * sq + a
                        h = scalar.activation(
                            out=schB2[:, a * TT5:(a + 1) * TT5], in_=cumrep[:, :],
                            func=AF.Exp,
                            scale=Arep[:, dc * S1L + s:dc * S1L + s + 1])
                        if a == 0:
                            c.seq(scalar, h, "S")
                    c.inc("a1s_pb", h, 1)

                @block.vector
                def _(vector, tt=tt, dc=dc, sq=sq, pidx=pidx):
                    c.wait(vector, "a1s_pb", pidx + 1)
                    c.wait(vector, "a1t_cb", pidx + 1)
                    schB2 = schB2p[pidx % 2]
                    c.seq(vector, vector.tensor_tensor(
                        out=schB2[:, :], in0=schB2[:, :],
                        in1=(ps_c if sq % 2 == 0 else ps_bc)[:, :],
                        op=OP.mult), "V")
                    s0 = 2 * sq
                    if sq == 0:
                        c.seq(vector, vector.tensor_scalar(
                            out=ycor[:, :], in0=schB2[:, 0:TT5],
                            scalar1=Hbuf[:, dc * S1L + s0:dc * S1L + s0 + 1],
                            scalar2=None, op0=OP.mult), "V")
                    else:
                        c.seq(vector, vector.scalar_tensor_tensor(
                            out=ycor[:, :], in0=schB2[:, 0:TT5],
                            scalar=Hbuf[:, dc * S1L + s0:dc * S1L + s0 + 1],
                            in1=ycor[:, :], op0=OP.mult, op1=OP.add), "V")
                    h = vector.scalar_tensor_tensor(
                        out=ycor[:, :], in0=schB2[:, TT5:2 * TT5],
                        scalar=Hbuf[:, dc * S1L + s0 + 1:dc * S1L + s0 + 2],
                        in1=ycor[:, :], op0=OP.mult, op1=OP.add)
                    nn = c.inc("a1v_pb", h, 1)
                    c.wait(vector, "a1v_pb", nn)

            @block.vector
            def _(vector, tt=tt, dc=dc, k=k):
                c.wait(vector, "a1v_pb", S1L // 2 * (k + 1))
                h = vector.tensor_copy(out=gb[:, 0:TT5], in_=ycor[:, :])
                nn = c.inc("a1v_yc", h, 1)
                c.wait(vector, "a1v_yc", nn)

            tail_z_y2(tt, dc, "B")

        tail_wout_x(tt, "B")

    barrier(c, block, sems=["a1v_Bz", "a1v_Bo2", "a1t_Bo", "a1s_Bz", "a1v_pb",
                            "a1s_pb", "a1t_cb", "a1v_prB", "a1t_prB", "a1v_yc",
                            "a1v_dt", "a1s_dt", "a1t_dt"],
            release=["a1"])
    stPB.close()
    stP.close()


def build():
    nc = bass.Bass("TRN2", target_bir_lowering=False)

    def din(name, shape, dt=F32):
        return nc.dram_tensor(name, list(shape), dt, kind="ExternalInput").ap()

    xin = din("xin", (TE, D))
    ident_in = din("ident", (128, 128))
    sel8_in = din("sel8", (8, 8 * 128))
    sel33_in = din("sel33", (33, 33 * 128))
    sel32_in = din("sel32", (32, 16 * 128))
    sel128_in = din("sel128", (128, 64 * 128))
    prevsel_in = din("prevsel", (128, 4))
    w = {k: din(k, shp) for k, shp in WEIGHT_SHAPES.items()}
    dram_chm = din("chmask", (128, 4))
    cc0_in = nc.dram_tensor("cc0_in", [128, 512], F32, kind="Internal").ap()
    cc0_out = nc.dram_tensor("cc0_out", [512, 512], F32, kind="Internal").ap()
    ccx_in = nc.dram_tensor("ccx_in", [128, 24], F32, kind="Internal").ap()
    ccx_out = nc.dram_tensor("ccx_out", [512, 24], F32, kind="Internal").ap()
    cc1_in = nc.dram_tensor("cc1_in", [128, 2048], F32, kind="Internal").ap()
    cc1_out = nc.dram_tensor("cc1_out", [512, 2048], F32, kind="Internal").ap()
    out = nc.dram_tensor("out", [TC, D], F32, kind="ExternalOutput").ap()
    dbg_out = nc.dram_tensor("dbg", [128, 1 << 15], F32, kind="ExternalOutput").ap() if DBG else None

    es = ExitStack()
    with es:
        c = Ctx(nc, es)
        for _nm in ["bT", "bV", "bS", "bG", "bQ", "bC", "ord_V", "ord_S"]:
            c.semrec(_nm)
        x = c.sb("x", [128, 8 * TE], BF16)
        xn = c.sb("xn", [128, 8 * TE], BF16)
        ident = c.sb("identt", [128, 128], BF16)
        sel8 = c.sb("sel8t", [8, 8 * 128], BF16)
        sel33 = c.sb("sel33t", [33, 33 * 128], BF16)
        chmt = c.sb("chmt", [128, 4], F32)
        prevselt = c.sb("prevselt", [128, 4], F32)
        identf = c.sb("identtf", [128, 128], F32)
        ones_rowf = c.sb("ones_rowf", [1, 128], F32)
        ones128 = c.sb("ones128", [128, 1], BF16)
        ones_row = c.sb("ones_row", [1, 128], BF16)

        block = es.enter_context(nc.Block())

        c.make_consts(CONST_VALUES)

        @block.gpsimd
        def _(gpsimd):
            gpsimd.memset(ones128[:], 1.0)
            gpsimd.memset(ones_row[:], 1.0)
            for v, t in c.consts.items():
                gpsimd.memset(t[:], float(v))
            c.inc("g_const", gpsimd.dma_start(out=sel8[:], in_=sel8_in), 16)
            c.inc("g_const", gpsimd.dma_start(out=sel33[:], in_=sel33_in), 16)
            c.inc("g_const", gpsimd.dma_start(out=chmt[:], in_=dram_chm), 16)
            c.inc("g_const", gpsimd.dma_start(out=prevselt[:], in_=prevsel_in), 16)
            c.inc("g_const", gpsimd.dma_start(out=identf[:], in_=ident_in), 16)
            gpsimd.memset(ones_rowf[:], 1.0)
            c.inc("g_const", gpsimd.dma_start(out=ident[:], in_=ident_in), 16)
        const_target = c.val("g_const")

        with nc.named_scope("load"):
            ph_load(c, block, xin, x, ident, const_target)
            barrier(c, block, sems=["ld_ev", "ld_tr"], release=["ld_"])
        if STAGE == 0:
            ph_store(c, block, x, out, ident)
            return nc
        with nc.named_scope("ln1"):
            ph_layernorm(c, block, x, xn, TE, ones128, ones_row, "ln1")

        if STAGE == 1:
            ph_store(c, block, xn, out, ident)
            return nc

        g = c.sb("gbuf", [128, 8 * TC], BF16)
        with nc.named_scope("ffn0"):
            ph_ffn0(c, block, xn, g, w, ones128, ones_row, ident, sel8)
        if STAGE == 2:
            ph_store(c, block, g, out, ident, toff=0, tstride=TC)
            return nc

        with nc.named_scope("attn0"):
            ph_attn0(c, block, x, xn, g, w, ones128, ones_row, ident, sel33,
                     chmt[:], (cc0_in, cc0_out))
        if STAGE == 3:
            ph_store(c, block, x, out, ident)
            return nc

        with nc.named_scope("edge"):
            ph_edge_exchange(c, block, x, chmt[:], prevselt[:], (ccx_in, ccx_out))
        with nc.named_scope("ln2"):
            ph_layernorm(c, block, x, xn, TE, ones128, ones_row, "ln2")
        with nc.named_scope("ffn1"):
            ph_ffn1(c, block, xn, g, w, ones128, ones_row, ones_rowf, ident, identf)
        if STAGE == 4:
            ph_store(c, block, g, out, ident, toff=0, tstride=TC)
            return nc

        with nc.named_scope("attn1"):
            ph_attn1(c, block, x, xn, g, w, ones128, ones_row, ident, sel32_in,
                     sel128_in, chmt[:], (cc1_in, cc1_out))
        with nc.named_scope("store"):
            ph_store(c, block, x, out, ident)
    return nc


WEIGHT_SHAPES = {
    "L0_W_in": (D, 2 * DI),
    "L0_conv_w": (DI, 4),
    "L0_conv_b": (DI,),
    "L0_W_xproj": (DI, 33),
    "L0_W_dt": (DI,),
    "L0_b_dt": (DI,),
    "L0_A_log": (DI, S0),
    "L0_Dp": (DI,),
    "L0_W_out": (DI, D),
    "F0_router_w": (D, 8),
    "F0_w1": (8, D, D),
    "F0_w2": (8, D, D),
    "F0_sgu": (D, 2 * D),
    "F0_sdown": (D, D),
    "F0_sgw": (D,),
    "L1_W_inT": (D, DPROJ),
    "L1_conv_w": (DI, 4),
    "L1_conv_b": (DI,),
    "L1_A_log": (NH, S1),
    "L1_D": (NH,),
    "L1_dt_bias": (NH,),
    "L1_W_outT": (DI, D),
    "Q_kernel": (D, D),
    "Q_bias": (D,),
    "Q_a": (D, 16),
    "Q_b": (16, D),
}


_BUILT = None


def make_in_maps(inputs):
    xs = np.ascontiguousarray(inputs["L0_attn_x"], dtype=np.float32)
    flat = xs.reshape(B * T, D)
    wmap = {
        "L0_W_in": inputs["L0_attn_W_in"],
        "L0_conv_w": inputs["L0_attn_conv_w"][:, 0, :],
        "L0_conv_b": inputs["L0_attn_conv_b"],
        "L0_W_xproj": inputs["L0_attn_W_xproj"],
        "L0_W_dt": inputs["L0_attn_W_dt"][0],
        "L0_b_dt": inputs["L0_attn_b_dt"],
        "L0_A_log": inputs["L0_attn_A_log"],
        "L0_Dp": inputs["L0_attn_Dp"],
        "L0_W_out": inputs["L0_attn_W_out"],
        "F0_router_w": inputs["L0_ffn_router_w"],
        "F0_w1": inputs["L0_ffn_w1"],
        "F0_w2": inputs["L0_ffn_w2"],
        "F0_sgu": inputs["L0_ffn_shared_gate_up"],
        "F0_sdown": inputs["L0_ffn_shared_down"],
        "F0_sgw": inputs["L0_ffn_shared_gate_w"][:, 0],
        "L1_W_inT": inputs["L1_attn_W_in"].T,
        "L1_conv_w": inputs["L1_attn_conv_w"][:, 0, :],
        "L1_conv_b": inputs["L1_attn_conv_b"],
        "L1_A_log": inputs["L1_attn_A_log"],
        "L1_D": inputs["L1_attn_D"],
        "L1_dt_bias": inputs["L1_attn_dt_bias"],
        "L1_W_outT": inputs["L1_attn_W_out"].T,
        "Q_kernel": inputs["L1_ffn_kernel"],
        "Q_bias": inputs["L1_ffn_bias"],
        "Q_a": inputs["L1_ffn_a"],
        "Q_b": inputs["L1_ffn_b"],
    }
    wmap = {k: np.ascontiguousarray(v, dtype=np.float32) for k, v in wmap.items()}
    ident = np.eye(128, dtype=np.float32)
    sel8 = np.zeros((8, 8 * 128), np.float32)
    for e in range(8):
        sel8[e, e * 128:(e + 1) * 128] = 1.0
    sel33 = np.zeros((33, 33 * 128), np.float32)
    for r in range(33):
        sel33[r, r * 128:(r + 1) * 128] = 1.0
    sel32 = np.zeros((32, 16 * 128), np.float32)
    for dcq in range(16):
        for p in range(128):
            sel32[2 * dcq + p // 64, dcq * 128 + p] = 1.0
    sel128 = np.zeros((128, 64 * 128), np.float32)
    for s in range(64):
        for p in range(128):
            sel128[(p // 64) * 64 + s, s * 128 + p] = 1.0

    in_maps = []
    for core in range(NCORES):
        t0 = core * TC
        pre = np.zeros((3, D), np.float32) if core % 4 == 0 else flat[t0 - 3:t0]
        xin_c = np.ascontiguousarray(np.concatenate([pre, flat[t0:t0 + TC]], axis=0))
        m = np.zeros((128, 4), np.float32)
        m[:, :core % 4] = 1.0
        pv = np.zeros((128, 4), np.float32)
        if core % 4 > 0:
            pv[:, core % 4 - 1] = 1.0
        im = dict(wmap)
        im["xin"] = xin_c
        im["ident"] = ident
        im["sel8"] = sel8
        im["sel33"] = sel33
        im["sel32"] = sel32
        im["sel128"] = sel128
        im["chmask"] = m
        im["prevsel"] = pv
        in_maps.append(im)
    return in_maps


def kernel(**inputs):
    global _BUILT
    in_maps = make_in_maps(inputs)
    if _BUILT is None:
        _BUILT = build()
    res = run_bass_kernel_spmd(_BUILT, in_maps, core_ids=list(range(NCORES)))
    parts = [res.results[i]["out"] for i in range(NCORES)]
    return np.concatenate(parts, axis=0).reshape(B, T, D).astype(np.float32)


# version 10
# speedup vs baseline: 1.2507x; 1.0355x over previous
import os
import numpy as np
from contextlib import ExitStack

import concourse.bass as bass
import concourse.mybir as mybir
from concourse.bass_utils import run_bass_kernel_spmd

F32 = mybir.dt.float32
BF16 = mybir.dt.bfloat16
AF = mybir.ActivationFunctionType
OP = mybir.AluOpType
AX = mybir.AxisListType

B, T, D = 2, 8192, 1024
NCORES = 8
TC = int(os.environ.get("KTC", (B * T) // NCORES))
TE = TC + 3
DI = 2048
S0 = 16
NH, S1, HD = 32, 64, 64
DPROJ = 2 * DI + NH * (2 * S1 + 1)
DBG = int(os.environ.get("KDBG", "0"))
STAGE = int(os.environ.get("KSTAGE", "99"))

NLEXP = "nlexp"
SILU = "silu"
CONST_VALUES = [1e-5, -0.5, 1.0, -1.0, 0.0]


class Ctx:
    def __init__(self, nc, es):
        self.nc = nc
        self.es = es
        self._sems = {}
        self.cur_table = "default"
        self.bar_n = 0

    def semrec(self, name):
        if name not in self._sems:
            self._semgen = getattr(self, "_semgen", 0) + 1
            h = self.es.enter_context(self.nc.semaphore(f"{name}__{self._semgen}"))
            self._sems[name] = [h, 0]
        return self._sems[name]

    def inc(self, name, instr, k=1):
        rec = self.semrec(name)
        instr.then_inc(rec[0], k)
        rec[1] += k
        return rec[1]

    def val(self, name):
        return self.semrec(name)[1]

    def wait(self, eng, name, v=None):
        rec = self.semrec(name)
        eng.wait_ge(rec[0], rec[1] if v is None else v)

    def seq(self, eng, instr, tag):
        name = f"ord_{tag}"
        n = self.inc(name, instr, 1)
        eng.wait_ge(self.semrec(name)[0], n)

    def sb(self, name, shape, dt=BF16, stack=None):
        return (stack or self.es).enter_context(self.nc.sbuf_tensor(name, shape, dt))

    def ps(self, name, shape, dt=F32, stack=None):
        return (stack or self.es).enter_context(self.nc.psum_tensor(name, shape, dt))

    def make_consts(self, values):
        self.consts = {}
        for v in values:
            t = self.sb(f"cst_{str(v).replace('-','m').replace('.','_')}", [128, 1], F32)
            self.consts[v] = t
            self.nc.const_aps.register(F32, v, t[:, :])

    def constf(self, v):
        return self.consts[v][:, :]

    def table(self, scalar, name):
        if self.cur_table != name:
            from concourse.hw_specs import get_activation_tables
            setname = {NLEXP: "natural_log_exp_and_others",
                       SILU: "silu_and_others"}[name]
            idx = list(get_activation_tables(self.nc.m.arch).keys()).index(setname)
            scalar.drain()
            scalar.add_instruction(mybir.InstLoadActFuncSet(
                name=self.nc.get_next_instruction_name(),
                act_func_set_id=idx, ins=[], outs=[]))
            scalar.drain()
            self.cur_table = name


def barrier(c, block, sems=(), release=()):
    c.bar_n += 1
    n = c.bar_n
    names = ["bT", "bV", "bS", "bG", "bQ"]
    extra = [(nm, c.val(nm)) for nm in sems if c.val(nm) > 0]

    def stage1(eng, myname, incer):
        for nm, v in extra:
            eng.wait_ge(c.semrec(nm)[0], v)
        c.inc(myname, incer(eng), 1)

    block.tensor(lambda e: stage1(e, "bT", lambda e: e.nop()))
    block.vector(lambda e: stage1(e, "bV", lambda e: e.engine_nop()))
    block.scalar(lambda e: stage1(e, "bS", lambda e: e.nop()))
    block.gpsimd(lambda e: stage1(e, "bG", lambda e: e.nop()))
    block.sync(lambda e: stage1(e, "bQ", lambda e: e.nop()))

    @block.gpsimd
    def _(gpsimd):
        for nm in names:
            gpsimd.wait_ge(c.semrec(nm)[0], n)
        todrop = [k for k in list(c._sems)
                  if any(k.startswith(p) for p in release)]
        if todrop:
            c.nc.clear_and_free_semaphores([c._sems[k][0] for k in todrop])
            for k in todrop:
                del c._sems[k]
        c.inc("bC", gpsimd.nop(), 1)

    for mk in [block.tensor, block.vector, block.scalar, block.sync]:
        def body(eng):
            eng.wait_ge(c.semrec("bC")[0], n)
        mk(body)


def wt_kpf(ap, p=128):
    return ap.rearrange("(k p) f -> k p f", p=p).transpose([1, 0, 2])


def ceil_div(a, b):
    return (a + b - 1) // b


def ph_load(c, block, xin, x, ident, consts_gd_target):
    nc = c.nc
    st = ExitStack()
    io_a = c.sb("io_a", [128, D], BF16, st)
    io_b = c.sb("io_b", [128, D], BF16, st)
    ps_tr = c.ps("ps_ld", [128, 8 * 128], BF16, st)

    ntg = ceil_div(TE, 128)
    for gi in range(ntg):
        t0 = gi * 128
        n = min(128, TE - t0)
        buf = io_a if gi % 2 == 0 else io_b

        dsem = f"ld_dma{gi % 2}"

        @block.gpsimd
        def _(gpsimd, t0=t0, n=n, buf=buf, gi=gi, dsem=dsem):
            if gi >= 2:
                c.wait(gpsimd, "ld_tr", gi - 1)
            c.inc(dsem, gpsimd.dma_start(out=buf[0:n, :], in_=xin[t0:t0 + n, :]), 16)

        @block.tensor
        def _(tensor, t0=t0, n=n, buf=buf, gi=gi, dsem=dsem):
            c.wait(tensor, dsem, 16 * (gi // 2 + 1))
            if gi == 0:
                c.wait(tensor, "g_const", consts_gd_target)
            if gi >= 1:
                c.wait(tensor, "ld_ev", gi)
            for dc in range(8):
                h = tensor.transpose(ps_tr[:, dc * 128:dc * 128 + n],
                                     buf[0:n, dc * 128:(dc + 1) * 128], ident[0:n, 0:n])
            c.inc("ld_tr", h, 1)

        @block.vector
        def _(vector, t0=t0, n=n, gi=gi):
            c.wait(vector, "ld_tr", gi + 1)
            c.inc("ld_ev", vector.tensor_copy(
                out=x[:].rearrange("p (d t) -> p d t", d=8)[:, :, t0:t0 + n],
                in_=ps_tr[:].rearrange("p (d t) -> p d t", d=8)[:, :, 0:n]), 1)

    st.close()


def ph_layernorm(c, block, src, dst, tokens, ones128, ones_row, tag):
    nc = c.nc
    st = ExitStack()
    sq = c.sb(f"ln_sq_{tag}", [128, 8 * 512], BF16, st)
    ps_sum = c.ps(f"ln_ps_s_{tag}", [1, 512], F32, st)
    ps_sq = c.ps(f"ln_ps_q_{tag}", [1, 512], F32, st)
    ps_bc = c.ps(f"ln_ps_bc_{tag}", [128, 512], F32, st)
    rs0 = c.sb(f"ln_rs0_{tag}", [1, 512], F32, st)
    rs1 = c.sb(f"ln_rs1_{tag}", [1, 512], F32, st)
    rowm = c.sb(f"ln_rowm_{tag}", [1, 512], BF16, st)
    rowr = c.sb(f"ln_rowr_{tag}", [1, 512], BF16, st)
    rf0 = c.sb(f"ln_rf0_{tag}", [1, 512], F32, st)
    rf1 = c.sb(f"ln_rf1_{tag}", [1, 512], F32, st)
    mbc = c.sb(f"ln_mbc_{tag}", [128, 512], BF16, st)
    rbc = c.sb(f"ln_rbc_{tag}", [128, 512], BF16, st)

    ngr = ceil_div(tokens, 512)
    base_sq = c.val("sS")

    for gi in range(ngr):
        t0 = gi * 512
        n = min(512, tokens - t0)
        sview = src[:].rearrange("p (d t) -> p d t", d=8)

        @block.scalar
        def _(scalar, t0=t0, n=n, gi=gi):
            if gi >= 1:
                c.wait(scalar, "lnT", gi)
            for dc in range(8):
                h = scalar.activation(out=sq[:].rearrange("p (d t) -> p d t", d=8)[:, dc, 0:n],
                                      in_=sview[:, dc, t0:t0 + n], func=AF.Square)
            c.inc("lnS", h, 1)

        @block.tensor
        def _(tensor, t0=t0, n=n, gi=gi):
            c.wait(tensor, "lnS", gi + 1)
            if gi >= 1:
                c.wait(tensor, "lnV", 2 * gi)
            sqv = sq[:].rearrange("p (d t) -> p d t", d=8)
            for dc in range(8):
                tensor.matmul(ps_sum[0:1, 0:n], ones128[:, 0:1],
                              sview[:, dc, t0:t0 + n], start=(dc == 0), stop=(dc == 7))
            for dc in range(8):
                h = tensor.matmul(ps_sq[0:1, 0:n], ones128[:, 0:1],
                                  sqv[:, dc, 0:n], start=(dc == 0), stop=(dc == 7))
            c.inc("lnT", h, 1)

        @block.vector
        def _(vector, t0=t0, n=n, gi=gi):
            c.wait(vector, "lnT", gi + 1)
            vector.tensor_copy(out=rs0[0:1, 0:n], in_=ps_sum[0:1, 0:n])
            c.seq(vector, vector.tensor_copy(out=rs1[0:1, 0:n], in_=ps_sq[0:1, 0:n]), "V")
            vector.tensor_scalar_mul(out=rf0[0:1, 0:n], in0=rs0[0:1, 0:n], scalar1=1.0 / D)
            c.seq(vector, vector.tensor_scalar_mul(out=rf1[0:1, 0:n], in0=rs1[0:1, 0:n],
                                                   scalar1=1.0 / D), "V")
            c.seq(vector, vector.tensor_mul(out=rs0[0:1, 0:n], in0=rf0[0:1, 0:n],
                                            in1=rf0[0:1, 0:n]), "V")
            vector.tensor_sub(out=rs1[0:1, 0:n], in0=rf1[0:1, 0:n], in1=rs0[0:1, 0:n])
            c.inc("lnVm", vector.tensor_copy(out=rowm[:, 0:n], in_=rf0[0:1, 0:n]), 1)

        @block.scalar
        def _(scalar, t0=t0, n=n, gi=gi):
            c.wait(scalar, "lnVm", gi + 1)
            if gi >= 1:
                c.wait(scalar, "lnT2", 2 * gi)
            c.table(scalar, NLEXP)
            if os.environ.get("KNORSTD"):
                c.inc("lnS2", scalar.activation(out=rowr[:, 0:n], in_=rs1[0:1, 0:n],
                                                func=AF.Copy, scale=0.0, bias=1.0), 1)
            else:
                c.seq(scalar, scalar.activation(out=rf1[0:1, 0:n], in_=rs1[0:1, 0:n],
                                                func=AF.Ln, bias=1e-5), "S")
                c.inc("lnS2", scalar.activation(out=rowr[:, 0:n], in_=rf1[0:1, 0:n],
                                                func=AF.Exp, scale=-0.5), 1)

        @block.tensor
        def _(tensor, t0=t0, n=n, gi=gi):
            c.wait(tensor, "lnS2", gi + 1)
            c.inc("lnT2", tensor.matmul(ps_bc[:, 0:n], ones_row[0:1, :],
                                        rowm[0:1, 0:n], start=True, stop=True), 1)

        @block.vector
        def _(vector, gi=gi, n=n):
            c.wait(vector, "lnT2", 2 * gi + 1)
            c.inc("lnV2", vector.tensor_copy(out=mbc[:, 0:n], in_=ps_bc[:, 0:n]), 1)

        @block.tensor
        def _(tensor, t0=t0, n=n, gi=gi):
            c.wait(tensor, "lnV2", 2 * gi + 1)
            c.inc("lnT2", tensor.matmul(ps_bc[:, 0:n], ones_row[0:1, :],
                                        rowr[0:1, 0:n], start=True, stop=True), 1)

        @block.vector
        def _(vector, t0=t0, n=n, gi=gi):
            c.wait(vector, "lnT2", 2 * gi + 2)
            h = vector.tensor_copy(out=rbc[:, 0:n], in_=ps_bc[:, 0:n])
            nn = c.inc("lnV2", h, 1)
            c.wait(vector, "lnV2", nn)
            dview = dst[:].rearrange("p (d t) -> p d t", d=8)
            for dc in range(8):
                h = vector.tensor_sub(out=dview[:, dc, t0:t0 + n],
                                      in0=sview[:, dc, t0:t0 + n], in1=mbc[:, 0:n])
            c.seq(vector, h, "V")
            for dc in range(8):
                h = vector.tensor_mul(out=dview[:, dc, t0:t0 + n],
                                      in0=dview[:, dc, t0:t0 + n], in1=rbc[:, 0:n])
            c.inc("lnV", h, 2)

    barrier(c, block, sems=["lnV", "lnV2", "lnVm", "lnS", "lnS2", "lnT", "lnT2"], release=["ln"])
    st.close()


def ph_store(c, block, x, out, ident, toff=3, tstride=TE):
    nc = c.nc
    st = ExitStack()
    ps_o = c.ps("ps_st", [128, 8 * 128], BF16, st)
    ot_a = c.sb("st_a", [128, D], F32, st)
    ot_b = c.sb("st_b", [128, D], F32, st)

    for gi in range(TC // 128):
        t0 = toff + gi * 128
        ot = ot_a if gi % 2 == 0 else ot_b

        @block.tensor
        def _(tensor, t0=t0, gi=gi):
            if gi >= 1:
                c.wait(tensor, "st_ev", gi)
            for dc in range(8):
                h = tensor.transpose(
                    ps_o[:, dc * 128:(dc + 1) * 128],
                    x[:].rearrange("p (d t) -> p d t", t=tstride)[:, dc, t0:t0 + 128],
                    ident[:])
            c.inc("st_tr", h, 1)

        dsem = f"st_dma{gi % 2}"

        @block.vector
        def _(vector, gi=gi, ot=ot, dsem=dsem):
            c.wait(vector, "st_tr", gi + 1)
            if gi >= 2:
                c.wait(vector, dsem, 16 * (gi // 2))
            c.inc("st_ev", vector.tensor_copy(out=ot[:, :], in_=ps_o[:, :]), 1)

        @block.sync
        def _(sync, gi=gi, ot=ot, dsem=dsem):
            c.wait(sync, "st_ev", gi + 1)
            c.inc(dsem, sync.dma_start(out=out[gi * 128:(gi + 1) * 128, :],
                                       in_=ot[:, :]), 16)

    @block.sync
    def _(sync):
        c.wait(sync, "st_dma0")
        c.wait(sync, "st_dma1")
    st.close()


def ph_debug_dump(c, block, dbg_out, srcs):
    st = ExitStack()
    tmp = c.sb("dbg_tmp", [128, 4096], F32, st)
    col = 0
    for ap in srcs:
        ncol = ap.shape[-1] if len(ap.shape) == 2 else int(np.prod(ap.shape[1:]))

        @block.vector
        def _(vector, ap=ap, col=col, ncol=ncol):
            c.wait(vector, "dbgq", c.val("dbgq"))
            c.inc("dbgv", vector.tensor_copy(out=tmp[0:ap.shape[0], 0:ncol], in_=ap), 1)

        @block.sync
        def _(sync, col=col, ncol=ncol, ap=ap):
            c.wait(sync, "dbgv")
            c.inc("dbgq", sync.dma_start(
                out=dbg_out[0:ap.shape[0], col:col + ncol],
                in_=tmp[0:ap.shape[0], 0:ncol]), 16)
        col += ncol

    @block.sync
    def _(sync):
        c.wait(sync, "dbgq")
    st.close()



def ph_ffn0(c, block, xn, g, wdram, ones128, ones_row, ident, sel8):
    nc = c.nc
    NTG = TC // 128
    NT5 = TC // 512
    xnv = xn[:].rearrange("p (d t) -> p d t", d=8)

    stP = ExitStack()
    combrows = c.sb("ffn_combrows", [8, TC], BF16, stP)
    sgrow = c.sb("ffn_sgrow", [1, TC], BF16, stP)
    sgbc = c.sb("ffn_sgbc", [128, TC], BF16, stP)

    stA = ExitStack()
    rw = c.sb("ffn_rw", [128, 8 * 8], BF16, stA)
    sgwt = c.sb("ffn_sgw", [128, 8], BF16, stA)
    comb = c.sb("ffn_comb", [128, NTG * 8], BF16, stA)
    prob = c.sb("ffn_prob", [128, 8], F32, stA)
    gcmp = c.sb("ffn_gcmp", [128, 64], F32, stA)
    cnt = c.sb("ffn_cnt", [128, 8], F32, stA)
    ncol = c.sb("ffn_ncol", [128, 1], F32, stA)
    scol = c.sb("ffn_scol", [128, 1], F32, stA)
    sgcol = c.sb("ffn_sgcol", [128, 1], F32, stA)
    ps_r = c.ps("ffn_ps_r", [128, 8], F32, stA)
    ps_tt = c.ps("ffn_ps_tt", [8, 128], BF16, stA)
    ps_sg = c.ps("ffn_ps_sg", [128, 1], F32, stA)

    @block.gpsimd
    def _(gpsimd):
        c.inc("ffnAg", gpsimd.dma_start(
            out=rw[:].rearrange("p (k e) -> p k e", k=8),
            in_=wt_kpf(wdram["F0_router_w"])), 16)
        c.inc("ffnAg", gpsimd.dma_start(
            out=sgwt[:], in_=wdram["F0_sgw"].rearrange("(k p) -> k p", p=128).transpose([1, 0]),
            allow_slow_non_contiguous=True), 16)

    for tg in range(NTG):
        t0 = 3 + tg * 128

        @block.tensor
        def _(tensor, tg=tg, t0=t0):
            c.wait(tensor, "ffnAg", 32)
            if tg >= 1:
                c.wait(tensor, "ffnAv", 2 * tg)
            for kc in range(8):
                tensor.matmul(ps_r[:, :], xnv[:, kc, t0:t0 + 128],
                              rw[:, kc * 8:(kc + 1) * 8],
                              start=(kc == 0), stop=(kc == 7))
            for kc in range(8):
                h = tensor.matmul(ps_sg[:, :], xnv[:, kc, t0:t0 + 128],
                                  sgwt[:, kc:kc + 1],
                                  start=(kc == 0), stop=(kc == 7))
            c.inc("ffnAt", h, 1)

        @block.vector
        def _(vector, tg=tg, t0=t0):
            c.wait(vector, "ffnAt", tg + 1)
            c.seq(vector, vector.tensor_copy(out=prob[:], in_=ps_r[:, :]), "V")
            c.seq(vector, vector.tensor_reduce(out=ncol[:], in_=prob[:],
                                               axis=AX.X, op=OP.max), "V")
            c.seq(vector, vector.tensor_scalar_mul(out=ncol[:], in0=ncol[:],
                                                   scalar1=-1.0), "V")
            c.inc("ffnAv1", vector.engine_nop(), 1)

        @block.scalar
        def _(scalar, tg=tg):
            c.wait(scalar, "ffnAv1", tg + 1)
            c.inc("ffnAs", scalar.activation(out=prob[:], in_=prob[:], func=AF.Exp,
                                             bias=ncol[:, :]), 1)

        @block.vector
        def _(vector, tg=tg):
            c.wait(vector, "ffnAs", tg + 1)
            c.seq(vector, vector.tensor_reduce(out=scol[:], in_=prob[:],
                                               axis=AX.X, op=OP.add), "V")
            c.seq(vector, vector.reciprocal(out=scol[:], in_=scol[:]), "V")
            c.seq(vector, vector.tensor_scalar(out=prob[:], in0=prob[:],
                                               scalar1=scol[:, :], scalar2=None,
                                               op0=OP.mult), "V")
            c.seq(vector, vector.tensor_tensor(
                out=gcmp[:],
                in0=prob[:].unsqueeze(1).broadcast_to((128, 8, 8)),
                in1=prob[:].unsqueeze(2).broadcast_to((128, 8, 8)),
                op=OP.is_gt), "V")
            c.seq(vector, vector.tensor_reduce(out=cnt[:], in_=gcmp[:].rearrange(
                "p (e f) -> p e f", e=8), axis=AX.X, op=OP.add), "V")
            c.seq(vector, vector.tensor_single_scalar(out=cnt[:], in_=cnt[:],
                                                      scalar=1.5, op=OP.is_le), "V")
            h = vector.tensor_tensor(out=comb[:, tg * 8:(tg + 1) * 8],
                                     in0=prob[:], in1=cnt[:], op=OP.mult)
            c.inc("ffnAv", h, 2)

    for tg in range(NTG):
        @block.tensor
        def _(tensor, tg=tg):
            c.wait(tensor, "ffnAv", 2 * NTG)
            if tg >= 1:
                c.wait(tensor, "ffnAv2", 2 * tg)
            h = tensor.transpose(ps_tt[:, :], comb[:, tg * 8:(tg + 1) * 8],
                                 ident[:])
            c.inc("ffnAt2", h, 1)

        @block.vector
        def _(vector, tg=tg):
            c.wait(vector, "ffnAt2", tg + 1)
            c.inc("ffnAv2", vector.tensor_copy(
                out=combrows[:, tg * 128:(tg + 1) * 128], in_=ps_tt[:, :]), 1)
            c.inc("ffnAv2", vector.engine_nop(), 1)

    @block.vector
    def _(vector):
        c.wait(vector, "ffnAv2", 2 * NTG)
        c.inc("ffnAv2_done", vector.engine_nop(), 1)

    barrier(c, block, sems=["ffnAv", "ffnAv2", "ffnAv2_done", "ffnAt", "ffnAt2",
                            "ffnAs", "ffnAv1"], release=["ffnAg", "ffnAt", "ffnAv1", "ffnAs"])
    stA.close()

    stB = ExitStack()
    sgl = c.sb("ffn_sgl", [128, 1], BF16, stB)
    ps_sg2 = c.ps("ffn_ps_sg2", [128, 1], F32, stB)
    ps_tt2 = c.ps("ffn_ps_tt2", [1, 128], BF16, stB)
    sgwt2 = c.sb("ffn_sgw2", [128, 8], BF16, stB)
    ps_bc = c.ps("ffn_ps_bc", [128, 512], F32, stB)

    @block.gpsimd
    def _(gpsimd):
        c.inc("ffnBg", gpsimd.dma_start(
            out=sgwt2[:], in_=wdram["F0_sgw"].rearrange("(k p) -> k p", p=128).transpose([1, 0]),
            allow_slow_non_contiguous=True), 16)

    @block.scalar
    def _(scalar):
        c.table(scalar, SILU)
        c.inc("ffnBs0", scalar.nop(), 1)

    for tg in range(NTG):
        t0 = 3 + tg * 128

        @block.tensor
        def _(tensor, tg=tg, t0=t0):
            c.wait(tensor, "ffnBg", 16)
            if tg >= 1:
                c.wait(tensor, "ffnBv", 2 * tg)
            for kc in range(8):
                h = tensor.matmul(ps_sg2[:, :], xnv[:, kc, t0:t0 + 128],
                                  sgwt2[:, kc:kc + 1],
                                  start=(kc == 0), stop=(kc == 7))
            c.inc("ffnBt", h, 1)

        @block.scalar
        def _(scalar, tg=tg):
            c.wait(scalar, "ffnBt", tg + 1)
            c.wait(scalar, "ffnBs0", 1)
            c.inc("ffnBs", scalar.activation(out=sgl[:], in_=ps_sg2[:, :],
                                             func=AF.Tanh, scale=0.5), 1)

        @block.vector
        def _(vector, tg=tg):
            c.wait(vector, "ffnBs", tg + 1)
            c.seq(vector, vector.tensor_scalar(out=sgl[:], in0=sgl[:], scalar1=0.5,
                                               scalar2=0.5, op0=OP.mult, op1=OP.add), "V")
            c.inc("ffnBv1", vector.engine_nop(), 1)

        @block.tensor
        def _(tensor, tg=tg):
            c.wait(tensor, "ffnBv1", tg + 1)
            c.inc("ffnBt2", tensor.transpose(ps_tt2[:, :], sgl[:, :], ident[:]), 1)

        @block.vector
        def _(vector, tg=tg):
            c.wait(vector, "ffnBt2", tg + 1)
            h = vector.tensor_copy(out=sgrow[0:1, tg * 128:(tg + 1) * 128],
                                   in_=ps_tt2[0:1, :])
            c.inc("ffnBv", h, 2)

    for w5 in range(NT5):
        @block.tensor
        def _(tensor, w5=w5):
            c.wait(tensor, "ffnBv", 2 * NTG)
            if w5 >= 1:
                c.wait(tensor, "ffnBv2", w5)
            c.inc("ffnBt3", tensor.matmul(
                ps_bc[:, :], ones_row[0:1, :], sgrow[0:1, w5 * 512:(w5 + 1) * 512],
                start=True, stop=True), 1)

        @block.vector
        def _(vector, w5=w5):
            c.wait(vector, "ffnBt3", w5 + 1)
            c.inc("ffnBv2", vector.tensor_copy(
                out=sgbc[:, w5 * 512:(w5 + 1) * 512], in_=ps_bc[:, :]), 1)

    barrier(c, block, sems=["ffnBv", "ffnBv2", "ffnBs", "ffnBt", "ffnBt2",
                            "ffnBt3", "ffnBv1", "ffnBs0"], release=["ffnB"])
    stB.close()

    stC = ExitStack()
    w1b = [c.sb(f"ffn_w1_{i}", [128, 8 * 1024], BF16, stC) for i in range(2)]
    w2s = c.sb("ffn_w2s", [128, 8 * 1024], BF16, stC)
    he = c.sb("ffn_he", [128, 8 * TC], BF16, stC)
    cbc = c.sb("ffn_cbc", [128, TC], BF16, stC)
    HW = NT5 * 512
    ps_h = c.ps("ffn_ps_h", [128, 2 * HW], F32, stC)
    hev = he[:].rearrange("p (o t) -> p o t", o=8)
    gv = g[:].rearrange("p (d t) -> p d t", d=8)

    for e in range(8):
        wb1, wb2 = w1b[e % 2], w2s

        @block.gpsimd
        def _(gpsimd, e=e, wb1=wb1, wb2=wb2):
            if e >= 2:
                c.wait(gpsimd, "ffnCt_done", e - 1)
            c.inc(f"ffnCg{e % 2}", gpsimd.dma_start(
                out=wb1[:].rearrange("p (k f) -> p k f", k=8),
                in_=wt_kpf(wdram["F0_w1"].rearrange("E d f -> (E d) f")
                           [e * 1024:(e + 1) * 1024, :])), 16)
            if e >= 1:
                c.wait(gpsimd, "ffnCt_w2done", e)
            c.inc("ffnCgw2", gpsimd.dma_start(
                out=wb2[:].rearrange("p (k f) -> p k f", k=8),
                in_=wt_kpf(wdram["F0_w2"].rearrange("E d f -> (E d) f")
                           [e * 1024:(e + 1) * 1024, :])), 16)

        for w5 in range(NT5):
            @block.tensor
            def _(tensor, e=e, w5=w5):
                c.wait(tensor, "ffnCs_h", max(0, 8 * e - 1))
                if w5 == 0:
                    c.wait(tensor, "ffnAv2_done", 1)
                c.inc("ffnCt_cb", tensor.matmul(
                    ps_h[:, w5 * 512:(w5 + 1) * 512], sel8[:, e * 128:(e + 1) * 128],
                    combrows[:, w5 * 512:(w5 + 1) * 512],
                    start=True, stop=True), 1)

        @block.vector
        def _(vector, e=e):
            c.wait(vector, "ffnCt_cb", NT5 * (e + 1))
            c.inc("ffnCv_cb", vector.tensor_copy(out=cbc[:, :], in_=ps_h[:, 0:HW]), 1)

        for oc in range(8):
            slot = (oc % 2) * HW

            @block.tensor
            def _(tensor, e=e, oc=oc, wb1=wb1, slot=slot):
                c.wait(tensor, f"ffnCg{e % 2}", 16 * (e // 2 + 1))
                if oc == 0:
                    c.wait(tensor, "ffnCv_cb", e + 1)
                c.wait(tensor, "ffnCs_h", max(0, 8 * e + oc - 1))
                for kc in range(8):
                    for tg in range(NT5):
                        h = tensor.matmul(
                            ps_h[:, slot + tg * 512:slot + (tg + 1) * 512],
                            wb1[:, (kc * 8 + oc) * 128:(kc * 8 + oc) * 128 + 128],
                            xnv[:, kc, 3 + tg * 512:3 + (tg + 1) * 512],
                            start=(kc == 0), stop=(kc == 7))
                c.inc("ffnCt_h", h, 1)

            @block.scalar
            def _(scalar, e=e, oc=oc, slot=slot):
                c.wait(scalar, "ffnCt_h", 8 * e + oc + 1)
                c.inc("ffnCs_h", scalar.activation(
                    out=hev[:, oc, :], in_=ps_h[:, slot:slot + HW], func=AF.Silu), 1)

        @block.vector
        def _(vector, e=e):
            c.wait(vector, "ffnCs_h", 8 * (e + 1))
            h = vector.tensor_tensor(
                out=hev[:, :, :], in0=hev[:, :, :],
                in1=cbc[:].unsqueeze(1).broadcast_to((128, 8, TC)),
                op=OP.mult)
            c.inc("ffnCv_sc", h, 1)

        for oc in range(8):
            slot = (oc % 2) * HW

            @block.tensor
            def _(tensor, e=e, oc=oc, wb2=wb2, slot=slot):
                c.wait(tensor, "ffnCv_sc", e + 1)
                c.wait(tensor, "ffnCgw2", 16 * (e + 1))
                c.wait(tensor, "ffnCv_a", max(0, 8 * e + oc - 1))
                for kc in range(8):
                    for tg in range(NT5):
                        h = tensor.matmul(
                            ps_h[:, slot + tg * 512:slot + (tg + 1) * 512],
                            wb2[:, (kc * 8 + oc) * 128:(kc * 8 + oc) * 128 + 128],
                            hev[:, kc, tg * 512:(tg + 1) * 512],
                            start=(kc == 0), stop=(kc == 7))
                c.inc("ffnCt_a", h, 1)

            @block.vector
            def _(vector, e=e, oc=oc, slot=slot):
                c.wait(vector, "ffnCt_a", 8 * e + oc + 1)
                if e == 0:
                    h = vector.tensor_copy(out=gv[:, oc, :], in_=ps_h[:, slot:slot + HW])
                else:
                    h = vector.tensor_add(out=gv[:, oc, :], in0=gv[:, oc, :],
                                          in1=ps_h[:, slot:slot + HW])
                c.inc("ffnCv_a", h, 1)

        @block.tensor
        def _(tensor, e=e):
            c.wait(tensor, "ffnCv_a", 8 * (e + 1))
            c.inc("ffnCt_done", tensor.nop(), 1)
            c.inc("ffnCt_w2done", tensor.nop(), 1)

    barrier(c, block, sems=["ffnCv_a", "ffnCs_h", "ffnCt_a", "ffnCt_cb",
                            "ffnCv_cb", "ffnCv_sc", "ffnCt_h"],
            release=["ffnC", "ffnAv"])
    stC.close()

    stD = ExitStack()
    sgu_b = c.sb("ffn_sgu", [128, 16 * 1024], BF16, stD)
    sdn_b = c.sb("ffn_sdn", [128, 8 * 1024], BF16, stD)
    gate = c.sb("ffn_gate", [128, 8 * 512], BF16, stD)
    up = c.sb("ffn_up", [128, 8 * 512], BF16, stD)
    ps_h2 = c.ps("ffn_ps_h2", [128, 2 * 512], F32, stD)
    ps_a2 = c.ps("ffn_ps_a2", [128, 4 * 512], F32, stD)
    gatev = gate[:].rearrange("p (o t) -> p o t", o=8)
    upv = up[:].rearrange("p (o t) -> p o t", o=8)

    @block.gpsimd
    def _(gpsimd):
        c.inc("ffnDg", gpsimd.dma_start(
            out=sgu_b[:].rearrange("p (k f) -> p k f", k=8),
            in_=wt_kpf(wdram["F0_sgu"])), 16)
        c.inc("ffnDg", gpsimd.dma_start(
            out=sdn_b[:].rearrange("p (k f) -> p k f", k=8),
            in_=wt_kpf(wdram["F0_sdown"])), 16)

    for tg in range(NT5):
        t0 = 3 + tg * 512
        for oc in range(16):
            @block.tensor
            def _(tensor, tg=tg, oc=oc, t0=t0):
                c.wait(tensor, "ffnDg", 32)
                k = 16 * tg + oc
                if k >= 2:
                    kp = k - 2
                    ns = 8 * (kp // 16) + min(kp % 16 + 1, 8)
                    nv = 8 * (kp // 16) + max(kp % 16 + 1 - 8, 0)
                    if ns > 0:
                        c.wait(tensor, "ffnDs_g", ns)
                    if nv > 0:
                        c.wait(tensor, "ffnDv_u", nv)
                for kc in range(8):
                    h = tensor.matmul(
                        ps_h2[:, (oc % 2) * 512:(oc % 2) * 512 + 512],
                        sgu_b[:, (kc * 16 + oc) * 128:(kc * 16 + oc) * 128 + 128],
                        xnv[:, kc, t0:t0 + 512],
                        start=(kc == 0), stop=(kc == 7))
                c.inc("ffnDt", h, 1)

            if oc < 8:
                @block.scalar
                def _(scalar, tg=tg, oc=oc):
                    c.wait(scalar, "ffnDt", 16 * tg + oc + 1)
                    c.inc("ffnDs_g", scalar.activation(
                        out=gatev[:, oc, :], in_=ps_h2[:, (oc % 2) * 512:(oc % 2) * 512 + 512],
                        func=AF.Silu), 1)
            else:
                @block.vector
                def _(vector, tg=tg, oc=oc):
                    c.wait(vector, "ffnDt", 16 * tg + oc + 1)
                    c.inc("ffnDv_u", vector.tensor_copy(
                        out=upv[:, oc - 8, :],
                        in_=ps_h2[:, (oc % 2) * 512:(oc % 2) * 512 + 512]), 1)

        @block.vector
        def _(vector, tg=tg):
            c.wait(vector, "ffnDs_g", 8 * (tg + 1))
            c.wait(vector, "ffnDv_u", 8 * (tg + 1))
            c.seq(vector, vector.tensor_tensor(out=gate[:, :], in0=gate[:, :],
                                               in1=up[:, :], op=OP.mult), "V")
            h = vector.tensor_tensor(
                out=gatev[:, :, :], in0=gatev[:, :, :],
                in1=sgbc[:, tg * 512:(tg + 1) * 512].unsqueeze(1)
                    .broadcast_to((128, 8, 512)),
                op=OP.mult)
            c.seq(vector, h, "V")
            c.inc("ffnDv_su", vector.engine_nop(), 1)

        for oc in range(8):
            @block.tensor
            def _(tensor, tg=tg, oc=oc):
                c.wait(tensor, "ffnDv_su", tg + 1)
                k = 8 * tg + oc
                if k >= 4:
                    c.wait(tensor, "ffnDv_a", k - 3)
                for kc in range(8):
                    h = tensor.matmul(
                        ps_a2[:, (oc % 4) * 512:(oc % 4) * 512 + 512],
                        sdn_b[:, (kc * 8 + oc) * 128:(kc * 8 + oc) * 128 + 128],
                        gatev[:, kc, :],
                        start=(kc == 0), stop=(kc == 7))
                c.inc("ffnDt_a", h, 1)

            @block.vector
            def _(vector, tg=tg, oc=oc):
                c.wait(vector, "ffnDt_a", 8 * tg + oc + 1)
                h = vector.tensor_add(
                    out=gv[:, oc, tg * 512:(tg + 1) * 512],
                    in0=gv[:, oc, tg * 512:(tg + 1) * 512],
                    in1=ps_a2[:, (oc % 4) * 512:(oc % 4) * 512 + 512])
                c.inc("ffnDv_a", h, 1)

    barrier(c, block, sems=["ffnDv_a", "ffnDt_a", "ffnDs_g", "ffnDv_u",
                            "ffnDv_su", "ffnDt"], release=["ffnD"])
    stD.close()

    for dc in range(8):
        @block.scalar
        def _(scalar, dc=dc):
            h = scalar.activation(out=gv[:, dc, :], in_=gv[:, dc, :], func=AF.Silu)
            c.inc("ffnEs", h, 1)
    barrier(c, block, sems=["ffnEs"], release=["ffnE"])
    stP.close()



def ph_attn0(c, block, x, xn, g, wdram, ones128, ones_row, ident, sel33, chm, ccs):
    nc = c.nc
    TT5 = 512
    NTT = TC // TT5
    xv = x[:].rearrange("p (d t) -> p d t", d=8)
    xnv = xn[:].rearrange("p (d t) -> p d t", d=8)
    gv = g[:].rearrange("p (d t) -> p d t", d=8)
    cc0_in, cc0_out = ccs[0], ccs[1]

    stP = ExitStack()
    As = c.sb("a0_As", [128, 16 * S0], F32, stP)
    convw = c.sb("a0_convw", [128, 16 * 4], F32, stP)
    convb = c.sb("a0_convb", [128, 16], F32, stP)
    wdtc = c.sb("a0_wdt", [128, 16], F32, stP)
    bdtc = c.sb("a0_bdt", [128, 16], F32, stP)
    dpc = c.sb("a0_dp", [128, 16], F32, stP)
    wxp = c.sb("a0_wxp", [128, 16 * 33], BF16, stP)
    carry = c.sb("a0_carry", [128, 16 * S0], F32, stP)
    cdel = c.sb("a0_cdel", [128, 16], F32, stP)
    dsacc = c.sb("a0_dsacc", [128, 16 * 4], F32, stP)
    xdblk = c.sb("a0_xdbl", [33, TC], BF16, stP)
    Hbuf = c.sb("a0_H", [128, 16 * S0], F32, stP)
    Pacc = c.sb("a0_Pacc", [128, 16 * S0], F32, stP)
    Prun = c.sb("a0_Prun", [128, 16 * S0], F32, stP)
    tcmb = c.sb("a0_tcmb", [128, 16 * S0], F32, stP)
    dsum = c.sb("a0_dsum", [128, 16], F32, stP)

    y2t = c.sb("a0_y2", [128, 16 * TT5], BF16, stP)
    winb = [c.sb(f"a0_win{i}", [128, 8 * 128], BF16, stP) for i in range(2)]
    woutb = [c.sb(f"a0_wout{i}", [128, 16 * 128], BF16, stP) for i in range(2)]
    xd0bc = c.sb("a0_xd0bc", [128, TT5], BF16, stP)
    vpb = c.sb("a0_vpb", [128, TT5], F32, stP)
    schb = [c.sb(f"a0_sch{i}", [128, TT5], F32, stP) for i in range(2)]
    delta = c.sb("a0_delta", [128, TT5], BF16, stP)
    gb = c.sb("a0_gb", [128, 2 * TT5], BF16, stP)
    zs = c.sb("a0_zs", [128, TT5], BF16, stP)
    cum = c.sb("a0_cum", [128, TT5], F32, stP)
    ycor = c.sb("a0_ycor", [128, TT5], BF16, stP)
    zex = c.sb("a0_zex", [128, TT5], F32, stP)
    agj = c.sb("a0_agj", [128, 2 * 16 * S0], F32, stP)
    ones512 = c.sb("a0_ones512", [128, TT5], BF16, stP)

    stPA = ExitStack()
    xcp = c.sb("a0_xcp", [128, 16 * (TT5 + 3)], BF16, stPA)
    ctx3 = c.sb("a0_ctx", [128, 16 * 3], BF16, stPA)
    t10 = c.sb("a0_t1", [128, 2 * TT5], BF16, stPA)
    wdc = c.sb("a0_wdc", [128, TT5], BF16, stPA)
    dab = [c.sb(f"a0_dab{i}", [128, 2 * (TT5 + 1)], BF16, stPA) for i in range(2)]
    ubb0 = c.sb("a0_ub", [128, 4 * (TT5 + 1)], BF16, stPA)
    hbb0 = c.sb("a0_hb", [128, 4 * (TT5 + 1)], BF16, stPA)
    ydc = c.sb("a0_ydc", [128, TT5], BF16, stPA)

    ps_a = c.ps("a0_ps_a", [128, 2 * TT5], F32, stP)
    ps_xd = c.ps("a0_ps_xd", [33, TT5], F32, stP)
    ps_c2 = c.ps("a0_ps_c2", [128, 2 * TT5], F32, stP)
    ps_bc = c.ps("a0_ps_bc", [128, 2 * TT5], F32, stP)

    xcpv = xcp[:].rearrange("p (k t) -> p k t", k=16)
    y2v = y2t[:].rearrange("p (k t) -> p k t", k=16)

    @block.gpsimd
    def _(gpsimd):
        c.inc("a0g", gpsimd.dma_start(out=As[:].rearrange("p (k s) -> p k s", k=16),
                                      in_=wt_kpf(wdram["L0_A_log"])), 16)
        c.inc("a0g", gpsimd.dma_start(out=convw[:].rearrange("p (k s) -> p k s", k=16),
                                      in_=wt_kpf(wdram["L0_conv_w"])), 16)
        for nm, dst in [("L0_conv_b", convb), ("L0_W_dt", wdtc),
                        ("L0_b_dt", bdtc), ("L0_Dp", dpc)]:
            c.inc("a0g", gpsimd.dma_start(
                out=dst[:], in_=wdram[nm].rearrange("(k p) -> k p", p=128)
                .transpose([1, 0]), allow_slow_non_contiguous=True), 16)
        c.inc("a0g", gpsimd.dma_start(out=wxp[:].rearrange("p (k s) -> p k s", k=16),
                                      in_=wt_kpf(wdram["L0_W_xproj"])), 16)
        gpsimd.memset(carry[:], 0.0)
        gpsimd.memset(cdel[:], 0.0)
        gpsimd.memset(dab[0][:], 0.0)
        gpsimd.memset(dab[1][:], 0.0)
        c.inc("a0g2", gpsimd.memset(ones512[:], 1.0), 1)

    @block.scalar
    def _(scalar):
        c.wait(scalar, "a0g", c.val("a0g"))
        c.wait(scalar, "a0g2", 1)
        c.table(scalar, NLEXP)
        c.inc("a0s0", scalar.activation(out=As[:], in_=As[:], func=AF.Exp), 1)

    @block.vector
    def _(vector):
        c.wait(vector, "a0s0", 1)
        c.seq(vector, vector.tensor_scalar_mul(out=As[:], in0=As[:], scalar1=-1.0), "V")
        c.inc("a0v0", vector.tensor_scalar_mul(out=convb[:], in0=convb[:],
                                               scalar1=-1.0), 1)

    def tail_z_y2(tt, dc, M):
        tb = 3 + tt * TT5
        k = 16 * tt + dc

        @block.gpsimd
        def _(gpsimd, tt=tt, dc=dc, k=k):
            if M == "A" and dc < 2:
                c.wait(gpsimd, "a0t_xz", 16 * (tt + 1))
            if k >= 2:
                c.wait(gpsimd, f"a0t_{M}z", k - 1)
            c.inc(f"a0g_{M}z{dc % 2}", gpsimd.dma_start(
                out=winb[dc % 2][:].rearrange("p (k f) -> p k f", k=8),
                in_=wt_kpf(wdram["L0_W_in"])[:, :, 2048 + dc * 128:2048 + (dc + 1) * 128]),
                16)

        @block.tensor
        def _(tensor, tt=tt, dc=dc, k=k):
            c.wait(tensor, f"a0g_{M}z{dc % 2}", 16 * (k // 2 + 1))
            if M == "A":
                c.wait(tensor, "a0v_xz", 16 * (tt + 1))
            if k >= 2:
                c.wait(tensor, f"a0v_{M}z", k - 1)
            slot = (dc % 2) * TT5
            for kc in range(8):
                h = tensor.matmul(ps_a[:, slot:slot + TT5],
                                  winb[dc % 2][:, kc * 128:(kc + 1) * 128],
                                  xnv[:, kc, tb:tb + TT5],
                                  start=(kc == 0), stop=(kc == 7))
            c.inc(f"a0t_{M}z", h, 1)

        @block.scalar
        def _(scalar, tt=tt, dc=dc, k=k):
            c.wait(scalar, f"a0t_{M}z", k + 1)
            if k >= 1:
                c.wait(scalar, f"a0v_{M}z", k)
            slot = (dc % 2) * TT5
            c.inc(f"a0s_{M}z", scalar.activation(
                out=zex[:, :], in_=ps_a[:, slot:slot + TT5], func=AF.Exp,
                scale=-1.0), 1)

        @block.vector
        def _(vector, tt=tt, dc=dc, k=k):
            c.wait(vector, f"a0s_{M}z", k + 1)
            slot = (dc % 2) * TT5
            c.seq(vector, vector.tensor_copy(out=zs[:, :],
                                             in_=ps_a[:, slot:slot + TT5]), "V")
            c.seq(vector, vector.tensor_scalar(
                out=zex[:, :], in0=zex[:, :], scalar1=1.0, scalar2=None,
                op0=OP.add), "V")
            c.seq(vector, vector.reciprocal(out=zex[:, :], in_=zex[:, :]), "V")
            c.seq(vector, vector.tensor_mul(out=zs[:, :], in0=zs[:, :],
                                            in1=zex[:, :]), "V")
            if M == "A":
                c.seq(vector, vector.scalar_tensor_tensor(
                    out=gb[:, 0:TT5], in0=xcpv[:, dc, 3:3 + TT5],
                    scalar=dpc[:, dc:dc + 1], in1=ydc[:, :],
                    op0=OP.mult, op1=OP.add), "V")
                h = vector.tensor_mul(out=y2v[:, dc, :], in0=gb[:, 0:TT5], in1=zs[:, :])
            else:
                h = vector.tensor_mul(out=y2v[:, dc, :], in0=ycor[:, :], in1=zs[:, :])
            nn = c.inc(f"a0v_{M}z", h, 1)
            c.wait(vector, f"a0v_{M}z", nn)

    def tail_wout_x(tt, M):
        for oc in range(8):
            k = 8 * tt + oc

            @block.gpsimd
            def _(gpsimd, tt=tt, oc=oc, k=k):
                if k >= 2:
                    c.wait(gpsimd, f"a0t_{M}o", k - 1)
                c.inc(f"a0g_{M}o{oc % 2}", gpsimd.dma_start(
                    out=woutb[oc % 2][:].rearrange("p (k f) -> p k f", k=16),
                    in_=wt_kpf(wdram["L0_W_out"])[:, :, oc * 128:(oc + 1) * 128]), 16)

            @block.tensor
            def _(tensor, tt=tt, oc=oc, k=k):
                c.wait(tensor, f"a0g_{M}o{oc % 2}", 16 * (k // 2 + 1))
                c.wait(tensor, f"a0v_{M}z", 16 * (tt + 1))
                if k >= 2:
                    c.wait(tensor, f"a0v_{M}o2", k - 1)
                slot = (oc % 2) * TT5
                for dc in range(16):
                    h = tensor.matmul(ps_a[:, slot:slot + TT5],
                                      woutb[oc % 2][:, dc * 128:(dc + 1) * 128],
                                      y2v[:, dc, :],
                                      start=(dc == 0), stop=(dc == 15))
                c.inc(f"a0t_{M}o", h, 1)

            @block.vector
            def _(vector, tt=tt, oc=oc, k=k):
                c.wait(vector, f"a0t_{M}o", k + 1)
                slot = (oc % 2) * TT5
                tb = 3 + tt * TT5
                c.seq(vector, vector.tensor_mul(
                    out=gb[:, TT5:2 * TT5],
                    in0=gv[:, oc, tt * TT5:(tt + 1) * TT5],
                    in1=ps_a[:, slot:slot + TT5]), "V")
                h = vector.tensor_add(out=xv[:, oc, tb:tb + TT5],
                                      in0=xv[:, oc, tb:tb + TT5],
                                      in1=gb[:, TT5:2 * TT5])
                nn = c.inc(f"a0v_{M}o2", h, 1)
                c.wait(vector, f"a0v_{M}o2", nn)

    for tt in range(NTT):
        tb = 3 + tt * TT5

        for oc in range(16):
            wb = winb[oc % 2]
            k = 16 * tt + oc

            @block.gpsimd
            def _(gpsimd, tt=tt, oc=oc, wb=wb, k=k):
                if tt > 0 and oc < 2:
                    c.wait(gpsimd, "a0t_Az", 16 * tt)
                if k >= 2:
                    c.wait(gpsimd, "a0t_xz", k - 1)
                c.inc(f"a0g_w{oc % 2}", gpsimd.dma_start(
                    out=wb[:].rearrange("p (k f) -> p k f", k=8),
                    in_=wt_kpf(wdram["L0_W_in"])[:, :, oc * 128:(oc + 1) * 128]), 16)

            @block.tensor
            def _(tensor, tt=tt, oc=oc, wb=wb, tb=tb, k=k):
                c.wait(tensor, f"a0g_w{oc % 2}", 16 * (k // 2 + 1))
                if k >= 2:
                    c.wait(tensor, "a0v_xz", k - 1)
                if tt == 0 and oc == 0:
                    c.wait(tensor, "a0v0", 1)
                if tt > 0 and oc == 0:
                    c.wait(tensor, "a0v_ctx", tt)
                if tt > 0 and oc < 2:
                    c.wait(tensor, "a0v_Ao2", 8 * tt)
                slot = (oc % 2) * TT5
                for kc in range(8):
                    h = tensor.matmul(ps_a[:, slot:slot + TT5],
                                      wb[:, kc * 128:(kc + 1) * 128],
                                      xnv[:, kc, tb:tb + TT5],
                                      start=(kc == 0), stop=(kc == 7))
                if tt == 0:
                    for kc in range(8):
                        h = tensor.matmul(ps_bc[:, slot:slot + 3],
                                          wb[:, kc * 128:(kc + 1) * 128],
                                          xnv[:, kc, 0:3],
                                          start=(kc == 0), stop=(kc == 7))
                c.inc("a0t_xz", h, 1)

            @block.vector
            def _(vector, tt=tt, oc=oc, k=k):
                c.wait(vector, "a0t_xz", k + 1)
                slot = (oc % 2) * TT5
                if tt == 0:
                    c.seq(vector, vector.tensor_copy(out=xcpv[:, oc, 0:3],
                                                     in_=ps_bc[:, slot:slot + 3]), "V")
                else:
                    c.seq(vector, vector.tensor_copy(
                        out=xcpv[:, oc, 0:3],
                        in_=ctx3[:].rearrange("p (k t) -> p k t", k=16)[:, oc, :]), "V")
                h = vector.tensor_copy(out=xcpv[:, oc, 3:3 + TT5],
                                       in_=ps_a[:, slot:slot + TT5])
                c.inc("a0v_xz", h, 1)

        @block.vector
        def _(vector, tt=tt):
            c.wait(vector, "a0v_xz", 16 * (tt + 1))
            h = vector.tensor_copy(out=ctx3[:].rearrange("p (k t) -> p k t", k=16),
                                   in_=xcpv[:, :, TT5:TT5 + 3])
            nn = c.inc("a0v_ctx", h, 1)
            c.wait(vector, "a0v_ctx", nn)

        for dc in range(16):
            k = 16 * tt + dc

            @block.vector
            def _(vector, tt=tt, dc=dc, k=k):
                if k >= 2:
                    c.wait(vector, "a0s_cv", k - 1)
                sch = schb[dc % 2]
                c.seq(vector, vector.tensor_scalar(
                    out=sch[:, :], in0=xcpv[:, dc, 0:TT5],
                    scalar1=convw[:, dc * 4:dc * 4 + 1], scalar2=None,
                    op0=OP.mult), "V")
                for j in range(1, 4):
                    h = vector.scalar_tensor_tensor(
                        out=sch[:, :], in0=xcpv[:, dc, j:j + TT5],
                        scalar=convw[:, dc * 4 + j:dc * 4 + j + 1],
                        in1=sch[:, :], op0=OP.mult, op1=OP.add)
                    if j < 3:
                        c.seq(vector, h, "V")
                c.inc("a0v_cv", h, 1)

            @block.scalar
            def _(scalar, tt=tt, dc=dc, k=k):
                c.wait(scalar, "a0v_cv", k + 1)
                if k >= 1:
                    c.wait(scalar, "a0v_cv2", k)
                h = scalar.activation(out=vpb[:, :], in_=schb[dc % 2][:, :],
                                      func=AF.Exp, scale=-1.0, bias=convb[:, dc:dc + 1])
                c.inc("a0s_cv", h, 1)

            @block.vector
            def _(vector, tt=tt, dc=dc, k=k):
                c.wait(vector, "a0s_cv", k + 1)
                c.seq(vector, vector.tensor_scalar(
                    out=vpb[:, :], in0=vpb[:, :], scalar1=1.0, scalar2=None,
                    op0=OP.add), "V")
                c.seq(vector, vector.reciprocal(out=vpb[:, :], in_=vpb[:, :]), "V")
                c.seq(vector, vector.tensor_scalar(
                    out=gb[:, 0:TT5], in0=schb[dc % 2][:, :],
                    scalar1=convb[:, dc:dc + 1], scalar2=None, op0=OP.subtract), "V")
                h = vector.tensor_mul(out=xcpv[:, dc, 3:3 + TT5], in0=gb[:, 0:TT5],
                                      in1=vpb[:, :])
                c.inc("a0v_cv2", h, 1)

        @block.tensor
        def _(tensor, tt=tt):
            c.wait(tensor, "a0v_cv2", 16 * (tt + 1))
            if tt >= 1:
                c.wait(tensor, "a0v_xd", tt)
            for kc in range(16):
                h = tensor.matmul(ps_xd[:, :], wxp[:, kc * 33:(kc + 1) * 33],
                                  xcpv[:, kc, 3:3 + TT5],
                                  start=(kc == 0), stop=(kc == 15))
            c.inc("a0t_xd", h, 1)

        @block.vector
        def _(vector, tt=tt):
            c.wait(vector, "a0t_xd", tt + 1)
            c.inc("a0v_xd", vector.tensor_copy(
                out=xdblk[:, tt * TT5:(tt + 1) * TT5], in_=ps_xd[:, :]), 1)

        @block.tensor
        def _(tensor, tt=tt):
            c.wait(tensor, "a0v_xd", tt + 1)
            if tt >= 1:
                c.wait(tensor, "a0v_xd0", tt)
                c.wait(tensor, "a0v_u", 128 * tt)
            c.inc("a0t_xd0", tensor.matmul(
                ps_bc[:, TT5:2 * TT5], sel33[:, 0:128],
                xdblk[:, tt * TT5:(tt + 1) * TT5], start=True, stop=True), 1)

        @block.vector
        def _(vector, tt=tt):
            c.wait(vector, "a0t_xd0", tt + 1)
            c.inc("a0v_xd0", vector.tensor_copy(out=xd0bc[:, :],
                                                in_=ps_bc[:, TT5:2 * TT5]), 1)

        for dc in range(16):
            k = 16 * tt + dc

            @block.vector
            def _(vector, tt=tt, dc=dc, k=k):
                c.wait(vector, "a0v_xd0", tt + 1)
                if k >= 1:
                    c.wait(vector, "a0s_dl", 2 * k)
                h = vector.tensor_scalar(
                    out=vpb[:, :], in0=xd0bc[:, :],
                    scalar1=wdtc[:, dc:dc + 1], scalar2=bdtc[:, dc:dc + 1],
                    op0=OP.mult, op1=OP.add)
                c.inc("a0v_dl1", h, 1)

            @block.scalar
            def _(scalar, tt=tt, dc=dc, k=k):
                c.wait(scalar, "a0v_dl1", k + 1)
                if k >= 1:
                    c.wait(scalar, "a0v_dl3", k)
                c.seq(scalar, scalar.activation(out=schb[0][:, :], in_=vpb[:, :],
                                                func=AF.Abs), "S")
                c.seq(scalar, scalar.activation(out=schb[0][:, :], in_=schb[0][:, :],
                                                func=AF.Exp, scale=-1.0), "S")
                h = scalar.activation(out=schb[1][:, :], in_=schb[0][:, :],
                                      func=AF.Ln, bias=1.0)
                c.inc("a0s_dl", h, 2)

            @block.vector
            def _(vector, tt=tt, dc=dc, k=k):
                c.wait(vector, "a0s_dl", 2 * (k + 1))
                if k >= 1:
                    c.wait(vector, "a0s_da", 8 * k)
                h = vector.scalar_tensor_tensor(
                    out=delta[:, :], in0=vpb[:, :], scalar=0.0, in1=schb[1][:, :],
                    op0=OP.max, op1=OP.add,
                    accum_out=dsacc[:, tt * 16 + dc:tt * 16 + dc + 1])
                nn = c.inc("a0v_dl2", h, 1)
                c.wait(vector, "a0v_dl2", nn)
                h = vector.tensor_mul(out=wdc[:, :], in0=delta[:, :],
                                      in1=xcpv[:, dc, 3:3 + TT5])
                nn = c.inc("a0v_dl3", h, 1)
                c.wait(vector, "a0v_dl3", nn)

            for sb in range(4):
                bidx = (16 * tt + dc) * 4 + sb

                @block.tensor
                def _(tensor, tt=tt, dc=dc, sb=sb, bidx=bidx):
                    c.wait(tensor, "a0v_dl3", 16 * tt + dc + 1)
                    if bidx >= 1:
                        c.wait(tensor, "a0v_u", 2 * bidx)
                        c.wait(tensor, "a0v_gb", bidx)
                    for q4 in range(4):
                        s = 4 * sb + q4
                        dst = ps_bc if q4 < 2 else ps_c2
                        h = tensor.matmul(dst[:, (q4 % 2) * TT5:(q4 % 2 + 1) * TT5],
                                          sel33[:, (1 + s) * 128:(2 + s) * 128],
                                          xdblk[:, tt * TT5:(tt + 1) * TT5],
                                          start=True, stop=True)
                    c.inc("a0t_B", h, 1)

                @block.tensor
                def _(tensor, tt=tt, dc=dc, sb=sb, bidx=bidx):
                    c.wait(tensor, "a0v_u", 2 * (bidx + 1))
                    for q4 in range(4):
                        s = 4 * sb + q4
                        dst = ps_bc if q4 < 2 else ps_c2
                        h = tensor.matmul(dst[:, (q4 % 2) * TT5:(q4 % 2 + 1) * TT5],
                                          sel33[:, (17 + s) * 128:(18 + s) * 128],
                                          xdblk[:, tt * TT5:(tt + 1) * TT5],
                                          start=True, stop=True)
                    c.inc("a0t_C", h, 1)

                @block.scalar
                def _(scalar, tt=tt, dc=dc, sb=sb, bidx=bidx):
                    c.wait(scalar, "a0v_dl2", 16 * tt + dc + 1)
                    if bidx >= 1:
                        c.wait(scalar, "a0v_sc2", bidx)
                    for half in range(2):
                        dav = dab[half][:].rearrange("p (a b) -> p a b", a=2)
                        for a in range(2):
                            s = 4 * sb + 2 * half + a
                            h = scalar.activation(
                                out=dav[:, a, 1:1 + TT5], in_=delta[:, :],
                                func=AF.Exp,
                                scale=As[:, dc * S0 + s:dc * S0 + s + 1])
                            if a == 0:
                                c.seq(scalar, h, "S")
                        c.inc("a0s_da", h, 1)

                @block.vector
                def _(vector, tt=tt, dc=dc, sb=sb, bidx=bidx):
                    c.wait(vector, "a0t_B", bidx + 1)
                    ubv = ubb0[:].rearrange("p (a b) -> p a b", a=4)
                    h = vector.tensor_tensor(
                        out=ubv[:, 0:2, 1:1 + TT5],
                        in0=wdc[:].unsqueeze(1).broadcast_to((128, 2, TT5)),
                        in1=ps_bc[:].rearrange("p (a b) -> p a b", a=2),
                        op=OP.mult)
                    c.inc("a0v_u", h, 1)
                    h = vector.tensor_tensor(
                        out=ubv[:, 2:4, 1:1 + TT5],
                        in0=wdc[:].unsqueeze(1).broadcast_to((128, 2, TT5)),
                        in1=ps_c2[:].rearrange("p (a b) -> p a b", a=2),
                        op=OP.mult)
                    c.inc("a0v_u", h, 1)
                    c.seq(vector, vector.tensor_copy(
                        out=ubv[:, :, 0:1],
                        in_=carry[:, dc * S0 + 4 * sb:dc * S0 + 4 * sb + 4]
                        .rearrange("p (a b) -> p a b", a=4)), "V")
                    c.wait(vector, "a0s_da", 2 * bidx + 1)
                    c.seq(vector, vector.tensor_tensor_scan(
                        out=hbb0[:, 0:2 * (TT5 + 1)], data0=dab[0][:, :],
                        data1=ubb0[:, 0:2 * (TT5 + 1)],
                        initial=0.0, op0=OP.mult, op1=OP.add), "V")
                    c.wait(vector, "a0s_da", 2 * bidx + 2)
                    h = vector.tensor_tensor_scan(
                        out=hbb0[:, 2 * (TT5 + 1):4 * (TT5 + 1)], data0=dab[1][:, :],
                        data1=ubb0[:, 2 * (TT5 + 1):4 * (TT5 + 1)],
                        initial=0.0, op0=OP.mult, op1=OP.add)
                    c.inc("a0v_sc2", h, 1)
                    hbv = hbb0[:].rearrange("p (a b) -> p a b", a=4)
                    c.wait(vector, "a0t_C", bidx + 1)
                    c.seq(vector, vector.tensor_tensor(
                        out=gb[:].rearrange("p (a b) -> p a b", a=2),
                        in0=hbv[:, 0:2, 1:1 + TT5],
                        in1=ps_bc[:].rearrange("p (a b) -> p a b", a=2),
                        op=OP.mult), "V")
                    c.seq(vector, vector.tensor_tensor(
                        out=t10[:].rearrange("p (a b) -> p a b", a=2),
                        in0=hbv[:, 2:4, 1:1 + TT5],
                        in1=ps_c2[:].rearrange("p (a b) -> p a b", a=2),
                        op=OP.mult), "V")
                    h = vector.tensor_add(out=t10[:, :], in0=t10[:, :],
                                          in1=gb[:, 0:2 * TT5])
                    c.inc("a0v_gb", h, 1)
                    c.seq(vector, vector.tensor_copy(
                        out=carry[:, dc * S0 + 4 * sb:dc * S0 + 4 * sb + 4]
                        .rearrange("p (a b) -> p a b", a=4),
                        in_=hbv[:, :, TT5:TT5 + 1]), "V")
                    if sb == 0:
                        h = vector.tensor_add(out=ydc[:, :], in0=t10[:, 0:TT5],
                                              in1=t10[:, TT5:2 * TT5])
                    else:
                        c.seq(vector, vector.tensor_add(
                            out=ydc[:, :], in0=ydc[:, :], in1=t10[:, 0:TT5]), "V")
                        h = vector.tensor_add(out=ydc[:, :], in0=ydc[:, :],
                                              in1=t10[:, TT5:2 * TT5])
                    nn = c.inc("a0v_scan", h, 1)
                    c.wait(vector, "a0v_scan", nn)

            tail_z_y2(tt, dc, "A")

        tail_wout_x(tt, "A")

    barrier(c, block, sems=["a0v_Az", "a0v_Ao2", "a0t_Ao", "a0s_Az", "a0v_scan",
                            "a0s_da", "a0t_B", "a0t_C", "a0v_u", "a0v_gb",
                            "a0v_sc2", "a0s_cv", "a0v_cv",
                            "a0v_xz", "a0t_xz", "a0v_dl2", "a0v_dl3", "a0s_dl", "a0v_cv2"],
            release=["a0g_w", "a0t_xz", "a0v_xz", "a0v_ctx", "a0v_cv", "a0s_cv", "a0v_cv2",
                     "a0t_xd", "a0v_xd", "a0v_dl", "a0s_dl", "a0t_B", "a0t_C",
                     "a0v_u", "a0v_gb", "a0v_sc2",
                     "a0s_da", "a0v_scan", "a0g_Az", "a0t_Az", "a0s_Az", "a0v_Az",
                     "a0g_Ao", "a0t_Ao", "a0v_Ao2"])
    stPA.close()
    cbcall = c.sb("a0_cbcall", [128, 16 * TT5], BF16, stP)
    cbav = cbcall[:].rearrange("p (s t) -> p s t", s=16)

    @block.vector
    def _(vector):
        c.seq(vector, vector.tensor_copy(out=dsum[:, :], in_=dsacc[:, 0:16]), "V")
        for _tt in range(1, NTT):
            c.seq(vector, vector.tensor_add(
                out=dsum[:, :], in0=dsum[:, :],
                in1=dsacc[:, _tt * 16:(_tt + 1) * 16]), "V")
        for dc in range(16):
            h = vector.tensor_scalar(
                out=Pacc[:, dc * S0:(dc + 1) * S0],
                in0=As[:, dc * S0:(dc + 1) * S0],
                scalar1=dsum[:, dc:dc + 1], scalar2=None, op0=OP.mult)
        c.inc("a0v_dec1", h, 1)

    @block.scalar
    def _(scalar):
        c.wait(scalar, "a0v_dec1", 1)
        c.inc("a0s_dec", scalar.activation(out=Pacc[:], in_=Pacc[:], func=AF.Exp), 1)

    @block.sync
    def _(sync):
        c.wait(sync, "a0s_dec", 1)
        c.inc("a0q_cc", sync.dma_start(out=cc0_in[:, 0:256], in_=Pacc[:]), 16)
        c.inc("a0q_cc", sync.dma_start(out=cc0_in[:, 256:512], in_=carry[:]), 16)

    @block.gpsimd
    def _(gpsimd):
        c.wait(gpsimd, "a0q_cc", 32)
        c.inc("a0g_cc", gpsimd.collective_compute(
            "AllGather", OP.bypass, ins=[cc0_in], outs=[cc0_out],
            replica_groups=[[0, 1, 2, 3], [4, 5, 6, 7]]), 1)

    @block.gpsimd
    def _(gpsimd):
        c.wait(gpsimd, "a0g_cc", 1)
        gpsimd.memset(Hbuf[:], 0.0)
        c.inc("a0g_hm", gpsimd.memset(Prun[:], 1.0), 1)

    for j in [2, 1, 0]:
        @block.sync
        def _(sync, j=j):
            c.wait(sync, "a0g_cc", 1)
            if j < 2:
                c.wait(sync, "a0v_cmb", 2 - j)
            c.inc("a0q_ag", sync.dma_start(
                out=agj[:], in_=cc0_out[j * 128:(j + 1) * 128, :]), 16)

        @block.vector
        def _(vector, j=j):
            c.wait(vector, "a0q_ag", 16 * (3 - j))
            c.wait(vector, "a0g_hm", 1)
            m = chm[:, j:j + 1]
            c.seq(vector, vector.tensor_mul(out=tcmb[:, :], in0=agj[:, 256:512],
                                            in1=Prun[:, :]), "V")
            c.seq(vector, vector.tensor_scalar(out=tcmb[:, :], in0=tcmb[:, :],
                                               scalar1=m, scalar2=None,
                                               op0=OP.mult), "V")
            c.seq(vector, vector.tensor_add(out=Hbuf[:, :], in0=Hbuf[:, :],
                                            in1=tcmb[:, :]), "V")
            c.seq(vector, vector.tensor_scalar(out=tcmb[:, :], in0=agj[:, 0:256],
                                               scalar1=1.0, scalar2=m,
                                               op0=OP.subtract, op1=OP.mult), "V")
            c.seq(vector, vector.tensor_scalar(out=tcmb[:, :], in0=tcmb[:, :],
                                               scalar1=1.0, scalar2=None,
                                               op0=OP.add), "V")
            h = vector.tensor_mul(out=Prun[:, :], in0=Prun[:, :], in1=tcmb[:, :])
            nn = c.inc("a0v_cmb", h, 1)
            c.wait(vector, "a0v_cmb", nn)

    barrier(c, block, sems=["a0v_cmb"])

    for tt in range(NTT):
        tb = 3 + tt * TT5

        @block.tensor
        def _(tensor, tt=tt):
            if tt >= 1:
                c.wait(tensor, "a0v_xd0B", tt)
                c.wait(tensor, "a0v_cbl", 16 * tt)
            c.inc("a0t_xd0B", tensor.matmul(
                ps_bc[:, 0:TT5], sel33[:, 0:128],
                xdblk[:, tt * TT5:(tt + 1) * TT5], start=True, stop=True), 1)

        @block.vector
        def _(vector, tt=tt):
            c.wait(vector, "a0t_xd0B", tt + 1)
            c.inc("a0v_xd0B", vector.tensor_copy(out=xd0bc[:, :],
                                                 in_=ps_bc[:, 0:TT5]), 1)

        for s in range(16):
            @block.tensor
            def _(tensor, tt=tt, s=s):
                c.wait(tensor, "a0v_xd0B", tt + 1)
                if tt * 16 + s >= 2:
                    c.wait(tensor, "a0v_cbl", 16 * tt + s - 1)
                c.inc("a0t_cbl", tensor.matmul(
                    ps_bc[:, (s % 2) * TT5:(s % 2 + 1) * TT5],
                    sel33[:, (17 + s) * 128:(18 + s) * 128],
                    xdblk[:, tt * TT5:(tt + 1) * TT5], start=True, stop=True), 1)

            @block.vector
            def _(vector, tt=tt, s=s):
                c.wait(vector, "a0t_cbl", 16 * tt + s + 1)
                c.inc("a0v_cbl", vector.tensor_copy(
                    out=cbav[:, s, :], in_=ps_bc[:, (s % 2) * TT5:(s % 2 + 1) * TT5]), 1)

        for dc in range(16):
            k = 16 * tt + dc

            @block.vector
            def _(vector, tt=tt, dc=dc, k=k):
                c.wait(vector, "a0v_cbl", 16 * (tt + 1))
                if k >= 1:
                    c.wait(vector, "a0s_dlB", 2 * k)
                h = vector.tensor_scalar(
                    out=vpb[:, :], in0=xd0bc[:, :],
                    scalar1=wdtc[:, dc:dc + 1], scalar2=bdtc[:, dc:dc + 1],
                    op0=OP.mult, op1=OP.add)
                c.inc("a0v_dl1B", h, 1)

            @block.scalar
            def _(scalar, tt=tt, dc=dc, k=k):
                c.wait(scalar, "a0v_dl1B", k + 1)
                if k >= 1:
                    c.wait(scalar, "a0v_dl2B", k)
                    c.wait(scalar, "a0v_pb", 16 * k)
                c.seq(scalar, scalar.activation(out=schb[0][:, :], in_=vpb[:, :],
                                                func=AF.Abs), "S")
                c.seq(scalar, scalar.activation(out=schb[0][:, :], in_=schb[0][:, :],
                                                func=AF.Exp, scale=-1.0), "S")
                h = scalar.activation(out=schb[1][:, :], in_=schb[0][:, :],
                                      func=AF.Ln, bias=1.0)
                c.inc("a0s_dlB", h, 2)

            @block.vector
            def _(vector, tt=tt, dc=dc, k=k):
                c.wait(vector, "a0s_dlB", 2 * (k + 1))
                if k >= 1:
                    c.wait(vector, "a0s_pb", 16 * k)
                c.seq(vector, vector.scalar_tensor_tensor(
                    out=delta[:, :], in0=vpb[:, :], scalar=0.0, in1=schb[1][:, :],
                    op0=OP.max, op1=OP.add), "V")
                c.seq(vector, vector.tensor_tensor_scan(
                    out=cum[:, :], data0=ones512[:, :], data1=delta[:, :],
                    initial=cdel[:, dc:dc + 1], op0=OP.mult, op1=OP.add), "V")
                h = vector.tensor_copy(out=cdel[:, dc:dc + 1], in_=cum[:, TT5 - 1:TT5])
                nn = c.inc("a0v_dl2B", h, 1)
                c.wait(vector, "a0v_dl2B", nn)

            for s in range(16):
                pidx = (16 * tt + dc) * 16 + s

                @block.scalar
                def _(scalar, tt=tt, dc=dc, s=s, pidx=pidx):
                    c.wait(scalar, "a0v_dl2B", 16 * tt + dc + 1)
                    if pidx >= 2:
                        c.wait(scalar, "a0v_pb", pidx - 1)
                    h = scalar.activation(
                        out=schb[s % 2][:, :], in_=cum[:, :], func=AF.Exp,
                        scale=As[:, dc * S0 + s:dc * S0 + s + 1])
                    c.inc("a0s_pb", h, 1)

                @block.vector
                def _(vector, tt=tt, dc=dc, s=s, pidx=pidx):
                    c.wait(vector, "a0s_pb", pidx + 1)
                    c.seq(vector, vector.tensor_mul(
                        out=gb[:, 0:TT5], in0=schb[s % 2][:, :], in1=cbav[:, s, :]), "V")
                    if s == 0:
                        h = vector.tensor_scalar(
                            out=ycor[:, :], in0=gb[:, 0:TT5],
                            scalar1=Hbuf[:, dc * S0 + s:dc * S0 + s + 1],
                            scalar2=None, op0=OP.mult)
                    else:
                        h = vector.scalar_tensor_tensor(
                            out=ycor[:, :], in0=gb[:, 0:TT5],
                            scalar=Hbuf[:, dc * S0 + s:dc * S0 + s + 1],
                            in1=ycor[:, :], op0=OP.mult, op1=OP.add)
                    nn = c.inc("a0v_pb", h, 1)
                    c.wait(vector, "a0v_pb", nn)

            tail_z_y2(tt, dc, "B")

        tail_wout_x(tt, "B")

    barrier(c, block, sems=["a0v_Bz", "a0v_Bo2", "a0t_Bo", "a0s_Bz", "a0v_pb",
                            "a0s_pb", "a0v_cbl", "a0t_cbl", "a0v_dl2B", "a0s_dlB"],
            release=["a0"])
    stP.close()


def ph_edge_exchange(c, block, x, chm, prevsel, ccs):
    nc = c.nc
    ccx_in, ccx_out = ccs
    st = ExitStack()
    ed = c.sb("ed_buf", [128, 8 * 3], F32, st)
    eda = c.sb("ed_all", [128, 4 * 24], F32, st)
    xv = x[:].rearrange("p (d t) -> p d t", d=8)

    @block.vector
    def _(vector):
        c.inc("edv", vector.tensor_copy(
            out=ed[:].rearrange("p (d t) -> p d t", d=8),
            in_=xv[:, :, TE - 3:TE]), 1)

    @block.sync
    def _(sync):
        c.wait(sync, "edv", 1)
        c.inc("edq", sync.dma_start(out=ccx_in[:, :], in_=ed[:]), 16)

    @block.gpsimd
    def _(gpsimd):
        c.wait(gpsimd, "edq", 16)
        c.inc("edg", gpsimd.collective_compute(
            "AllGather", OP.bypass, ins=[ccx_in], outs=[ccx_out],
            replica_groups=[[0, 1, 2, 3], [4, 5, 6, 7]]), 1)

    @block.sync
    def _(sync):
        c.wait(sync, "edg", 1)
        c.inc("edq2", sync.dma_start(
            out=eda[:].rearrange("p (j t) -> p j t", j=4),
            in_=ccx_out.rearrange("(j p) t -> j p t", p=128).transpose([1, 0, 2])), 16)

    @block.vector
    def _(vector):
        c.wait(vector, "edq2", 16)
        c.seq(vector, vector.tensor_scalar(
            out=ed[:, :], in0=eda[:, 0:24], scalar1=prevsel[:, 0:1],
            scalar2=None, op0=OP.mult), "V")
        for j in range(1, 4):
            c.seq(vector, vector.scalar_tensor_tensor(
                out=ed[:, :], in0=eda[:, j * 24:(j + 1) * 24],
                scalar=prevsel[:, j:j + 1], in1=ed[:, :],
                op0=OP.mult, op1=OP.add), "V")
        h = vector.tensor_copy(out=xv[:, :, 0:3],
                               in_=ed[:].rearrange("p (d t) -> p d t", d=8))
        nn = c.inc("edv2", h, 1)
        c.wait(vector, "edv2", nn)

    barrier(c, block, sems=["edv2"], release=["ed"])
    st.close()


def ph_ffn1(c, block, xn, g, wdram, ones128, ones_row, ones_rowf, ident, identf):
    nc = c.nc
    NT5 = TC // 512
    xnv = xn[:].rearrange("p (d t) -> p d t", d=8)
    gv = g[:].rearrange("p (d t) -> p d t", d=8)
    st = ExitStack()
    qk = c.sb("f1_qk", [128, 8 * 1024], BF16, st)
    qf = c.sb("f1_qf", [128, 4 * 1024], F32, st)
    qh = c.sb("f1_qh", [128, 4 * 1024], F32, st)
    at = c.sb("f1_a", [128, 8 * 16], BF16, st)
    bt = c.sb("f1_b", [16, 1024], BF16, st)
    bias = c.sb("f1_bias", [128, 8], F32, st)
    mx = c.sb("f1_mx", [128, 2], F32, st)
    mxr = c.sb("f1_mxr", [1, 128], F32, st)
    mxc = c.sb("f1_mxc", [128, 1], F32, st)
    u16 = c.sb("f1_u", [16, TC], BF16, st)
    ps_m = c.ps("f1_ps", [128, 2 * 512], F32, st)
    ps_u = c.ps("f1_psu", [16, 512], F32, st)
    ps_t = c.ps("f1_pst", [1, 128], F32, st)
    mxf = c.sb("f1_mxf", [128, 1], F32, st)

    @block.gpsimd
    def _(gpsimd):
        c.inc("f1g", gpsimd.dma_start(
            out=at[:].rearrange("p (k r) -> p k r", k=8), in_=wt_kpf(wdram["Q_a"])), 16)
        c.inc("f1g", gpsimd.dma_start(out=bt[:], in_=wdram["Q_b"]), 16)
        c.inc("f1g", gpsimd.dma_start(
            out=bias[:], in_=wdram["Q_bias"].rearrange("(k p) -> k p", p=128)
            .transpose([1, 0]), allow_slow_non_contiguous=True), 16)

    for hh in range(2):
        @block.gpsimd
        def _(gpsimd, hh=hh):
            if hh == 1:
                c.wait(gpsimd, "f1v_mx", 1)
            c.inc("f1g_q", gpsimd.dma_start(
                out=qf[:].rearrange("p (k f) -> p k f", k=4),
                in_=wt_kpf(wdram["Q_kernel"])[:, hh * 4:(hh + 1) * 4, :]), 16)

        @block.vector
        def _(vector, hh=hh):
            c.wait(vector, "f1g_q", 16 * (hh + 1))
            h = vector.tensor_reduce(out=mx[:, hh:hh + 1], in_=qf[:, :],
                                     axis=AX.X, op=OP.max, apply_absolute_value=True)
            c.inc("f1v_mx", h, 1)

    @block.vector
    def _(vector):
        c.wait(vector, "f1v_mx", 2)
        c.seq(vector, vector.tensor_reduce(out=mx[:, 0:1], in_=mx[:, 0:2],
                                           axis=AX.X, op=OP.max), "V")
        c.inc("f1v_mx2", vector.tensor_copy(out=mxf[:, 0:1], in_=mx[:, 0:1]), 1)

    @block.tensor
    def _(tensor):
        c.wait(tensor, "f1v_mx2", 1)
        c.inc("f1t_mx", tensor.transpose(ps_t[0:1, 0:128], mxf[:, 0:1], identf[:]), 1)

    @block.vector
    def _(vector):
        c.wait(vector, "f1t_mx", 1)
        c.seq(vector, vector.tensor_copy(out=mxr[0:1, :], in_=ps_t[0:1, 0:128]), "V")
        c.seq(vector, vector.tensor_reduce(out=mx[0:1, 0:1], in_=mxr[0:1, :],
                                           axis=AX.X, op=OP.max), "V")
        c.inc("f1v_mx3", vector.engine_nop(), 1)

    @block.tensor
    def _(tensor):
        c.wait(tensor, "f1v_mx3", 1)
        c.inc("f1t_bc", tensor.matmul(ps_m[:, 0:1], ones_rowf[0:1, :],
                                      mx[0:1, 0:1], start=True, stop=True), 1)

    @block.vector
    def _(vector):
        c.wait(vector, "f1t_bc", 1)
        c.seq(vector, vector.tensor_scalar_mul(out=mxc[:, 0:1], in0=ps_m[:, 0:1],
                                               scalar1=1.0 / 7.0), "V")
        c.seq(vector, vector.reciprocal(out=mx[:, 0:1], in_=mxc[:, 0:1]), "V")
        c.inc("f1v_sc", vector.engine_nop(), 1)

    for hh in range(2):
        @block.gpsimd
        def _(gpsimd, hh=hh):
            c.wait(gpsimd, "f1v_mx", 2)
            if hh == 1:
                c.wait(gpsimd, "f1v_qz", 1)
            c.inc("f1g_q2", gpsimd.dma_start(
                out=qf[:].rearrange("p (k f) -> p k f", k=4),
                in_=wt_kpf(wdram["Q_kernel"])[:, hh * 4:(hh + 1) * 4, :]), 16)

        @block.vector
        def _(vector, hh=hh):
            c.wait(vector, "f1g_q2", 16 * (hh + 1))
            c.wait(vector, "f1v_sc", 1)
            c.seq(vector, vector.tensor_scalar(
                out=qf[:, :], in0=qf[:, :], scalar1=mx[:, 0:1], scalar2=None,
                op0=OP.mult), "V")
            c.seq(vector, vector.tensor_single_scalar(
                out=qh[:, :], in_=qf[:, :], scalar=-7.5, op=OP.is_ge), "V")
            for kk in range(-6, 8):
                c.seq(vector, vector.scalar_tensor_tensor(
                    out=qh[:, :], in0=qf[:, :], scalar=float(kk) - 0.5,
                    in1=qh[:, :], op0=OP.is_ge, op1=OP.add), "V")
            c.seq(vector, vector.tensor_scalar(
                out=qf[:, :], in0=qh[:, :], scalar1=-8.0, scalar2=None,
                op0=OP.add), "V")
            h = vector.tensor_scalar(
                out=qk[:, hh * 4096:(hh + 1) * 4096], in0=qf[:, :],
                scalar1=mxc[:, 0:1], scalar2=None, op0=OP.mult)
            nn = c.inc("f1v_qz", h, 1)
            c.wait(vector, "f1v_qz", nn)

    for tg in range(NT5):
        @block.tensor
        def _(tensor, tg=tg):
            c.wait(tensor, "f1g", 48)
            if tg >= 1:
                c.wait(tensor, "f1v_u", tg)
            for kc in range(8):
                h = tensor.matmul(ps_u[:, :], at[:, kc * 16:(kc + 1) * 16],
                                  xnv[:, kc, 3 + tg * 512:3 + (tg + 1) * 512],
                                  start=(kc == 0), stop=(kc == 7))
            c.inc("f1t_u", h, 1)

        @block.vector
        def _(vector, tg=tg):
            c.wait(vector, "f1t_u", tg + 1)
            c.inc("f1v_u", vector.tensor_copy(
                out=u16[:, tg * 512:(tg + 1) * 512], in_=ps_u[:, :]), 1)

    @block.scalar
    def _(scalar):
        c.table(scalar, SILU)
        c.inc("f1s0", scalar.nop(), 1)

    for oc in range(8):
        for tg in range(NT5):
            k = NT5 * oc + tg
            slot = (k % 2) * 512

            @block.tensor
            def _(tensor, oc=oc, tg=tg, k=k, slot=slot):
                c.wait(tensor, "f1v_qz", 2)
                c.wait(tensor, "f1v_u", NT5)
                if k >= 2:
                    c.wait(tensor, "f1s_g", k - 1)
                for kc in range(8):
                    tensor.matmul(ps_m[:, slot:slot + 512],
                                  qk[:, (kc * 8 + oc) * 128:(kc * 8 + oc) * 128 + 128],
                                  xnv[:, kc, 3 + tg * 512:3 + (tg + 1) * 512],
                                  start=(kc == 0), stop=False)
                h = tensor.matmul(ps_m[:, slot:slot + 512],
                                  bt[:, oc * 128:(oc + 1) * 128],
                                  u16[:, tg * 512:(tg + 1) * 512],
                                  start=False, stop=True)
                c.inc("f1t_g", h, 1)

            @block.scalar
            def _(scalar, oc=oc, tg=tg, k=k, slot=slot):
                c.wait(scalar, "f1t_g", k + 1)
                c.wait(scalar, "f1s0", 1)
                h = scalar.activation(out=gv[:, oc, tg * 512:(tg + 1) * 512],
                                      in_=ps_m[:, slot:slot + 512], func=AF.Silu,
                                      bias=bias[:, oc:oc + 1])
                c.inc("f1s_g", h, 1)

    barrier(c, block, sems=["f1s_g", "f1t_g", "f1v_qz", "f1v_u", "f1v_sc",
                            "f1t_u", "f1g", "f1s0"], release=["f1"])
    st.close()


def ph_attn1(c, block, x, xn, g, wdram, ones128, ones_row, ident, sel32_in,
             sel128_in, chm, ccs):
    nc = c.nc
    TT5 = 512
    NTT = TC // TT5
    S1L = 64
    xv = x[:].rearrange("p (d t) -> p d t", d=8)
    xnv = xn[:].rearrange("p (d t) -> p d t", d=8)
    gv = g[:].rearrange("p (d t) -> p d t", d=8)
    cc1_in, cc1_out = ccs

    stP = ExitStack()
    sel32 = c.sb("a1_sel32", [32, 16 * 128], F32, stP)[:]
    sel128 = c.sb("a1_sel128", [128, 64 * 128], BF16, stP)[:]
    Arep = c.sb("a1_Arep", [128, 16 * S1L], F32, stP)
    Atmp = c.sb("a1_Atmp", [32, S1L], F32, stP)
    Dsrc = c.sb("a1_Dsrc", [32, 1], F32, stP)
    Drep = c.sb("a1_Drep", [128, 16], F32, stP)
    dtb = c.sb("a1_dtb", [32, 1], F32, stP)
    dtw = c.sb("a1_dtw", [128, 8 * 32], BF16, stP)
    convw = c.sb("a1_convw", [128, 16 * 4], F32, stP)
    convb = c.sb("a1_convb", [128, 16], F32, stP)
    carry = c.sb("a1_carry", [128, 16 * S1L], BF16, stP)
    dsacc = c.sb("a1_dsacc", [32, 4], F32, stP)
    dsum32 = c.sb("a1_dsum32", [32, 1], F32, stP)
    cdel32 = c.sb("a1_cdel32", [32, 1], F32, stP)
    dtt = c.sb("a1_dtt", [32, TT5], F32, stP)
    dts1 = c.sb("a1_dts1", [32, TT5], F32, stP)
    dts2 = c.sb("a1_dts2", [32, TT5], F32, stP)
    y2t = c.sb("a1_y2", [128, 16 * TT5], BF16, stP)
    winb = [c.sb(f"a1_win{i}", [128, 8 * 128], BF16, stP) for i in range(2)]
    woutb = [c.sb("a1_wout", [128, 16 * 128], BF16, stP)] * 2
    y2v = y2t[:].rearrange("p (k t) -> p k t", k=16)

    stPA = ExitStack()
    bcw = [c.sb("a1_bcw", [128, 8 * 256], BF16, stPA)] * 2
    xh = c.sb("a1_xh", [128, TT5 + 3], BF16, stPA)
    ctx3 = c.sb("a1_ctx", [128, 16 * 3], BF16, stPA)
    Bt = c.sb("a1_Bt", [128, TT5], BF16, stPA)
    Ct = c.sb("a1_Ct", [128, TT5], BF16, stPA)
    drep = c.sb("a1_drep", [128, TT5], F32, stPA)
    wdc = c.sb("a1_wdc", [128, TT5], BF16, stPA)
    sch1 = c.sb("a1_sch1", [128, TT5], F32, stPA)
    vpb = c.sb("a1_vpb", [128, TT5], F32, stPA)
    dab = [c.sb(f"a1_dab{i}", [128, 2 * (TT5 + 1)], F32, stPA) for i in range(2)]
    ubb = c.sb("a1_ub", [128, 4 * (TT5 + 1)], BF16, stPA)
    hbb = c.sb("a1_hb", [128, 4 * (TT5 + 1)], BF16, stPA)
    t1b = c.sb("a1_t1", [128, 2 * TT5], BF16, stPA)
    ydc = c.sb("a1_ydc", [128, TT5], BF16, stPA)
    gb = c.sb("a1_gb", [128, 2 * TT5], BF16, stPA)
    zs = c.sb("a1_zs", [128, TT5], BF16, stPA)
    zex = c.sb("a1_zex", [128, TT5], F32, stPA)

    ps_a = c.ps("a1_ps_a", [128, 2 * TT5], F32, stP)
    ps_bc = c.ps("a1_ps_bc", [128, 2 * TT5], F32, stP)
    ps_c = c.ps("a1_ps_c", [128, 2 * TT5], F32, stP)
    ps_dr = c.ps("a1_ps_dr", [128, TT5], F32, stP)
    ps_K = c.ps("a1_ps_K", [32, TT5], F32, stP)

    WIN = wdram["L1_W_inT"]
    WOUT = wdram["L1_W_outT"]

    @block.gpsimd
    def _(gpsimd):
        c.inc("a1g", gpsimd.dma_start(out=sel32, in_=sel32_in), 16)
        c.inc("a1g", gpsimd.dma_start(out=sel128, in_=sel128_in), 16)
        c.inc("a1g", gpsimd.dma_start(out=Atmp[:], in_=wdram["L1_A_log"]), 16)
        c.inc("a1g", gpsimd.dma_start(
            out=Dsrc[:], in_=wdram["L1_D"].rearrange("(n o) -> n o", o=1)), 16)
        c.inc("a1g", gpsimd.dma_start(
            out=dtb[:], in_=wdram["L1_dt_bias"].rearrange("(n o) -> n o", o=1)), 16)
        c.inc("a1g", gpsimd.dma_start(out=convw[:].rearrange("p (k s) -> p k s", k=16),
                                      in_=wt_kpf(wdram["L1_conv_w"])), 16)
        c.inc("a1g", gpsimd.dma_start(
            out=convb[:], in_=wdram["L1_conv_b"].rearrange("(k p) -> k p", p=128)
            .transpose([1, 0]), allow_slow_non_contiguous=True), 16)
        for kc in range(8):
            c.inc("a1g", gpsimd.dma_start(
                out=dtw[:, kc * 32:(kc + 1) * 32],
                in_=wt_kpf(WIN)[:, kc, 4096:8224].rearrange(
                    "p (n r) -> p n r", r=129)[:, :, 128],
                allow_slow_non_contiguous=True), 16)
        gpsimd.memset(carry[:], 0.0)
        gpsimd.memset(cdel32[:], 0.0)
        gpsimd.memset(dab[0][:], 0.0)
        c.inc("a1g2", gpsimd.memset(dab[1][:], 0.0), 1)

    @block.scalar
    def _(scalar):
        c.wait(scalar, "a1g", c.val("a1g"))
        c.wait(scalar, "a1g2", 1)
        c.table(scalar, NLEXP)
        c.inc("a1s0", scalar.activation(out=Atmp[:], in_=Atmp[:], func=AF.Exp), 1)

    @block.vector
    def _(vector):
        c.wait(vector, "a1s0", 1)
        c.seq(vector, vector.tensor_scalar_mul(out=Atmp[:], in0=Atmp[:],
                                               scalar1=-1.0), "V")
        c.inc("a1v0", vector.tensor_scalar_mul(out=convb[:], in0=convb[:],
                                               scalar1=-1.0), 1)

    for dc in range(16):
        @block.tensor
        def _(tensor, dc=dc):
            c.wait(tensor, "a1v0", 1)
            if dc >= 1:
                c.wait(tensor, "a1v_ar", dc)
            tensor.matmul(ps_dr[:, 0:S1L], sel32[:, dc * 128:(dc + 1) * 128],
                          Atmp[:], start=True, stop=True)
            c.inc("a1t_ar", tensor.matmul(
                ps_dr[:, S1L:S1L + 1], sel32[:, dc * 128:(dc + 1) * 128],
                Dsrc[:], start=True, stop=True), 1)

        @block.vector
        def _(vector, dc=dc):
            c.wait(vector, "a1t_ar", dc + 1)
            c.seq(vector, vector.tensor_copy(
                out=Arep[:, dc * S1L:(dc + 1) * S1L], in_=ps_dr[:, 0:S1L]), "V")
            h = vector.tensor_copy(out=Drep[:, dc:dc + 1],
                                   in_=ps_dr[:, S1L:S1L + 1])
            nn = c.inc("a1v_ar", h, 1)
            c.wait(vector, "a1v_ar", nn)

    def tail_z_y2(tt, dc, M):
        tb = 3 + tt * TT5
        k = 16 * tt + dc

        @block.gpsimd
        def _(gpsimd, tt=tt, dc=dc, k=k):
            if M == "A":
                c.wait(gpsimd, "a1t_pr", k + 1)
            if k >= 2:
                c.wait(gpsimd, f"a1t_{M}z", k - 1)
            c.inc(f"a1g_{M}z{dc % 2}", gpsimd.dma_start(
                out=winb[dc % 2][:].rearrange("p (k f) -> p k f", k=8),
                in_=wt_kpf(WIN)[:, :, 2048 + dc * 128:2048 + (dc + 1) * 128]), 16)

        @block.tensor
        def _(tensor, tt=tt, dc=dc, k=k):
            c.wait(tensor, f"a1g_{M}z{dc % 2}", 16 * (k // 2 + 1))
            if M == "A":
                c.wait(tensor, "a1v_yD", k + 1)
            else:
                c.wait(tensor, "a1v_pb", S1L // 2 * (k + 1))
            if k >= 2:
                c.wait(tensor, f"a1v_{M}z", k - 1)
            for kc in range(8):
                h = tensor.matmul(ps_a[:, TT5:2 * TT5],
                                  winb[dc % 2][:, kc * 128:(kc + 1) * 128],
                                  xnv[:, kc, tb:tb + TT5],
                                  start=(kc == 0), stop=(kc == 7))
            c.inc(f"a1t_{M}z", h, 1)

        @block.scalar
        def _(scalar, tt=tt, dc=dc, k=k):
            c.wait(scalar, f"a1t_{M}z", k + 1)
            if k >= 1:
                c.wait(scalar, f"a1v_{M}z", k)
            c.inc(f"a1s_{M}z", scalar.activation(
                out=zex[:, :], in_=ps_a[:, TT5:2 * TT5], func=AF.Exp,
                scale=-1.0), 1)

        @block.vector
        def _(vector, tt=tt, dc=dc, k=k):
            c.wait(vector, f"a1s_{M}z", k + 1)
            c.seq(vector, vector.tensor_copy(out=zs[:, :],
                                             in_=ps_a[:, TT5:2 * TT5]), "V")
            c.seq(vector, vector.tensor_scalar(
                out=zex[:, :], in0=zex[:, :], scalar1=1.0, scalar2=None,
                op0=OP.add), "V")
            c.seq(vector, vector.reciprocal(out=zex[:, :], in_=zex[:, :]), "V")
            c.seq(vector, vector.tensor_mul(out=zs[:, :], in0=zs[:, :],
                                            in1=zex[:, :]), "V")
            if M == "A":
                h = vector.tensor_mul(out=y2v[:, dc, :], in0=ydc[:, :], in1=zs[:, :])
            else:
                h = vector.tensor_mul(out=y2v[:, dc, :], in0=gb[:, 0:TT5],
                                      in1=zs[:, :])
            nn = c.inc(f"a1v_{M}z", h, 1)
            c.wait(vector, f"a1v_{M}z", nn)

    def tail_wout_x(tt, M):
        for oc in range(8):
            k = 8 * tt + oc

            @block.gpsimd
            def _(gpsimd, tt=tt, oc=oc, k=k):
                if k >= 1:
                    c.wait(gpsimd, f"a1t_{M}o", k)
                c.inc(f"a1g_{M}o{oc % 2}", gpsimd.dma_start(
                    out=woutb[oc % 2][:].rearrange("p (k f) -> p k f", k=16),
                    in_=wt_kpf(WOUT)[:, :, oc * 128:(oc + 1) * 128]), 16)

            @block.tensor
            def _(tensor, tt=tt, oc=oc, k=k):
                c.wait(tensor, f"a1g_{M}o{oc % 2}", 16 * (k // 2 + 1))
                c.wait(tensor, f"a1v_{M}z", 16 * (tt + 1))
                if k >= 2:
                    c.wait(tensor, f"a1v_{M}o2", k - 1)
                slot = (oc % 2) * TT5
                for dc in range(16):
                    h = tensor.matmul(ps_c[:, slot:slot + TT5],
                                      woutb[oc % 2][:, dc * 128:(dc + 1) * 128],
                                      y2v[:, dc, :],
                                      start=(dc == 0), stop=(dc == 15))
                c.inc(f"a1t_{M}o", h, 1)

            @block.vector
            def _(vector, tt=tt, oc=oc, k=k):
                c.wait(vector, f"a1t_{M}o", k + 1)
                slot = (oc % 2) * TT5
                tb = 3 + tt * TT5
                c.seq(vector, vector.tensor_mul(
                    out=gb[:, TT5:2 * TT5],
                    in0=gv[:, oc, tt * TT5:(tt + 1) * TT5],
                    in1=ps_c[:, slot:slot + TT5]), "V")
                h = vector.tensor_add(out=xv[:, oc, tb:tb + TT5],
                                      in0=xv[:, oc, tb:tb + TT5],
                                      in1=gb[:, TT5:2 * TT5])
                nn = c.inc(f"a1v_{M}o2", h, 1)
                c.wait(vector, f"a1v_{M}o2", nn)

    def dt_block(tt, M):
        tb = 3 + tt * TT5
        ti = (0 if M == "A" else NTT) + tt

        @block.tensor
        def _(tensor, tt=tt, ti=ti):
            if ti == 0:
                c.wait(tensor, "a1v_ar", 16)
            else:
                c.wait(tensor, f"a1v_dt", ti)
            for kc in range(8):
                h = tensor.matmul(ps_K[:, :], dtw[:, kc * 32:(kc + 1) * 32],
                                  xnv[:, kc, tb:tb + TT5],
                                  start=(kc == 0), stop=(kc == 7))
            c.inc("a1t_dt", h, 1)

        @block.scalar
        def _(scalar, tt=tt, ti=ti):
            c.wait(scalar, "a1t_dt", ti + 1)
            if ti >= 1:
                c.wait(scalar, "a1v_dt", ti)
            c.seq(scalar, scalar.activation(out=dts1[:, :], in_=ps_K[:, :],
                                            func=AF.Abs, bias=dtb[:, 0:1]), "S")
            c.seq(scalar, scalar.activation(out=dts1[:, :], in_=dts1[:, :],
                                            func=AF.Exp, scale=-1.0), "S")
            h = scalar.activation(out=dts1[:, :], in_=dts1[:, :],
                                  func=AF.Ln, bias=1.0)
            c.inc("a1s_dt", h, 1)

        @block.vector
        def _(vector, tt=tt, ti=ti, M=M):
            c.wait(vector, "a1s_dt", ti + 1)
            if M == "A" and tt >= 1:
                c.wait(vector, "a1t_pr", 16 * tt)
            if M == "B" and tt == 0:
                c.wait(vector, "a1g_o5", 1)
            if M == "B" and tt >= 1:
                c.wait(vector, "a1t_prB", 16 * tt)
            c.seq(vector, vector.tensor_scalar(
                out=dts2[:, :], in0=ps_K[:, :], scalar1=dtb[:, 0:1],
                scalar2=None, op0=OP.add), "V")
            if M == "A":
                h = vector.scalar_tensor_tensor(
                    out=dtt[:, :], in0=dts2[:, :], scalar=0.0, in1=dts1[:, :],
                    op0=OP.max, op1=OP.add,
                    accum_out=dsacc[:, tt:tt + 1])
            else:
                c.seq(vector, vector.scalar_tensor_tensor(
                    out=dtt[:, :], in0=dts2[:, :], scalar=0.0, in1=dts1[:, :],
                    op0=OP.max, op1=OP.add), "V")
                c.seq(vector, vector.tensor_tensor_scan(
                    out=cum32[:, :], data0=ones512[0:32, :], data1=dtt[:, :],
                    initial=cdel32[:, 0:1], op0=OP.mult, op1=OP.add), "V")
                h = vector.tensor_copy(out=cdel32[:, 0:1],
                                       in_=cum32[:, TT5 - 1:TT5])
            nn = c.inc("a1v_dt", h, 1)
            c.wait(vector, "a1v_dt", nn)

    for tt in range(NTT):
        tb = 3 + tt * TT5
        dt_block(tt, "A")

        for dc in range(16):
            k = 16 * tt + dc

            @block.gpsimd
            def _(gpsimd, tt=tt, dc=dc, k=k):
                if k >= 1:
                    c.wait(gpsimd, "a1t_pr", k)
                if k >= 2:
                    c.wait(gpsimd, "a1t_Az", k - 1)
                c.inc(f"a1g_x{dc % 2}", gpsimd.dma_start(
                    out=winb[dc % 2][:].rearrange("p (k f) -> p k f", k=8),
                    in_=wt_kpf(WIN)[:, :, dc * 128:(dc + 1) * 128]), 16)
                for q in range(2):
                    n = 2 * dc + q
                    c.inc(f"a1g_b{dc % 2}", gpsimd.dma_start(
                        out=bcw[dc % 2][:].rearrange("p (k f) -> p k f", k=8)
                        [:, :, q * 128:(q + 1) * 128],
                        in_=wt_kpf(WIN)[:, :, 4096 + n * 129:4096 + n * 129 + 128]), 16)

            @block.tensor
            def _(tensor, tt=tt, dc=dc, k=k):
                c.wait(tensor, f"a1g_x{dc % 2}", 16 * (k // 2 + 1))
                c.wait(tensor, f"a1g_b{dc % 2}", 32 * (k // 2 + 1))
                c.wait(tensor, "a1v_dt", tt + 1)
                if k >= 1:
                    c.wait(tensor, "a1v_pr", k)
                if k >= 1:
                    c.wait(tensor, "a1v_scan", 16 * k)
                if tt >= 1:
                    c.wait(tensor, "a1v_Ao2", 8 * tt)
                for kc in range(8):
                    tensor.matmul(ps_a[:, 0:TT5],
                                  winb[dc % 2][:, kc * 128:(kc + 1) * 128],
                                  xnv[:, kc, tb:tb + TT5],
                                  start=(kc == 0), stop=(kc == 7))
                if tt == 0:
                    for kc in range(8):
                        tensor.matmul(ps_dr[:, 0:3],
                                      winb[dc % 2][:, kc * 128:(kc + 1) * 128],
                                      xnv[:, kc, 0:3],
                                      start=(kc == 0), stop=(kc == 7))
                for q in range(2):
                    for kc in range(8):
                        tensor.matmul(ps_bc[:, q * TT5:(q + 1) * TT5],
                                      bcw[dc % 2][:, kc * 256 + q * 128:
                                                  kc * 256 + (q + 1) * 128],
                                      xnv[:, kc, tb:tb + TT5],
                                      start=(kc == 0), stop=(kc == 7))
                h = tensor.matmul(ps_c[:, 0:TT5],
                                  sel32[:, dc * 128:(dc + 1) * 128],
                                  dtt[:, :], start=True, stop=True)
                c.inc("a1t_pr", h, 1)

            @block.vector
            def _(vector, tt=tt, dc=dc, k=k):
                c.wait(vector, "a1t_pr", k + 1)
                if k >= 1:
                    c.wait(vector, "a1t_C", 16 * k)
                    c.wait(vector, "a1s_da", 32 * k)
                if tt == 0:
                    c.seq(vector, vector.tensor_copy(out=xh[:, 0:3],
                                                     in_=ps_dr[:, 0:3]), "V")
                else:
                    c.seq(vector, vector.tensor_copy(
                        out=xh[:, 0:3],
                        in_=ctx3[:].rearrange("p (k t) -> p k t", k=16)[:, dc, :]), "V")
                c.seq(vector, vector.tensor_copy(out=xh[:, 3:3 + TT5],
                                                 in_=ps_a[:, 0:TT5]), "V")
                c.seq(vector, vector.tensor_copy(
                    out=ctx3[:].rearrange("p (k t) -> p k t", k=16)[:, dc, :],
                    in_=xh[:, TT5:TT5 + 3]), "V")
                for q in range(2):
                    c.seq(vector, vector.tensor_copy(
                        out=Bt[q * 64:(q + 1) * 64, :],
                        in_=ps_bc[0:64, q * TT5:(q + 1) * TT5]), "V")
                    c.seq(vector, vector.tensor_copy(
                        out=Ct[q * 64:(q + 1) * 64, :],
                        in_=ps_bc[64:128, q * TT5:(q + 1) * TT5]), "V")
                h = vector.tensor_copy(out=drep[:, :], in_=ps_c[:, 0:TT5])
                nn = c.inc("a1v_pr", h, 1)
                c.wait(vector, "a1v_pr", nn)

            @block.vector
            def _(vector, tt=tt, dc=dc, k=k):
                if k >= 1:
                    c.wait(vector, "a1s_cv", k)
                c.seq(vector, vector.tensor_scalar(
                    out=sch1[:, :], in0=xh[:, 0:TT5],
                    scalar1=convw[:, dc * 4:dc * 4 + 1], scalar2=None,
                    op0=OP.mult), "V")
                for j in range(1, 4):
                    h = vector.scalar_tensor_tensor(
                        out=sch1[:, :], in0=xh[:, j:j + TT5],
                        scalar=convw[:, dc * 4 + j:dc * 4 + j + 1],
                        in1=sch1[:, :], op0=OP.mult, op1=OP.add)
                    if j < 3:
                        c.seq(vector, h, "V")
                c.inc("a1v_cv", h, 1)

            @block.scalar
            def _(scalar, tt=tt, dc=dc, k=k):
                c.wait(scalar, "a1v_cv", k + 1)
                if k >= 1:
                    c.wait(scalar, "a1v_cv2", k)
                h = scalar.activation(out=vpb[:, :], in_=sch1[:, :],
                                      func=AF.Exp, scale=-1.0,
                                      bias=convb[:, dc:dc + 1])
                c.inc("a1s_cv", h, 1)

            @block.vector
            def _(vector, tt=tt, dc=dc, k=k):
                c.wait(vector, "a1s_cv", k + 1)
                c.seq(vector, vector.tensor_scalar(
                    out=vpb[:, :], in0=vpb[:, :], scalar1=1.0, scalar2=None,
                    op0=OP.add), "V")
                c.seq(vector, vector.reciprocal(out=vpb[:, :], in_=vpb[:, :]), "V")
                c.seq(vector, vector.tensor_scalar(
                    out=gb[:, 0:TT5], in0=sch1[:, :],
                    scalar1=convb[:, dc:dc + 1], scalar2=None, op0=OP.subtract), "V")
                c.seq(vector, vector.tensor_mul(out=xh[:, 3:3 + TT5],
                                                in0=gb[:, 0:TT5], in1=vpb[:, :]), "V")
                h = vector.tensor_mul(out=wdc[:, :], in0=drep[:, :],
                                      in1=xh[:, 3:3 + TT5])
                nn = c.inc("a1v_cv2", h, 1)
                c.wait(vector, "a1v_cv2", nn)

            for sb in range(16):
                bidx = (16 * tt + dc) * 16 + sb

                @block.tensor
                def _(tensor, tt=tt, dc=dc, sb=sb, bidx=bidx):
                    c.wait(tensor, "a1v_cv2", 16 * tt + dc + 1)
                    if bidx >= 1:
                        c.wait(tensor, "a1v_u", 2 * bidx)
                        c.wait(tensor, "a1v_gb", bidx)
                    if tt >= 1:
                        c.wait(tensor, "a1v_Ao2", 8 * tt)
                    for q4 in range(4):
                        s = 4 * sb + q4
                        dst = ps_bc if q4 < 2 else ps_c
                        h = tensor.matmul(dst[:, (q4 % 2) * TT5:(q4 % 2 + 1) * TT5],
                                          sel128[:, s * 128:(s + 1) * 128],
                                          Bt[:, :], start=True, stop=True)
                    c.inc("a1t_B", h, 1)

                @block.tensor
                def _(tensor, tt=tt, dc=dc, sb=sb, bidx=bidx):
                    c.wait(tensor, "a1v_u", 2 * (bidx + 1))
                    for q4 in range(4):
                        s = 4 * sb + q4
                        dst = ps_bc if q4 < 2 else ps_c
                        h = tensor.matmul(dst[:, (q4 % 2) * TT5:(q4 % 2 + 1) * TT5],
                                          sel128[:, s * 128:(s + 1) * 128],
                                          Ct[:, :], start=True, stop=True)
                    c.inc("a1t_C", h, 1)

                @block.scalar
                def _(scalar, tt=tt, dc=dc, sb=sb, bidx=bidx):
                    c.wait(scalar, "a1v_cv2", 16 * tt + dc + 1)
                    if bidx >= 1:
                        c.wait(scalar, "a1v_sc2", bidx)
                    for half in range(2):
                        dav = dab[half][:].rearrange("p (a b) -> p a b", a=2)
                        for a in range(2):
                            s = 4 * sb + 2 * half + a
                            h = scalar.activation(
                                out=dav[:, a, 1:1 + TT5], in_=drep[:, :],
                                func=AF.Exp,
                                scale=Arep[:, dc * S1L + s:dc * S1L + s + 1])
                            if a == 0:
                                c.seq(scalar, h, "S")
                        c.inc("a1s_da", h, 1)

                @block.vector
                def _(vector, tt=tt, dc=dc, sb=sb, bidx=bidx):
                    c.wait(vector, "a1t_B", bidx + 1)
                    ubv = ubb[:].rearrange("p (a b) -> p a b", a=4)
                    h = vector.tensor_tensor(
                        out=ubv[:, 0:2, 1:1 + TT5],
                        in0=wdc[:].unsqueeze(1).broadcast_to((128, 2, TT5)),
                        in1=ps_bc[:].rearrange("p (a b) -> p a b", a=2),
                        op=OP.mult)
                    c.inc("a1v_u", h, 1)
                    h = vector.tensor_tensor(
                        out=ubv[:, 2:4, 1:1 + TT5],
                        in0=wdc[:].unsqueeze(1).broadcast_to((128, 2, TT5)),
                        in1=ps_c[:].rearrange("p (a b) -> p a b", a=2),
                        op=OP.mult)
                    c.inc("a1v_u", h, 1)
                    c.seq(vector, vector.tensor_copy(
                        out=ubv[:, :, 0:1],
                        in_=carry[:, dc * S1L + 4 * sb:dc * S1L + 4 * sb + 4]
                        .rearrange("p (a b) -> p a b", a=4)), "V")
                    c.wait(vector, "a1s_da", 2 * bidx + 1)
                    c.seq(vector, vector.tensor_tensor_scan(
                        out=hbb[:, 0:2 * (TT5 + 1)], data0=dab[0][:, :],
                        data1=ubb[:, 0:2 * (TT5 + 1)],
                        initial=0.0, op0=OP.mult, op1=OP.add), "V")
                    c.wait(vector, "a1s_da", 2 * bidx + 2)
                    h = vector.tensor_tensor_scan(
                        out=hbb[:, 2 * (TT5 + 1):4 * (TT5 + 1)], data0=dab[1][:, :],
                        data1=ubb[:, 2 * (TT5 + 1):4 * (TT5 + 1)],
                        initial=0.0, op0=OP.mult, op1=OP.add)
                    c.inc("a1v_sc2", h, 1)
                    hbv = hbb[:].rearrange("p (a b) -> p a b", a=4)
                    c.wait(vector, "a1t_C", bidx + 1)
                    c.seq(vector, vector.tensor_tensor(
                        out=gb[:].rearrange("p (a b) -> p a b", a=2),
                        in0=hbv[:, 0:2, 1:1 + TT5],
                        in1=ps_bc[:].rearrange("p (a b) -> p a b", a=2),
                        op=OP.mult), "V")
                    c.seq(vector, vector.tensor_tensor(
                        out=t1b[:].rearrange("p (a b) -> p a b", a=2),
                        in0=hbv[:, 2:4, 1:1 + TT5],
                        in1=ps_c[:].rearrange("p (a b) -> p a b", a=2),
                        op=OP.mult), "V")
                    h = vector.tensor_add(out=t1b[:, :], in0=t1b[:, :],
                                          in1=gb[:, 0:2 * TT5])
                    c.inc("a1v_gb", h, 1)
                    c.seq(vector, vector.tensor_copy(
                        out=carry[:, dc * S1L + 4 * sb:dc * S1L + 4 * sb + 4]
                        .rearrange("p (a b) -> p a b", a=4),
                        in_=hbv[:, :, TT5:TT5 + 1]), "V")
                    if sb == 0:
                        h = vector.tensor_add(out=ydc[:, :], in0=t1b[:, 0:TT5],
                                              in1=t1b[:, TT5:2 * TT5])
                    else:
                        c.seq(vector, vector.tensor_add(
                            out=ydc[:, :], in0=ydc[:, :], in1=t1b[:, 0:TT5]), "V")
                        h = vector.tensor_add(out=ydc[:, :], in0=ydc[:, :],
                                              in1=t1b[:, TT5:2 * TT5])
                    nn = c.inc("a1v_scan", h, 1)
                    c.wait(vector, "a1v_scan", nn)

            @block.vector
            def _(vector, tt=tt, dc=dc, k=k):
                c.wait(vector, "a1v_scan", 16 * (k + 1))
                h = vector.scalar_tensor_tensor(
                    out=ydc[:, :], in0=xh[:, 3:3 + TT5],
                    scalar=Drep[:, dc:dc + 1], in1=ydc[:, :],
                    op0=OP.mult, op1=OP.add)
                nn = c.inc("a1v_yD", h, 1)
                c.wait(vector, "a1v_yD", nn)

            tail_z_y2(tt, dc, "A")

        tail_wout_x(tt, "A")

    barrier(c, block, sems=["a1v_Az", "a1v_Ao2", "a1t_Ao", "a1s_Az", "a1v_scan",
                            "a1s_da", "a1t_B", "a1t_C", "a1v_u", "a1v_gb",
                            "a1v_sc2", "a1s_cv", "a1v_cv",
                            "a1v_cv2", "a1t_pr", "a1v_pr", "a1v_yD", "a1v_dt",
                            "a1s_dt", "a1t_dt"])
    stPA.close()
    Hbuf = c.sb("a1_H", [128, 16 * S1L], F32, stP)
    cum32 = c.sb("a1_cum32", [32, TT5], F32, stP)
    ones512 = c.sb("a1_ones512", [32, TT5], BF16, stP)

    @block.gpsimd
    def _(gpsimd):
        c.inc("a1g_o5", gpsimd.memset(ones512[:], 1.0), 1)

    stE = ExitStack()
    Pacc = c.sb("a1_Pacc", [128, 16 * S1L], F32, stE)
    Prun = c.sb("a1_Prun", [128, 16 * S1L], F32, stE)
    tcmb = c.sb("a1_tcmb", [128, 16 * S1L], F32, stE)
    agj = c.sb("a1_agj", [128, 2 * 16 * S1L], F32, stE)
    dsr = c.sb("a1_dsr", [128, 16], F32, stE)

    @block.vector
    def _(vector):
        h = vector.tensor_copy(out=dsum32[:, :], in_=dsacc[:, 0:1])
        for _tt in range(1, NTT):
            c.seq(vector, h, "V")
            h = vector.tensor_add(out=dsum32[:, :], in0=dsum32[:, :],
                                  in1=dsacc[:, _tt:_tt + 1])
        nn = c.inc("a1v_ds", h, 1)
        c.wait(vector, "a1v_ds", nn)

    for dc in range(16):
        @block.tensor
        def _(tensor, dc=dc):
            c.wait(tensor, "a1v_ds", 1)
            if dc >= 1:
                c.wait(tensor, "a1v_dsr", dc)
            c.inc("a1t_dsr", tensor.matmul(
                ps_dr[:, 0:1], sel32[:, dc * 128:(dc + 1) * 128],
                dsum32[:, :], start=True, stop=True), 1)

        @block.vector
        def _(vector, dc=dc):
            c.wait(vector, "a1t_dsr", dc + 1)
            h = vector.tensor_copy(out=dsr[:, dc:dc + 1], in_=ps_dr[:, 0:1])
            nn = c.inc("a1v_dsr", h, 1)
            c.wait(vector, "a1v_dsr", nn)

    @block.vector
    def _(vector):
        c.wait(vector, "a1v_dsr", 16)
        for dc in range(16):
            h = vector.tensor_scalar(
                out=Pacc[:, dc * S1L:(dc + 1) * S1L],
                in0=Arep[:, dc * S1L:(dc + 1) * S1L],
                scalar1=dsr[:, dc:dc + 1], scalar2=None, op0=OP.mult)
        c.inc("a1v_dec1", h, 1)

    @block.scalar
    def _(scalar):
        c.wait(scalar, "a1v_dec1", 1)
        c.inc("a1s_dec", scalar.activation(out=Pacc[:], in_=Pacc[:], func=AF.Exp), 1)

    @block.sync
    def _(sync):
        c.wait(sync, "a1s_dec", 1)
        c.inc("a1q_cc", sync.dma_start(out=cc1_in[:, 0:1024], in_=Pacc[:]), 16)

    @block.gpsimd
    def _(gpsimd):
        c.wait(gpsimd, "a1s_dec", 1)
        c.inc("a1q_cc2", gpsimd.dma_start(out=cc1_in[:, 1024:2048],
                                          in_=carry[:]), 16)

    @block.gpsimd
    def _(gpsimd):
        c.wait(gpsimd, "a1q_cc", 16)
        c.wait(gpsimd, "a1q_cc2", 16)
        c.inc("a1g_cc", gpsimd.collective_compute(
            "AllGather", OP.bypass, ins=[cc1_in], outs=[cc1_out],
            replica_groups=[[0, 1, 2, 3], [4, 5, 6, 7]]), 1)

    @block.gpsimd
    def _(gpsimd):
        c.wait(gpsimd, "a1g_cc", 1)
        gpsimd.memset(Hbuf[:], 0.0)
        c.inc("a1g_hm", gpsimd.memset(Prun[:], 1.0), 1)

    for j in [2, 1, 0]:
        @block.sync
        def _(sync, j=j):
            c.wait(sync, "a1g_cc", 1)
            if j < 2:
                c.wait(sync, "a1v_cmb", 2 - j)
            c.inc("a1q_ag", sync.dma_start(
                out=agj[:], in_=cc1_out[j * 128:(j + 1) * 128, :]), 16)

        @block.vector
        def _(vector, j=j):
            c.wait(vector, "a1q_ag", 16 * (3 - j))
            c.wait(vector, "a1g_hm", 1)
            m = chm[:, j:j + 1]
            c.seq(vector, vector.tensor_mul(out=tcmb[:, :], in0=agj[:, 1024:2048],
                                            in1=Prun[:, :]), "V")
            c.seq(vector, vector.tensor_scalar(out=tcmb[:, :], in0=tcmb[:, :],
                                               scalar1=m, scalar2=None,
                                               op0=OP.mult), "V")
            c.seq(vector, vector.tensor_add(out=Hbuf[:, :], in0=Hbuf[:, :],
                                            in1=tcmb[:, :]), "V")
            c.seq(vector, vector.tensor_scalar(out=tcmb[:, :], in0=agj[:, 0:1024],
                                               scalar1=1.0, scalar2=m,
                                               op0=OP.subtract, op1=OP.mult), "V")
            c.seq(vector, vector.tensor_scalar(out=tcmb[:, :], in0=tcmb[:, :],
                                               scalar1=1.0, scalar2=None,
                                               op0=OP.add), "V")
            h = vector.tensor_mul(out=Prun[:, :], in0=Prun[:, :], in1=tcmb[:, :])
            nn = c.inc("a1v_cmb", h, 1)
            c.wait(vector, "a1v_cmb", nn)

    barrier(c, block, sems=["a1v_cmb", "a1v_ds", "a1t_dsr", "a1v_dsr",
                            "a1v_dec1", "a1s_dec", "a1q_cc", "a1q_cc2", "a1g_cc",
                            "a1g_hm", "a1q_ag"])
    stE.close()

    stPB = ExitStack()
    bcwB = [c.sb("a1_bcwB", [128, 8 * 256], BF16, stPB)] * 2
    CtB = c.sb("a1_CtB", [128, TT5], BF16, stPB)
    cumrep = c.sb("a1_cumrep", [128, TT5], F32, stPB)
    schB2p = [c.sb(f"a1_schB2{i}", [128, 2 * TT5], F32, stPB) for i in range(2)]
    ycor = c.sb("a1_ycor", [128, TT5], F32, stPB)
    gbB = c.sb("a1_gbB", [128, 2 * TT5], BF16, stPB)
    zsB = c.sb("a1_zsB", [128, TT5], BF16, stPB)
    zexB = c.sb("a1_zexB", [128, TT5], F32, stPB)

    gb = gbB
    zs = zsB
    zex = zexB

    for tt in range(NTT):
        tb = 3 + tt * TT5
        dt_block(tt, "B")

        for dc in range(16):
            k = 16 * tt + dc

            @block.gpsimd
            def _(gpsimd, tt=tt, dc=dc, k=k):
                if k >= 1:
                    c.wait(gpsimd, "a1t_prB", k)
                for q in range(2):
                    n = 2 * dc + q
                    c.inc(f"a1g_bB{dc % 2}", gpsimd.dma_start(
                        out=bcwB[dc % 2][:].rearrange("p (k f) -> p k f", k=8)
                        [:, :, q * 128:(q + 1) * 128],
                        in_=wt_kpf(WIN)[:, :, 4096 + n * 129:4096 + n * 129 + 128]), 16)

            @block.tensor
            def _(tensor, tt=tt, dc=dc, k=k):
                c.wait(tensor, f"a1g_bB{dc % 2}", 32 * (k // 2 + 1))
                c.wait(tensor, "a1v_dt", NTT + tt + 1)
                if k >= 1:
                    c.wait(tensor, "a1v_prB", k)
                    c.wait(tensor, "a1v_pb", S1L // 2 * k)
                for q in range(2):
                    for kc in range(8):
                        tensor.matmul(ps_bc[:, q * TT5:(q + 1) * TT5],
                                      bcwB[dc % 2][:, kc * 256 + q * 128:
                                                   kc * 256 + (q + 1) * 128],
                                      xnv[:, kc, tb:tb + TT5],
                                      start=(kc == 0), stop=(kc == 7))
                h = tensor.matmul(ps_dr[:, 0:TT5],
                                  sel32[:, dc * 128:(dc + 1) * 128],
                                  cum32[:, :], start=True, stop=True)
                c.inc("a1t_prB", h, 1)

            @block.vector
            def _(vector, tt=tt, dc=dc, k=k):
                c.wait(vector, "a1t_prB", k + 1)
                if k >= 1:
                    c.wait(vector, "a1t_cb", S1L // 2 * k)
                    c.wait(vector, "a1s_pb", S1L // 2 * k)
                for q in range(2):
                    c.seq(vector, vector.tensor_copy(
                        out=CtB[q * 64:(q + 1) * 64, :],
                        in_=ps_bc[64:128, q * TT5:(q + 1) * TT5]), "V")
                h = vector.tensor_copy(out=cumrep[:, :], in_=ps_dr[:, 0:TT5])
                nn = c.inc("a1v_prB", h, 1)
                c.wait(vector, "a1v_prB", nn)

            for sq in range(S1L // 2):
                pidx = (16 * tt + dc) * (S1L // 2) + sq

                @block.tensor
                def _(tensor, tt=tt, dc=dc, sq=sq, pidx=pidx):
                    c.wait(tensor, "a1v_prB", k + 1)
                    if pidx >= 2:
                        c.wait(tensor, "a1v_pb", pidx - 1)
                    if tt >= 1:
                        c.wait(tensor, "a1v_Bo2", 8 * tt)
                    pdst = ps_c if sq % 2 == 0 else ps_bc
                    for a in range(2):
                        s = 2 * sq + a
                        h = tensor.matmul(pdst[:, a * TT5:(a + 1) * TT5],
                                          sel128[:, s * 128:(s + 1) * 128],
                                          CtB[:, :], start=True, stop=True)
                    c.inc("a1t_cb", h, 1)

                @block.scalar
                def _(scalar, tt=tt, dc=dc, sq=sq, pidx=pidx):
                    c.wait(scalar, "a1v_prB", k + 1)
                    if pidx >= 2:
                        c.wait(scalar, "a1v_pb", pidx - 1)
                    schB2 = schB2p[pidx % 2]
                    for a in range(2):
                        s = 2 * sq + a
                        h = scalar.activation(
                            out=schB2[:, a * TT5:(a + 1) * TT5], in_=cumrep[:, :],
                            func=AF.Exp,
                            scale=Arep[:, dc * S1L + s:dc * S1L + s + 1])
                        if a == 0:
                            c.seq(scalar, h, "S")
                    c.inc("a1s_pb", h, 1)

                @block.vector
                def _(vector, tt=tt, dc=dc, sq=sq, pidx=pidx):
                    c.wait(vector, "a1s_pb", pidx + 1)
                    c.wait(vector, "a1t_cb", pidx + 1)
                    schB2 = schB2p[pidx % 2]
                    c.seq(vector, vector.tensor_tensor(
                        out=schB2[:, :], in0=schB2[:, :],
                        in1=(ps_c if sq % 2 == 0 else ps_bc)[:, :],
                        op=OP.mult), "V")
                    s0 = 2 * sq
                    if sq == 0:
                        c.seq(vector, vector.tensor_scalar(
                            out=ycor[:, :], in0=schB2[:, 0:TT5],
                            scalar1=Hbuf[:, dc * S1L + s0:dc * S1L + s0 + 1],
                            scalar2=None, op0=OP.mult), "V")
                    else:
                        c.seq(vector, vector.scalar_tensor_tensor(
                            out=ycor[:, :], in0=schB2[:, 0:TT5],
                            scalar=Hbuf[:, dc * S1L + s0:dc * S1L + s0 + 1],
                            in1=ycor[:, :], op0=OP.mult, op1=OP.add), "V")
                    h = vector.scalar_tensor_tensor(
                        out=ycor[:, :], in0=schB2[:, TT5:2 * TT5],
                        scalar=Hbuf[:, dc * S1L + s0 + 1:dc * S1L + s0 + 2],
                        in1=ycor[:, :], op0=OP.mult, op1=OP.add)
                    nn = c.inc("a1v_pb", h, 1)
                    c.wait(vector, "a1v_pb", nn)

            @block.vector
            def _(vector, tt=tt, dc=dc, k=k):
                c.wait(vector, "a1v_pb", S1L // 2 * (k + 1))
                h = vector.tensor_copy(out=gb[:, 0:TT5], in_=ycor[:, :])
                nn = c.inc("a1v_yc", h, 1)
                c.wait(vector, "a1v_yc", nn)

            tail_z_y2(tt, dc, "B")

        tail_wout_x(tt, "B")

    barrier(c, block, sems=["a1v_Bz", "a1v_Bo2", "a1t_Bo", "a1s_Bz", "a1v_pb",
                            "a1s_pb", "a1t_cb", "a1v_prB", "a1t_prB", "a1v_yc",
                            "a1v_dt", "a1s_dt", "a1t_dt"],
            release=["a1"])
    stPB.close()
    stP.close()


def build():
    nc = bass.Bass("TRN2", target_bir_lowering=False)

    def din(name, shape, dt=F32):
        return nc.dram_tensor(name, list(shape), dt, kind="ExternalInput").ap()

    xin = din("xin", (TE, D))
    ident_in = din("ident", (128, 128))
    sel8_in = din("sel8", (8, 8 * 128))
    sel33_in = din("sel33", (33, 33 * 128))
    sel32_in = din("sel32", (32, 16 * 128))
    sel128_in = din("sel128", (128, 64 * 128))
    prevsel_in = din("prevsel", (128, 4))
    w = {k: din(k, shp) for k, shp in WEIGHT_SHAPES.items()}
    dram_chm = din("chmask", (128, 4))
    cc0_in = nc.dram_tensor("cc0_in", [128, 512], F32, kind="Internal").ap()
    cc0_out = nc.dram_tensor("cc0_out", [512, 512], F32, kind="Internal").ap()
    ccx_in = nc.dram_tensor("ccx_in", [128, 24], F32, kind="Internal").ap()
    ccx_out = nc.dram_tensor("ccx_out", [512, 24], F32, kind="Internal").ap()
    cc1_in = nc.dram_tensor("cc1_in", [128, 2048], F32, kind="Internal").ap()
    cc1_out = nc.dram_tensor("cc1_out", [512, 2048], F32, kind="Internal").ap()
    out = nc.dram_tensor("out", [TC, D], F32, kind="ExternalOutput").ap()
    dbg_out = nc.dram_tensor("dbg", [128, 1 << 15], F32, kind="ExternalOutput").ap() if DBG else None

    es = ExitStack()
    with es:
        c = Ctx(nc, es)
        for _nm in ["bT", "bV", "bS", "bG", "bQ", "bC", "ord_V", "ord_S"]:
            c.semrec(_nm)
        x = c.sb("x", [128, 8 * TE], BF16)
        xn = c.sb("xn", [128, 8 * TE], BF16)
        ident = c.sb("identt", [128, 128], BF16)
        sel8 = c.sb("sel8t", [8, 8 * 128], BF16)
        sel33 = c.sb("sel33t", [33, 33 * 128], BF16)
        chmt = c.sb("chmt", [128, 4], F32)
        prevselt = c.sb("prevselt", [128, 4], F32)
        identf = c.sb("identtf", [128, 128], F32)
        ones_rowf = c.sb("ones_rowf", [1, 128], F32)
        ones128 = c.sb("ones128", [128, 1], BF16)
        ones_row = c.sb("ones_row", [1, 128], BF16)

        block = es.enter_context(nc.Block())

        c.make_consts(CONST_VALUES)

        @block.gpsimd
        def _(gpsimd):
            gpsimd.memset(ones128[:], 1.0)
            gpsimd.memset(ones_row[:], 1.0)
            for v, t in c.consts.items():
                gpsimd.memset(t[:], float(v))
            c.inc("g_const", gpsimd.dma_start(out=sel8[:], in_=sel8_in), 16)
            c.inc("g_const", gpsimd.dma_start(out=sel33[:], in_=sel33_in), 16)
            c.inc("g_const", gpsimd.dma_start(out=chmt[:], in_=dram_chm), 16)
            c.inc("g_const", gpsimd.dma_start(out=prevselt[:], in_=prevsel_in), 16)
            c.inc("g_const", gpsimd.dma_start(out=identf[:], in_=ident_in), 16)
            gpsimd.memset(ones_rowf[:], 1.0)
            c.inc("g_const", gpsimd.dma_start(out=ident[:], in_=ident_in), 16)
        const_target = c.val("g_const")

        with nc.named_scope("load"):
            ph_load(c, block, xin, x, ident, const_target)
            barrier(c, block, sems=["ld_ev", "ld_tr"], release=["ld_"])
        if STAGE == 0:
            ph_store(c, block, x, out, ident)
            return nc
        with nc.named_scope("ln1"):
            ph_layernorm(c, block, x, xn, TE, ones128, ones_row, "ln1")

        if STAGE == 1:
            ph_store(c, block, xn, out, ident)
            return nc

        g = c.sb("gbuf", [128, 8 * TC], BF16)
        with nc.named_scope("ffn0"):
            ph_ffn0(c, block, xn, g, w, ones128, ones_row, ident, sel8)
        if STAGE == 2:
            ph_store(c, block, g, out, ident, toff=0, tstride=TC)
            return nc

        with nc.named_scope("attn0"):
            ph_attn0(c, block, x, xn, g, w, ones128, ones_row, ident, sel33,
                     chmt[:], (cc0_in, cc0_out))
        if STAGE == 3:
            ph_store(c, block, x, out, ident)
            return nc

        with nc.named_scope("edge"):
            ph_edge_exchange(c, block, x, chmt[:], prevselt[:], (ccx_in, ccx_out))
        with nc.named_scope("ln2"):
            ph_layernorm(c, block, x, xn, TE, ones128, ones_row, "ln2")
        with nc.named_scope("ffn1"):
            ph_ffn1(c, block, xn, g, w, ones128, ones_row, ones_rowf, ident, identf)
        if STAGE == 4:
            ph_store(c, block, g, out, ident, toff=0, tstride=TC)
            return nc

        with nc.named_scope("attn1"):
            ph_attn1(c, block, x, xn, g, w, ones128, ones_row, ident, sel32_in,
                     sel128_in, chmt[:], (cc1_in, cc1_out))
        with nc.named_scope("store"):
            ph_store(c, block, x, out, ident)
    return nc


WEIGHT_SHAPES = {
    "L0_W_in": (D, 2 * DI),
    "L0_conv_w": (DI, 4),
    "L0_conv_b": (DI,),
    "L0_W_xproj": (DI, 33),
    "L0_W_dt": (DI,),
    "L0_b_dt": (DI,),
    "L0_A_log": (DI, S0),
    "L0_Dp": (DI,),
    "L0_W_out": (DI, D),
    "F0_router_w": (D, 8),
    "F0_w1": (8, D, D),
    "F0_w2": (8, D, D),
    "F0_sgu": (D, 2 * D),
    "F0_sdown": (D, D),
    "F0_sgw": (D,),
    "L1_W_inT": (D, DPROJ),
    "L1_conv_w": (DI, 4),
    "L1_conv_b": (DI,),
    "L1_A_log": (NH, S1),
    "L1_D": (NH,),
    "L1_dt_bias": (NH,),
    "L1_W_outT": (DI, D),
    "Q_kernel": (D, D),
    "Q_bias": (D,),
    "Q_a": (D, 16),
    "Q_b": (16, D),
}


_BUILT = None


def make_in_maps(inputs):
    xs = np.ascontiguousarray(inputs["L0_attn_x"], dtype=np.float32)
    flat = xs.reshape(B * T, D)
    wmap = {
        "L0_W_in": inputs["L0_attn_W_in"],
        "L0_conv_w": inputs["L0_attn_conv_w"][:, 0, :],
        "L0_conv_b": inputs["L0_attn_conv_b"],
        "L0_W_xproj": inputs["L0_attn_W_xproj"],
        "L0_W_dt": inputs["L0_attn_W_dt"][0],
        "L0_b_dt": inputs["L0_attn_b_dt"],
        "L0_A_log": inputs["L0_attn_A_log"],
        "L0_Dp": inputs["L0_attn_Dp"],
        "L0_W_out": inputs["L0_attn_W_out"],
        "F0_router_w": inputs["L0_ffn_router_w"],
        "F0_w1": inputs["L0_ffn_w1"],
        "F0_w2": inputs["L0_ffn_w2"],
        "F0_sgu": inputs["L0_ffn_shared_gate_up"],
        "F0_sdown": inputs["L0_ffn_shared_down"],
        "F0_sgw": inputs["L0_ffn_shared_gate_w"][:, 0],
        "L1_W_inT": inputs["L1_attn_W_in"].T,
        "L1_conv_w": inputs["L1_attn_conv_w"][:, 0, :],
        "L1_conv_b": inputs["L1_attn_conv_b"],
        "L1_A_log": inputs["L1_attn_A_log"],
        "L1_D": inputs["L1_attn_D"],
        "L1_dt_bias": inputs["L1_attn_dt_bias"],
        "L1_W_outT": inputs["L1_attn_W_out"].T,
        "Q_kernel": inputs["L1_ffn_kernel"],
        "Q_bias": inputs["L1_ffn_bias"],
        "Q_a": inputs["L1_ffn_a"],
        "Q_b": inputs["L1_ffn_b"],
    }
    wmap = {k: np.ascontiguousarray(v, dtype=np.float32) for k, v in wmap.items()}
    ident = np.eye(128, dtype=np.float32)
    sel8 = np.zeros((8, 8 * 128), np.float32)
    for e in range(8):
        sel8[e, e * 128:(e + 1) * 128] = 1.0
    sel33 = np.zeros((33, 33 * 128), np.float32)
    for r in range(33):
        sel33[r, r * 128:(r + 1) * 128] = 1.0
    sel32 = np.zeros((32, 16 * 128), np.float32)
    for dcq in range(16):
        for p in range(128):
            sel32[2 * dcq + p // 64, dcq * 128 + p] = 1.0
    sel128 = np.zeros((128, 64 * 128), np.float32)
    for s in range(64):
        for p in range(128):
            sel128[(p // 64) * 64 + s, s * 128 + p] = 1.0

    in_maps = []
    for core in range(NCORES):
        t0 = core * TC
        pre = np.zeros((3, D), np.float32) if core % 4 == 0 else flat[t0 - 3:t0]
        xin_c = np.ascontiguousarray(np.concatenate([pre, flat[t0:t0 + TC]], axis=0))
        m = np.zeros((128, 4), np.float32)
        m[:, :core % 4] = 1.0
        pv = np.zeros((128, 4), np.float32)
        if core % 4 > 0:
            pv[:, core % 4 - 1] = 1.0
        im = dict(wmap)
        im["xin"] = xin_c
        im["ident"] = ident
        im["sel8"] = sel8
        im["sel33"] = sel33
        im["sel32"] = sel32
        im["sel128"] = sel128
        im["chmask"] = m
        im["prevsel"] = pv
        in_maps.append(im)
    return in_maps


def kernel(**inputs):
    global _BUILT
    in_maps = make_in_maps(inputs)
    if _BUILT is None:
        _BUILT = build()
    res = run_bass_kernel_spmd(_BUILT, in_maps, core_ids=list(range(NCORES)))
    parts = [res.results[i]["out"] for i in range(NCORES)]
    return np.concatenate(parts, axis=0).reshape(B, T, D).astype(np.float32)
